# version 28
# baseline (speedup 1.0000x reference)
import numpy as np
import concourse.bass as bass
import concourse.mybir as mybir
from concourse.bass_utils import run_bass_kernel_spmd

F32 = mybir.dt.float32
BF16 = mybir.dt.bfloat16
AF = mybir.ActivationFunctionType
ALU = mybir.AluOpType
AX = mybir.AxisListType

T = 4096
C = 1024
H = 16
D = 64
NT = T // 128
NCH = C // 128
SW = 512
NWIN = T // SW
NQ = 4
NEWTON_ITERS = 24
EPS = 1e-5

SUBS = [("attn0", 0), ("ffn", 0), ("attn1", 1), ("ffn", 1)]
TRUNC_LAST = "full"
INTERLEAVE_PROJ = False
NO_GRAM = False
NO_GRAMMM = False
NO_KSTTT = False
NO_GRMM = False
GRAM_SS = False
NO_BCAST = False

SEM_NAMES = (
    ["t", "v", "a", "g"]
    + ["rr%d" % i for i in range(4)]
    + ["sg%d" % i for i in range(4)]
    + ["pq0", "pq1", "pk0", "pk1", "pv0"]
    + ["tr", "w", "str", "dn", "mq"]
    + ["rx%d" % i for i in range(4)]
)


class Ctx:

    def __init__(self, me, eng, sems):
        self.me = me
        self.eng = eng
        self.sems = sems
        self.cnt = {k: 0 for k in sems}
        self.rr_guard = {i: ("v", 0) for i in range(3)}
        self.ln_state = {"var": {0: 0, 1: 0}, "s2": {0: 0, 1: 0},
                         "norm": {0: 0, 1: 0},
                         "ln": {0: 0, 1: 0}}
        self.str_mile = {}
        self.wb_free_lo = 0

    def inc(self, s, n=1):
        self.cnt[s] += n
        return self.cnt[s]

    def val(self, s):
        return self.cnt[s]

    def op(self, engine, builder, inc=None):
        ins = builder(self.eng) if self.me == engine else None
        if inc:
            for s, n in inc:
                self.inc(s, n)
                if ins is not None:
                    ins.then_inc(self.sems[s], n)
        return ins

    def wait(self, engine, s, v):
        if v <= 0:
            return
        if self.me == engine:
            self.eng.wait_ge(self.sems[s], v)

    def dma(self, engine, builder, sem):
        return self.op(engine, builder, inc=[(sem, 16)])


def build_program(sub_lo, sub_hi):
    nc = bass.Bass("TRN2", target_bir_lowering=False)

    xin = nc.dram_tensor("xin", [T, C], F32, kind="ExternalInput").ap()
    xout = nc.dram_tensor("xout", [T, C], F32, kind="ExternalOutput").ap()

    subs = SUBS[sub_lo:sub_hi]
    win = {}
    for si, (kind, L) in enumerate(subs):
        pre = "s%d_" % si
        d = {}
        if kind == "ffn":
            for nm, shp in (("wr", [C, C]), ("wkf", [C, C]), ("wvf", [C, C]),
                            ("mur", [128, 8]), ("muk", [128, 8])):
                d[nm] = nc.dram_tensor(pre + nm, shp, F32, kind="ExternalInput").ap()
        else:
            for nm in ("wq", "wk", "wv", "wo"):
                d[nm] = nc.dram_tensor(pre + nm, [C, C], F32, kind="ExternalInput").ap()
            d["bo"] = nc.dram_tensor(pre + "bo", [1, C], F32, kind="ExternalInput").ap()
            if kind == "attn1":
                for nm, shp in (("bq", [128, 8]), ("bk", [128, 8]),
                                ("bvr", [1, C]), ("bkr", [1, C])):
                    d[nm] = nc.dram_tensor(pre + nm, shp, F32, kind="ExternalInput").ap()
        win[si] = d

    XT = nc.dram_tensor("XTs", [T, C], BF16, kind="Internal").ap()
    QTd = nc.dram_tensor("QTs", [C, T], BF16, kind="Internal").ap()
    KTd = nc.dram_tensor("KTs", [C, T], BF16, kind="Internal").ap()
    QPTd = nc.dram_tensor("QPTs", [C, T], BF16, kind="Internal").ap()
    VSd = nc.dram_tensor("VSs", [T, H * 65], BF16, kind="Internal").ap()
    MQd = nc.dram_tensor("MQs", [3 * C], F32, kind="Internal").ap()
    MKd = nc.dram_tensor("MKs", [C], F32, kind="Internal").ap()
    RAd = nc.dram_tensor("RAs", [T, C], F32, kind="Internal").ap()
    RBd = nc.dram_tensor("RBs", [T, C], F32, kind="Internal").ap()
    APLd = nc.dram_tensor("APLs", [128, SW], F32, kind="Internal").ap()

    n_sub = len(subs)
    rsrc, rdst = [None] * n_sub, [None] * n_sub
    cur = xin
    for si in range(n_sub):
        rsrc[si] = cur
        rdst[si] = xout if si == n_sub - 1 else (RAd if si % 2 == 0 else RBd)
        cur = rdst[si]

    A1 = nc.alloc_sbuf_tensor("A1", [128, 32768], BF16).ap()
    WB = nc.alloc_sbuf_tensor("WB", [128, 3 * 8192], BF16).ap()
    RQ = [nc.alloc_sbuf_tensor("RQ%d" % i, [128, 1024], F32).ap() for i in range(3)]
    STG = [nc.alloc_sbuf_tensor("STG%d" % i, [128, 1040], BF16).ap() for i in range(4)]
    AR2 = nc.alloc_sbuf_tensor("AR2", [128, 8192], BF16).ap()
    STATS = nc.alloc_sbuf_tensor("STATS", [128, 16], F32).ap()
    DPREV = nc.alloc_sbuf_tensor("DPREV", [128, 1024], F32).ap()
    XPC = nc.alloc_sbuf_tensor("XPC", [128, 8], BF16).ap()
    XPREV = nc.alloc_sbuf_tensor("XPREV", [128, 1024], BF16).ap()
    MASKS = nc.alloc_sbuf_tensor("MASKS", [128, 4 * SW], BF16).ap()
    EB = [nc.alloc_sbuf_tensor("EB%d" % i, [128, 1024], BF16).ap() for i in range(2)]
    DEN = nc.alloc_sbuf_tensor("DEN", [128, SW], F32).ap()
    RECIP = nc.alloc_sbuf_tensor("RECIP", [128, SW], F32).ap()
    DSTG = [nc.alloc_sbuf_tensor("DSTG%d" % i, [1, SW], F32).ap() for i in range(2)]
    RSTG = [nc.alloc_sbuf_tensor("RSTG%d" % i, [1, SW], F32).ap() for i in range(2)]
    ONES1 = nc.alloc_sbuf_tensor("ONES1", [1, 128], F32).ap()
    ONES64 = nc.alloc_sbuf_tensor("ONES64", [64, 1], F32).ap()
    BROW = nc.alloc_sbuf_tensor("BROW", [1, C], F32).ap()
    BBC = nc.alloc_sbuf_tensor("BBC", [128, C], F32).ap()
    BBC2 = nc.alloc_sbuf_tensor("BBC2", [128, C], F32).ap()
    MQS = nc.alloc_sbuf_tensor("MQS", [128, 8], F32).ap()
    MQSQ = nc.alloc_sbuf_tensor("MQSQ", [64, 16], F32).ap()
    MKS = nc.alloc_sbuf_tensor("MKS", [128, 8], F32).ap()
    MAXR = nc.alloc_sbuf_tensor("MAXR", [1, 80], F32).ap()
    NEGB = nc.alloc_sbuf_tensor("NEGB", [128, 1], F32).ap()
    MUR = nc.alloc_sbuf_tensor("MUR", [128, 8], F32).ap()
    MUK = nc.alloc_sbuf_tensor("MUK", [128, 8], F32).ap()
    BQP = nc.alloc_sbuf_tensor("BQP", [128, 8], F32).ap()
    BKP = nc.alloc_sbuf_tensor("BKP", [128, 8], F32).ap()
    GRAM = nc.alloc_sbuf_tensor("GRAM", [64, H * 64], F32).ap()
    ABD = [nc.alloc_sbuf_tensor("ABD%d" % p, [128, 128], F32).ap() for p in range(8)]
    XBD = [nc.alloc_sbuf_tensor("XBD%d" % p, [128, 128], F32).ap() for p in range(8)]
    XST = nc.alloc_sbuf_tensor("XST", [128, 512], F32).ap()
    XSTB = nc.alloc_sbuf_tensor("XSTB", [128, 512], BF16).ap()
    YSB = nc.alloc_sbuf_tensor("YSB", [128, 512], F32).ap()
    I2S = nc.alloc_sbuf_tensor("I2S", [128, 64], F32).ap()
    TWOI8 = nc.alloc_sbuf_tensor("TWOI8", [128, 512], F32).ap()
    RS16 = nc.alloc_sbuf_tensor("RS16", [64, H], F32).ap()
    CIN = nc.alloc_sbuf_tensor("CIN", [1, H], F32).ap()
    CBC = nc.alloc_sbuf_tensor("CBC", [128, H], F32).ap()
    EPSB = nc.alloc_sbuf_tensor("EPSB", [128, 1], F32).ap()

    PS = nc.alloc_psum_tensor("PS", [128, 4096], F32).ap()

    sems = {s: nc.alloc_semaphore(s) for s in SEM_NAMES}

    env = dict(locals())

    def emit(me, eng):
        cx = Ctx(me, eng, sems)
        emit_prologue(cx, env)
        for si, (kind, L) in enumerate(subs):
            if kind == "ffn":
                emit_ffn(cx, env, win[si], rsrc[si], rdst[si])
            else:
                emit_attn(cx, env, kind, win[si], rsrc[si], rdst[si],
                          trunc=(TRUNC_LAST if si == len(subs) - 1 else "full"))
        cx.wait("sync", "str", cx.val("str"))

    with nc.Block() as block:
        block.tensor(lambda eng: emit("tensor", eng))
        block.vector(lambda eng: emit("vector", eng))
        block.scalar(lambda eng: emit("scalar", eng))
        block.gpsimd(lambda eng: emit("gpsimd", eng))
        block.sync(lambda eng: emit("sync", eng))

    return nc



def emit_prologue(cx, env):
    MASKS, ONES1, ONES64, I2S, TWOI8 = (env["MASKS"], env["ONES1"],
                                        env["ONES64"], env["I2S"], env["TWOI8"])
    for phi in range(4):
        m = MASKS[:, phi * SW:(phi + 1) * SW]
        cx.op("gpsimd", lambda e, m=m: e.memset(m, 1.0))
        cx.op("gpsimd", lambda e, m=m, phi=phi: e.affine_select(
            out=m, in_=m, compare_op=ALU.is_ge, fill=0.0,
            base=-128 * phi, pattern=[[1, SW]], channel_multiplier=-1))
    cx.op("gpsimd", lambda e: e.memset(ONES1[:, :], 1.0))
    cx.op("gpsimd", lambda e: e.memset(env["EPSB"][:, :], EPS))
    cx.op("gpsimd", lambda e: e.memset(ONES64[:, :], 1.0))
    cx.op("gpsimd", lambda e: e.memset(I2S[:, :], 1.0))
    cx.op("gpsimd", lambda e: e.affine_select(
        out=I2S[0:64, :], in_=I2S[0:64, :], compare_op=ALU.is_equal,
        fill=0.0, base=0, pattern=[[-1, 64]], channel_multiplier=1))
    cx.op("gpsimd", lambda e: e.affine_select(
        out=I2S[64:128, :], in_=I2S[64:128, :], compare_op=ALU.is_equal,
        fill=0.0, base=64, pattern=[[-1, 64]], channel_multiplier=1))
    for p in range(8):
        cx.op("gpsimd", lambda e, p=p: e.tensor_scalar_mul(
            TWOI8[:, 64 * p:64 * p + 64], I2S[:, :], 2.0))
    cx.op("gpsimd", lambda e: e.nop(), inc=[("g", 1)])


def emit_ln(cx, env, rsrc, t0, t1, window_cb=None, tile_cb=None,
            act_sum=False):
    RQ, STG, STATS, XT = (env["RQ"], env["STG"], env["STATS"], env["XT"])
    st = cx.ln_state
    vnorm = {}

    def emit_tr(i):
        ssl = i % 4
        stg = STG[ssl][:, 0:1024]
        cx.wait("scalar", "a", vnorm[i])
        cx.dma("scalar", lambda e, stg=stg, i=i: e.dma_start(
            out=XT[128 * i:128 * (i + 1), :], in_=stg), "sg%d" % ssl)
        if window_cb is not None and i % 8 == 7:
            window_cb(i)

    var_of = {}

    def stage_a(i):
        sl = i % 3
        rsem = "rr%d" % sl
        ssl = i % 4
        ssem = "sg%d" % ssl
        rt = RQ[sl]
        stg = STG[ssl][:, 0:1024]
        lane = (i - t0) % 2
        b = 8 * lane
        geng, gval = cx.rr_guard[sl]
        cx.wait("sync", geng, gval)
        cx.wait("sync", "str", cx.str_mile.get(i, 0))
        cx.dma("sync", lambda e, rt=rt, i=i: e.dma_start(
            out=rt[:, :], in_=rsrc[128 * i:128 * (i + 1), :]), rsem)
        v_ld = cx.val(rsem)
        cx.wait("scalar", "v", st["var"][lane])
        cx.wait("scalar", rsem, v_ld)
        cx.wait("scalar", ssem, cx.val(ssem))
        cx.op("scalar", lambda e, rt=rt, stg=stg, b=b: e.activation(
            stg, rt[:, :], AF.Square, accum_out=STATS[:, b + 1:b + 2]),
            inc=[("a", 1)])
        a_sq = cx.val("a")
        if act_sum:
            cx.wait("scalar", "v", st["s2"][lane])
            cx.op("scalar", lambda e, rt=rt, stg=stg, b=b: e.activation(
                stg, rt[:, :], AF.Identity, accum_out=STATS[:, b:b + 1]),
                inc=[("a", 1)])
            a_sum = cx.val("a")
            cx.wait("vector", "a", a_sum)
        else:
            cx.wait("vector", "v", st["s2"][lane])
            cx.wait("vector", rsem, v_ld)
            cx.op("vector", lambda e, rt=rt, b=b: e.reduce_sum(
                STATS[:, b:b + 1], rt[:, :], axis=AX.X), inc=[("v", 1)])
            v_red = cx.val("v")
            cx.wait("vector", "v", v_red)
        cx.wait("vector", "a", st["norm"][lane])
        cx.op("vector", lambda e, b=b: e.tensor_scalar_mul(
            STATS[:, b + 2:b + 3], STATS[:, b:b + 1], -1.0 / C))
        cx.op("vector", lambda e, b=b: e.tensor_tensor(
            STATS[:, b + 3:b + 4], STATS[:, b:b + 1], STATS[:, b:b + 1],
            ALU.mult), inc=[("v", 1)])
        st["s2"][lane] = v_s2 = cx.val("v")
        cx.wait("vector", "a", a_sq)
        cx.wait("vector", "a", st["ln"][lane])
        cx.wait("vector", "v", v_s2)
        cx.op("vector", lambda e, b=b: e.scalar_tensor_tensor(
            STATS[:, b + 4:b + 5], STATS[:, b + 3:b + 4], -1.0 / C,
            STATS[:, b + 1:b + 2], ALU.mult, ALU.add), inc=[("v", 1)])
        st["var"][lane] = var_of[i] = cx.val("v")

    def stage_b(i):
        sl = i % 3
        ssl = i % 4
        ssem = "sg%d" % ssl
        rt = RQ[sl]
        stg = STG[ssl][:, 0:1024]
        lane = (i - t0) % 2
        b = 8 * lane
        cx.wait("scalar", "v", var_of[i])
        cx.op("scalar", lambda e, b=b: e.activation(
            STATS[:, b + 5:b + 6], STATS[:, b + 4:b + 5], AF.Ln,
            bias=env["EPSB"][:, :], scale=1.0 / C), inc=[("a", 1)])
        a_ln = cx.val("a")
        st["ln"][lane] = a_ln
        cx.wait("scalar", "a", a_ln)
        cx.wait("scalar", "a", st["norm"][lane])
        cx.op("scalar", lambda e, b=b: e.activation(
            STATS[:, b + 6:b + 7], STATS[:, b + 5:b + 6], AF.Exp, scale=-0.5),
            inc=[("a", 1)])
        cx.op("scalar", lambda e, b=b: e.activation(
            STATS[:, b + 7:b + 8], STATS[:, b + 2:b + 3], AF.Identity,
            scale=STATS[:, b + 6:b + 7]), inc=[("a", 1)])
        a_bias = cx.val("a")
        cx.wait("scalar", "a", a_bias)
        cx.wait("scalar", ssem, cx.val(ssem))
        cx.op("scalar", lambda e, rt=rt, stg=stg, b=b: e.activation(
            stg, rt[:, :], AF.Identity, bias=STATS[:, b + 7:b + 8],
            scale=STATS[:, b + 6:b + 7]), inc=[("a", 1)])
        a_norm = cx.val("a")
        st["norm"][lane] = a_norm
        cx.rr_guard[sl] = ("a", a_norm)
        vnorm[i] = a_norm

    stage_a(t0)
    for i in range(t0 + 1, t1):
        stage_a(i)
        stage_b(i - 1)
        if i - 3 >= t0:
            emit_tr(i - 3)
        if tile_cb is not None:
            tile_cb(i)
    stage_b(t1 - 1)
    for i in (t1 - 3, t1 - 2, t1 - 1):
        if i >= t0:
            emit_tr(i)


def wait_all_sg(cx, engine):
    for sl in range(4):
        cx.wait(engine, "sg%d" % sl, cx.val("sg%d" % sl))
    cx.wait(engine, "pq1", cx.val("pq1"))


def mm2(cx, lhsT, rhs_a, rhs_b, out_a, out_b, start, stop, tinc=False):
    cx.op("tensor", lambda e: e.matmul(out_a, lhsT, rhs_a, start=start, stop=stop))
    cx.op("tensor", lambda e: e.matmul(out_b, lhsT, rhs_b, start=start, stop=stop),
          inc=[("t", 1)] if tinc else None)


def bcast_row(cx, env, row_ap, dst, banks=(4, 5)):
    PS, ONES1 = env["PS"], env["ONES1"]
    b0 = PS[:, 512 * banks[0]:512 * banks[0] + 512]
    b1 = PS[:, 512 * banks[1]:512 * banks[1] + 512]
    cx.op("tensor", lambda e: e.matmul(b0, ONES1[:, :], row_ap[:, 0:512],
                                       start=True, stop=True))
    cx.op("tensor", lambda e: e.matmul(b1, ONES1[:, :], row_ap[:, 512:1024],
                                       start=True, stop=True), inc=[("t", 1)])
    t_b = cx.val("t")
    cx.wait("vector", "t", t_b)
    cx.op("vector", lambda e: e.tensor_copy(dst[:, 0:512], b0))
    cx.op("vector", lambda e: e.tensor_copy(dst[:, 512:1024], b1), inc=[("v", 1)])
    v_b = cx.val("v")
    cx.wait("tensor", "v", v_b)
    return v_b


def emit_attn(cx, env, kind, w, rsrc, rdst, trunc="full"):
    A1, WB, PS, STG, RQ, AR2 = (env["A1"], env["WB"], env["PS"],
                                env["STG"], env["RQ"], env["AR2"])
    QTd, KTd, QPTd, VSd, XT = (env["QTd"], env["KTd"], env["QPTd"], env["VSd"],
                               env["XT"])
    MQS, MKS, MAXR, NEGB = env["MQS"], env["MKS"], env["MAXR"], env["NEGB"]
    MQd = env["MQd"]
    BROW, BBC, BBC2, BQP, BKP = (env["BROW"], env["BBC"], env["BBC2"],
                                 env["BQP"], env["BKP"])
    ONES1 = env["ONES1"]
    MASKS, EB, DEN, RECIP, DSTG, RSTG = (env["MASKS"], env["EB"], env["DEN"],
                                         env["RECIP"], env["DSTG"], env["RSTG"])
    l1 = (kind == "attn1")
    sm_scale = (1.0 / np.sqrt(D)) if not l1 else (1.0 / np.sqrt(C))

    def bank(i):
        return PS[:, 512 * i:512 * (i + 1)]

    def bankpair(i):
        return PS[:, 1024 * i:1024 * (i + 1)]

    t_entry = cx.val("t")

    for name, col0, buf in (("wq", 0, WB), ("wk", 8192, WB), ("wv", 16384, WB)):
        cx.wait("gpsimd", "t", t_entry if name == "wv" else cx.wb_free_lo)
        src = w[name].rearrange("(ci p) o -> p ci o", p=128)
        dst = buf[:, col0:col0 + 8192].rearrange("p (ci o) -> p ci o", ci=8)
        cx.dma("gpsimd", lambda e, dst=dst, src=src: e.dma_start(out=dst, in_=src), "w")
    if l1:
        cx.dma("gpsimd", lambda e: e.dma_start(out=BQP[:, :], in_=w["bq"]), "w")
        cx.dma("gpsimd", lambda e: e.dma_start(out=BKP[:, :], in_=w["bk"]), "w")
    w_all = cx.val("w")

    cx.op("gpsimd", lambda e: e.memset(MQS[:, :], 0.0))
    cx.op("gpsimd", lambda e: e.memset(MKS[:, :], 0.0), inc=[("g", 1)])
    g_z = cx.val("g")
    trw = {}

    def phase_a_window(j):
        w2 = j // 8
        if w2 == 0:
            cx.wait("scalar", "t", t_entry)
        wait_all_sg(cx, "scalar")
        for ci in range(NCH):
            dst = A1[:, ci * 4096 + 1024 * w2:ci * 4096 + 1024 * (w2 + 1)]
            src = XT[1024 * w2:1024 * (w2 + 1), 128 * ci:128 * ci + 128]
            cx.dma("scalar", lambda e, dst=dst, src=src: e.dma_start_transpose(
                dst, src), "tr")
        trw[w2] = cx.val("tr")

    evac_guard = {0: 0, 1: 0}

    def proj_pass_gen():
        cx.wait("tensor", "w", w_all)
        cx.wait("vector", "g", g_z)
        for wcol0, out_dram, bias_pp, track, mbuf in (
                (0, QTd, BQP if l1 else None, True, MQS),
                (8192, KTd, BKP if l1 else None, True, MKS)):
            for w2 in range(4):
                cx.wait("tensor", "tr", trw[w2])
                for dt in range(NCH):
                    it = w2 * 8 + dt
                    par = it % 2
                    pa, pb = bank(2 * par), bank(2 * par + 1)
                    cx.wait("tensor", "v", evac_guard[par])
                    for ci in range(NCH):
                        lhsT = WB[:, wcol0 + 1024 * ci + 128 * dt:
                                  wcol0 + 1024 * ci + 128 * dt + 128]
                        rhs_a = A1[:, ci * 4096 + 1024 * w2:
                                   ci * 4096 + 1024 * w2 + 512]
                        rhs_b = A1[:, ci * 4096 + 1024 * w2 + 512:
                                   ci * 4096 + 1024 * (w2 + 1)]
                        mm2(cx, lhsT, rhs_a, rhs_b, pa, pb,
                            start=(ci == 0), stop=(ci == NCH - 1),
                            tinc=(ci == NCH - 1))
                    t_done = cx.val("t")
                    sl = it % 8
                    stg = AR2[:, 1024 * sl:1024 * sl + 1024]
                    cx.wait("vector", "t", t_done)
                    cx.wait("vector", "pq1", max(0, cx.val("pq1") - 7 * 16))
                    if bias_pp is not None:
                        cx.op("vector", lambda e, stg=stg, dt=dt, pa=pa:
                              e.tensor_scalar(stg[:, 0:512], pa,
                                              bias_pp[:, dt:dt + 1], None,
                                              ALU.add))
                        cx.op("vector", lambda e, stg=stg, dt=dt, pb=pb:
                              e.tensor_scalar(stg[:, 512:1024], pb,
                                              bias_pp[:, dt:dt + 1], None,
                                              ALU.add), inc=[("v", 1)])
                    else:
                        cx.op("vector", lambda e, stg=stg, pa=pa: e.tensor_copy(
                            stg[:, 0:512], pa))
                        cx.op("vector", lambda e, stg=stg, pb=pb: e.tensor_copy(
                            stg[:, 512:1024], pb), inc=[("v", 1)])
                    v_evac = cx.val("v")
                    evac_guard[par] = v_evac
                    if track:
                        cx.wait("vector", "v", v_evac)
                        cx.op("vector", lambda e, stg=stg: e.reduce_max(
                            env["STATS"][:, 7:8], stg, axis=AX.X,
                            apply_absolute_value=True), inc=[("v", 1)])
                        v_r = cx.val("v")
                        cx.wait("vector", "v", v_r)
                        cx.op("vector", lambda e, dt=dt, mbuf=mbuf: e.tensor_tensor(
                            mbuf[:, dt:dt + 1], mbuf[:, dt:dt + 1],
                            env["STATS"][:, 7:8], ALU.max), inc=[("v", 1)])
                    cx.wait("gpsimd", "v", v_evac)
                    cx.dma("gpsimd", lambda e, stg=stg, dt=dt, w2=w2,
                           out_dram=out_dram: e.dma_start(
                        out=out_dram[128 * dt:128 * (dt + 1),
                                     1024 * w2:1024 * (w2 + 1)],
                        in_=stg), "pq1")
                    yield

    pgen = proj_pass_gen()
    tiles_seen = [0]

    def phase_a_tile(i):
        tiles_seen[0] = i
        if INTERLEAVE_PROJ and i >= 11:
            next(pgen, None)

    emit_ln(cx, env, rsrc, 0, NT, window_cb=phase_a_window,
            tile_cb=phase_a_tile)
    for _ in pgen:
        pass
    if trunc == "proj":
        return

    if l1 and not NO_BCAST:
        cx.dma("gpsimd", lambda e: e.dma_start(out=BROW[:, :], in_=w["bvr"]), "w")
        w_bv = cx.val("w")
        cx.wait("tensor", "w", w_bv)
        bcast_row(cx, env, BROW, BBC2, banks=(4, 5))
        t_after_bv = cx.val("t")
        cx.wait("gpsimd", "t", t_after_bv)
        cx.dma("gpsimd", lambda e: e.dma_start(out=BROW[:, :], in_=w["bkr"]), "w")
        w_bk = cx.val("w")
        cx.wait("tensor", "w", w_bk)
        bcast_row(cx, env, BROW, BBC, banks=(4, 5))

    evac_v = {0: evac_guard[0], 1: evac_guard[1]}
    kst_guard = 0
    l1g = l1 and not NO_GRAM
    if l1g:
        cx.wait("vector", "pq1", cx.val("pq1"))
    for j in range(NT):
        par = j % 2
        pv = bankpair(par)
        pkv = bankpair(2) if l1g else None
        cx.wait("tensor", "v", evac_v[par])
        for ci in range(NCH):
            lhsT = A1[:, ci * 4096 + 128 * j:ci * 4096 + 128 * (j + 1)]
            cx.op("tensor", lambda e, lhsT=lhsT, pv=pv, ci=ci: e.matmul(
                pv[:, 0:512], lhsT, WB[:, 16384 + 1024 * ci:16384 + 1024 * ci + 512],
                start=(ci == 0), stop=(ci == NCH - 1)))
            cx.op("tensor", lambda e, lhsT=lhsT, pv=pv, ci=ci: e.matmul(
                pv[:, 512:1024], lhsT,
                WB[:, 16384 + 1024 * ci + 512:16384 + 1024 * (ci + 1)],
                start=(ci == 0), stop=(ci == NCH - 1)),
                inc=[("t", 1)] if (not l1g and ci == NCH - 1) else None)
            if l1g:
                cx.op("tensor", lambda e, lhsT=lhsT, pkv=pkv, ci=ci: e.matmul(
                    pkv[:, 0:512], lhsT, WB[:, 8192 + 1024 * ci:8192 + 1024 * ci + 512],
                    start=(ci == 0), stop=(ci == NCH - 1)))
                cx.op("tensor", lambda e, lhsT=lhsT, pkv=pkv, ci=ci: e.matmul(
                    pkv[:, 512:1024], lhsT,
                    WB[:, 8192 + 1024 * ci + 512:8192 + 1024 * (ci + 1)],
                    start=(ci == 0), stop=(ci == NCH - 1)),
                    inc=[("t", 1)] if ci == NCH - 1 else None)
        t_v = cx.val("t")
        sl = j % 4
        ssem = "sg%d" % sl
        stg = STG[sl]
        cx.wait("vector", "t", t_v)
        cx.wait("vector", ssem, cx.val(ssem))
        stg3 = stg.rearrange("p (h d) -> p h d", h=H)
        pv3 = pv.rearrange("p (h d) -> p h d", h=H)
        if l1 and not NO_BCAST:
            cx.op("vector", lambda e, stg3=stg3, pv3=pv3: e.tensor_tensor(
                stg3[:, :, 0:64], pv3,
                BBC2[:, :].rearrange("p (h d) -> p h d", h=H), ALU.add),
                inc=[("v", 1)])
        else:
            cx.op("vector", lambda e, stg3=stg3, pv3=pv3: e.tensor_copy(
                stg3[:, :, 0:64], pv3), inc=[("v", 1)])
        v_evac = cx.val("v")
        cx.wait("vector", "v", v_evac)
        cx.op("vector", lambda e, stg3=stg3: e.memset(stg3[:, :, 64:65], 1.0),
              inc=[("v", 1)])
        v_ones = cx.val("v")
        evac_v[par] = v_ones
        if l1g and not NO_GRAMMM:
            kst = env["AR2"][:, 0:1024]
            if not NO_KSTTT:
                cx.wait("vector", "t", kst_guard)
                cx.op("vector", lambda e, kst=kst, pkv=pkv: e.tensor_tensor(
                    kst, pkv, BBC[:, :], ALU.add), inc=[("v", 1)])
                v_kst = cx.val("v")
                cx.wait("tensor", "v", v_kst)
            if not NO_GRMM:
                for h in range(H):
                    lh = kst[:, 64 * h:64 * (h + 1)]
                    st = (j == 0 and (h == 0 or h == 8)) if not GRAM_SS else True
                    sp = False if not GRAM_SS else True
                    cx.op("tensor", lambda e, lh=lh, st=st, sp=sp, h=h: e.matmul(
                        PS[0:64, 3072 + 64 * h:3072 + 64 * (h + 1)], lh, lh,
                        start=st, stop=sp),
                        inc=[("t", 1)] if h == H - 1 else None)
                kst_guard = cx.val("t")
        cx.wait("sync", "v", v_ones)
        cx.dma("sync", lambda e, stg=stg, j=j: e.dma_start(
            out=VSd[128 * j:128 * (j + 1), :], in_=stg[:, :]), ssem)

    cx.wait("gpsimd", "t", cx.val("t"))
    srco = w["wo"].rearrange("(ci p) o -> p ci o", p=128)
    dsto = WB[:, 12352:20544].rearrange("p (ci o) -> p ci o", ci=8)
    cx.dma("gpsimd", lambda e: e.dma_start(out=dsto, in_=srco), "w")
    w_wo = cx.val("w")

    if trunc == "vpass":
        return
    if l1:
        v_xb = emit_newton(cx, env)
        cx.wait("vector", "v", v_xb)
        cx.op("vector", lambda e: e.reduce_max(
            env["STATS"][:, 15:16], env["XST"][:, :], axis=AX.X,
            apply_absolute_value=True), inc=[("v", 1)])
        v_gm = cx.val("v")
        MROW3 = env["DPREV"][0:1, 0:128]
        cx.wait("gpsimd", "v", v_gm)
        cx.dma("gpsimd", lambda e: e.dma_start(
            out=MQd[2048:2176].rearrange("(a c) -> a c", c=1),
            in_=env["STATS"][:, 15:16]), "mq")
        vq3 = cx.val("mq")
        cx.wait("gpsimd", "mq", vq3)
        cx.dma("gpsimd", lambda e, MROW3=MROW3: e.dma_start(
            out=MROW3, in_=MQd[2048:2176].rearrange("(a c) -> a c", a=1)), "mq")
        mg_done = cx.val("mq")
        cx.wait("vector", "mq", mg_done)
        cx.op("vector", lambda e, MROW3=MROW3: e.reduce_max(
            MAXR[0:1, 67:68], MROW3, axis=AX.X), inc=[("v", 1)])
    if trunc == "newton":
        return

    MROW = env["BBC2"][0:1, :]
    MROW2 = env["BBC2"][64:65, :]

    def fold_max(mbuf, row, col):
        cx.wait("gpsimd", "v", cx.val("v"))
        cx.dma("gpsimd", lambda e, mbuf=mbuf, col=col: e.dma_start(
            out=MQd[1024 * col:1024 * (col + 1)], in_=mbuf[:, :]), "mq")
        vq = cx.val("mq")
        cx.wait("gpsimd", "mq", vq)
        cx.dma("gpsimd", lambda e, row=row, col=col: e.dma_start(
            out=row, in_=MQd[1024 * col:1024 * (col + 1)]
            .rearrange("(a c) -> a c", a=1)), "mq")
        return cx.val("mq")

    fold_max(MQS, MROW, 0)
    mq_done = fold_max(MKS, MROW2, 1)
    cx.wait("vector", "mq", mq_done)
    cx.op("vector", lambda e: e.reduce_max(
        MAXR[0:1, 64:65], MROW, axis=AX.X))
    cx.op("vector", lambda e: e.reduce_max(
        MAXR[0:1, 65:66], MROW2, axis=AX.X), inc=[("v", 1)])
    v_hm = cx.val("v")
    cx.wait("vector", "v", v_hm)
    cx.op("vector", lambda e: e.tensor_tensor(
        MAXR[0:1, 66:67], MAXR[0:1, 64:65], MAXR[0:1, 65:66], ALU.mult),
        inc=[("v", 1)])
    v_b1 = cx.val("v")
    cx.wait("vector", "v", v_b1)
    if l1:
        cx.op("vector", lambda e: e.tensor_tensor(
            MAXR[0:1, 66:67], MAXR[0:1, 66:67], MAXR[0:1, 67:68], ALU.mult),
            inc=[("v", 1)])
        cx.wait("vector", "v", cx.val("v"))
    cx.op("vector", lambda e: e.tensor_scalar(
        MAXR[0:1, 66:67], MAXR[0:1, 66:67],
        (64.0 * 64.0 if l1 else 64.0) * sm_scale, 80.0,
        ALU.mult, ALU.min), inc=[("v", 1)])
    v_b2 = cx.val("v")
    cx.wait("vector", "v", v_b2)
    cx.op("vector", lambda e: e.tensor_scalar_mul(
        MAXR[0:1, 66:67], MAXR[0:1, 66:67], -1.0), inc=[("v", 1)])
    v_b3 = cx.val("v")
    cx.wait("tensor", "v", v_b3)
    cx.op("tensor", lambda e: e.matmul(PS[:, 0:1], ONES1[:, :], MAXR[0:1, 66:67],
                                       start=True, stop=True), inc=[("t", 1)])
    t_nb = cx.val("t")
    cx.wait("vector", "t", t_nb)
    cx.op("vector", lambda e: e.tensor_copy(NEGB[:, :], PS[:, 0:1]), inc=[("v", 1)])
    v_nb = cx.val("v")
    cx.wait("tensor", "v", v_nb)
    qp_mile = emit_qproj(cx, env, v_xb) if l1 else None

    qsrc = QPTd if l1 else QTd
    dn_base = cx.val("dn")
    pair_done = {}
    s_war = {0: 0, 1: 0}
    eb_war = {0: 0, 1: 0}
    o_ev = {0: 0, 1: 0}
    for p in range(8):
        qb = WB[:, 0:4096]
        kb = WB[:, 4096:8192]
        vb = WB[:, 8192:8192 + 4160]
        tp_guard = cx.val("t") if p == 0 else pair_done[p - 1]
        cx.wait("sync", "pq1", cx.val("pq1"))
        if qp_mile is not None:
            for snm, sval in qp_mile[p]:
                cx.wait("sync", snm, sval)
        cx.wait("sync", "t", tp_guard)
        q_lds = []
        for c in range(2):
            cx.dma("sync", lambda e, qb=qb, p=p, c=c: e.dma_start(
                out=qb[:, 2048 * c:2048 * (c + 1)],
                in_=qsrc[128 * p:128 * (p + 1), 2048 * c:2048 * (c + 1)]), "pq0")
            q_lds.append(cx.val("pq0"))
        cx.wait("scalar", "pq1", cx.val("pq1"))
        cx.wait("scalar", "t", tp_guard)
        k_lds = []
        for c in range(2):
            cx.dma("scalar", lambda e, kb=kb, p=p, c=c: e.dma_start(
                out=kb[:, 2048 * c:2048 * (c + 1)],
                in_=KTd[128 * p:128 * (p + 1), 2048 * c:2048 * (c + 1)]), "pk0")
            k_lds.append(cx.val("pk0"))
        wait_all_sg(cx, "gpsimd")
        cx.wait("gpsimd", "t", tp_guard)
        v_lds = []
        for g in range(4):
            cx.dma("gpsimd", lambda e, vb=vb, p=p, g=g: e.dma_start(
                out=vb.rearrange("q (j d) -> q j d", d=130)[:, 8 * g:8 * (g + 1), :],
                in_=VSd[:, 65 * 2 * p:65 * 2 * p + 130]
                .rearrange("(j q) d -> q j d", q=128)[:, 8 * g:8 * (g + 1), :]),
                "pv0")
            v_lds.append(cx.val("pv0"))
        q_seen, k_seen, v_seen = set(), set(), set()

        gates = {}

        def opair(wi):
            ob = 4 if wi % 2 == 0 else 6
            return (PS[0:65, ob * 512:(ob + 1) * 512],
                    PS[0:65, (ob + 1) * 512:(ob + 2) * 512])

        def emit_S(wi, j, jm, qb=qb, kb=kb):
            par = j % 2
            s01 = PS[:, 1024 * par:1024 * (par + 1)]
            cx.wait("tensor", "a", s_war[par])
            if wi == 0 and j < 2:
                cx.wait("tensor", "pq0", q_lds[1])
                cx.wait("tensor", "pk0", k_lds[1])
            cx.op("tensor", lambda e, kb=kb, qb=qb, s01=s01, j=j, wi=wi:
                  e.matmul(s01[:, 0:512], kb[0:64, 128 * j:128 * (j + 1)],
                           qb[0:64, SW * wi:SW * (wi + 1)],
                           start=True, stop=True))
            cx.op("tensor", lambda e, kb=kb, qb=qb, s01=s01, j=j, wi=wi:
                  e.matmul(s01[:, 512:1024], kb[64:128, 128 * j:128 * (j + 1)],
                           qb[64:128, SW * wi:SW * (wi + 1)],
                           start=True, stop=True), inc=[("t", 1)])
            t_s = cx.val("t")
            cx.wait("scalar", "t", t_s)
            cx.wait("scalar", "t", eb_war[par])
            cx.op("scalar", lambda e, s01=s01, par=par: e.activation(
                EB[par][:, :], s01, AF.Exp, bias=NEGB[:, 0:1],
                scale=sm_scale), inc=[("a", 1)])
            a_exp = cx.val("a")
            s_war[par] = a_exp
            masked = (not l1) and (j >= 4 * wi)
            if masked:
                phi = j - 4 * wi
                cx.wait("vector", "a", a_exp)
                cx.op("vector", lambda e, par=par, phi=phi: e.tensor_tensor(
                    EB[par][:, 0:512], EB[par][:, 0:512],
                    MASKS[:, SW * phi:SW * (phi + 1)], ALU.mult))
                cx.op("vector", lambda e, par=par, phi=phi: e.tensor_tensor(
                    EB[par][:, 512:1024], EB[par][:, 512:1024],
                    MASKS[:, SW * phi:SW * (phi + 1)], ALU.mult),
                    inc=[("v", 1)])
                gates[(wi, j)] = ("v", cx.val("v"))
            else:
                gates[(wi, j)] = ("a", a_exp)

        def emit_AV(wi, j, jm, vb=vb):
            par = j % 2
            o0, o1 = opair(wi)
            g_eng, g_val = gates[(wi, j)]
            cx.wait("tensor", g_eng, g_val)
            if j == 0:
                cx.wait("tensor", "v", o_ev[wi % 2])
                if wi == 0:
                    cx.wait("tensor", "pv0", v_lds[3])
            cx.op("tensor", lambda e, vb=vb, o0=o0, j=j, jm=jm, par=par:
                  e.matmul(o0, vb[:, 130 * j:130 * j + 65],
                           EB[par][:, 0:512],
                           start=(j == 0), stop=(j == jm - 1)))
            cx.op("tensor", lambda e, vb=vb, o1=o1, j=j, jm=jm, par=par:
                  e.matmul(o1, vb[:, 130 * j + 65:130 * j + 130],
                           EB[par][:, 512:1024],
                           start=(j == 0), stop=(j == jm - 1)),
                  inc=[("t", 1)])
            eb_war[par] = cx.val("t")
            if j == jm - 1:
                emit_evac(wi)

        def emit_evac(wi):
            o0, o1 = opair(wi)
            t_av = cx.val("t")
            idx = p * NWIN + wi
            sl0, sl1 = 0, 1
            cx.wait("vector", "t", t_av)
            cx.op("vector", lambda e, p=p, wi=wi, o0=o0: e.tensor_copy(
                A1[0:64, 4096 * p + SW * wi:4096 * p + SW * (wi + 1)], o0[0:64, :]))
            cx.op("vector", lambda e, p=p, wi=wi, o1=o1: e.tensor_copy(
                A1[64:128, 4096 * p + SW * wi:4096 * p + SW * (wi + 1)], o1[0:64, :]),
                inc=[("v", 1)])
            cx.wait("vector", "dn", dn_base + 16 * (2 * idx))
            cx.op("vector", lambda e, sl0=sl0, o0=o0: e.tensor_copy(
                DSTG[sl0][:, :], o0[64:65, :]))
            cx.op("vector", lambda e, sl1=sl1, o1=o1: e.tensor_copy(
                DSTG[sl1][:, :], o1[64:65, :]), inc=[("v", 1)])
            v_den = cx.val("v")
            cx.wait("gpsimd", "v", v_den)
            cx.dma("gpsimd", lambda e, sl0=sl0, p=p, wi=wi: e.dma_start(
                out=DEN[(2 * p) * 8 + wi:(2 * p) * 8 + wi + 1, :],
                in_=DSTG[sl0][:, :]), "dn")
            cx.dma("gpsimd", lambda e, sl1=sl1, p=p, wi=wi: e.dma_start(
                out=DEN[(2 * p + 1) * 8 + wi:(2 * p + 1) * 8 + wi + 1, :],
                in_=DSTG[sl1][:, :]), "dn")
            o_ev[wi % 2] = v_den

        seq = []
        for wi in range(NWIN):
            jm = NT if l1 else 4 * wi + 4
            for j in range(jm):
                seq.append((wi, j, jm))
        emit_S(*seq[0])
        for idx in range(1, len(seq)):
            emit_S(*seq[idx])
            emit_AV(*seq[idx - 1])
        emit_AV(*seq[-1])
        pair_done[p] = cx.val("t")
        if trunc == "c1" and p == 0:
            return

    APLd = env["APLd"]
    DPREV, BBC2v = env["DPREV"], env["BBC2"]
    aslot = [DPREV[0:1, 0:1024], BBC2[0:1, 0:1024], env["BBC"][0:1, 0:1024]]
    cx.wait("vector", "dn", cx.val("dn"))
    cx.op("vector", lambda e: e.reciprocal(RECIP[:, :], DEN[:, :]), inc=[("v", 1)])
    v_rec = cx.val("v")
    cx.wait("gpsimd", "v", v_rec)
    cx.dma("gpsimd", lambda e: e.dma_start(out=APLd, in_=RECIP[:, :]), "mq")
    r_full = cx.val("mq")
    cx.wait("gpsimd", "mq", r_full)
    apply_done = {}
    slot_free = {0: 0, 1: 0, 2: 0}
    for g in range(64):
        h = g // 4
        k = g % 4
        s = g % 3
        rsem = "rx%d" % (g % 2)
        cx.wait("gpsimd", "t", slot_free[s])
        cx.dma("gpsimd", lambda e, h=h, k=k, s=s: e.dma_start(
            out=aslot[s],
            in_=APLd[8 * h + 2 * k:8 * h + 2 * k + 2, :]
            .rearrange("(a r) c -> a (r c)", a=1)), rsem)
        r_mv = cx.val(rsem)
        bp = 2 + 2 * s
        bb = PS[0:64, bp * 512:(bp + 2) * 512]
        if g >= 3:
            cx.wait("tensor", "v", apply_done[g - 3])
        cx.wait("tensor", rsem, r_mv)
        cx.op("tensor", lambda e, bb=bb, s=s: e.matmul(
            bb[:, 0:512], ONES1[:, 0:64], aslot[s][:, 0:512],
            start=True, stop=True))
        cx.op("tensor", lambda e, bb=bb, s=s: e.matmul(
            bb[:, 512:1024], ONES1[:, 0:64], aslot[s][:, 512:1024],
            start=True, stop=True), inc=[("t", 1)])
        slot_free[s] = t_bc = cx.val("t")
        pi = h // 2
        half = h % 2
        ao = A1[64 * half:64 * half + 64,
                4096 * pi + 1024 * k:4096 * pi + 1024 * (k + 1)]
        cx.wait("vector", "t", t_bc)
        cx.op("vector", lambda e, ao=ao, bb=bb: e.tensor_tensor(
            ao, ao, bb, ALU.mult), inc=[("v", 1)])
        apply_done[g] = cx.val("v")
    v_apply = cx.val("v")
    cx.wb_free_lo = cx.val("t")
    if trunc == "apply":
        return

    cx.wait("gpsimd", "t", cx.val("t"))
    cx.dma("gpsimd", lambda e: e.dma_start(out=BROW[:, :], in_=w["bo"]), "w")
    w_bo = cx.val("w")
    cx.wait("tensor", "v", v_apply)
    cx.wait("tensor", "w", w_bo)
    cx.wait("tensor", "w", w_wo)
    bcast_row(cx, env, BROW, BBC, banks=(6, 7))
    v_bo = cx.val("v")
    evac_d = {0: 0, 1: 0}
    for i in range(NT):
        par = i % 2
        pd = bankpair(par)
        sl = i % 3
        rsem = "rr%d" % sl
        rt = RQ[sl]
        geng, gval = cx.rr_guard[sl]
        cx.wait("sync", geng, gval)
        cx.wait("sync", "str", cx.str_mile.get(i, 0))
        cx.dma("sync", lambda e, rt=rt, i=i: e.dma_start(
            out=rt[:, :], in_=rsrc[128 * i:128 * (i + 1), :]), rsem)
        r_ld = cx.val(rsem)
        cx.wait("tensor", "v", evac_d[par])
        for ci in range(NCH):
            lhsT = A1[:, 4096 * ci + 128 * i:4096 * ci + 128 * (i + 1)]
            mm2(cx, lhsT, WB[:, 12352 + 1024 * ci:12352 + 1024 * ci + 512],
                WB[:, 12352 + 1024 * ci + 512:12352 + 1024 * (ci + 1)],
                pd[:, 0:512], pd[:, 512:1024],
                start=(ci == 0), stop=(ci == NCH - 1), tinc=(ci == NCH - 1))
        t_wo = cx.val("t")
        cx.wait("vector", "t", t_wo)
        cx.wait("vector", rsem, r_ld)
        cx.wait("vector", "v", v_bo)
        cx.op("vector", lambda e, rt=rt, pd=pd: e.tensor_tensor(
            rt[:, :], rt[:, :], pd[:, :], ALU.add), inc=[("v", 1)])
        v_add1 = cx.val("v")
        evac_d[par] = v_add1
        cx.wait("vector", "v", v_add1)
        cx.op("vector", lambda e, rt=rt: e.tensor_tensor(
            rt[:, :], rt[:, :], BBC[:, :], ALU.add), inc=[("v", 1)])
        v_add2 = cx.val("v")
        cx.wait("scalar", "v", v_add2)
        cx.dma("scalar", lambda e, rt=rt, i=i: e.dma_start(
            out=rdst[128 * i:128 * (i + 1), :], in_=rt[:, :]), "str")
        cx.rr_guard[sl] = ("str", cx.val("str"))
        cx.str_mile[i] = cx.val("str")


def emit_newton(cx, env):
    PS, GRAM, ABD, XBD, XST, XSTB, YSB, I2S, TWOI8 = (
        env["PS"], env["GRAM"], env["ABD"], env["XBD"], env["XST"], env["XSTB"],
        env["YSB"], env["I2S"], env["TWOI8"])
    RS16, CIN, CBC, ONES64, ONES1, MQS, STG = (
        env["RS16"], env["CIN"], env["CBC"], env["ONES64"], env["ONES1"],
        env["MQS"], env["STG"])
    QTd, QPTd, WB = env["QTd"], env["QPTd"], env["WB"]

    t_gram = cx.val("t")
    cx.wait("vector", "t", t_gram)
    cx.op("vector", lambda e: e.tensor_copy(GRAM[:, 0:512], PS[0:64, 3072:3584]))
    cx.op("vector", lambda e: e.tensor_copy(GRAM[:, 512:1024], PS[0:64, 3584:4096]),
          inc=[("v", 1)])
    v_gram = cx.val("v")
    cx.wait("vector", "v", v_gram)
    cx.op("vector", lambda e: e.reduce_sum(
        RS16[:, :], GRAM[:, :].rearrange("p (h d) -> p h d", h=H), axis=AX.X,
        apply_absolute_value=True), inc=[("v", 1)])
    v_rs = cx.val("v")
    cx.wait("tensor", "v", v_rs)
    cx.op("tensor", lambda e: e.matmul(PS[0:1, 0:H], ONES64[:, :], RS16[:, :],
                                       start=True, stop=True), inc=[("t", 1)])
    t_rs = cx.val("t")
    cx.wait("vector", "t", t_rs)
    cx.op("vector", lambda e: e.tensor_copy(CIN[:, :], PS[0:1, 0:H]), inc=[("v", 1)])
    v_ci = cx.val("v")
    cx.wait("vector", "v", v_ci)
    cx.op("vector", lambda e: e.reciprocal(CIN[:, :], CIN[:, :]), inc=[("v", 1)])
    v_cr = cx.val("v")
    cx.wait("tensor", "v", v_cr)
    cx.op("tensor", lambda e: e.matmul(PS[:, 0:H], ONES1[:, :], CIN[:, :],
                                       start=True, stop=True), inc=[("t", 1)])
    t_cb = cx.val("t")
    cx.wait("vector", "t", t_cb)
    cx.op("vector", lambda e: e.tensor_copy(CBC[:, :], PS[:, 0:H]), inc=[("v", 1)])
    v_cb = cx.val("v")
    cx.wait("tensor", "v", v_cb)

    for p in range(8):
        cx.op("gpsimd", lambda e, p=p: e.memset(ABD[p][:, :], 0.0))
        cx.op("gpsimd", lambda e, p=p: e.memset(XBD[p][:, :], 0.0))
    cx.op("gpsimd", lambda e: e.nop(), inc=[("g", 1)])
    g_zero = cx.val("g")
    cx.wait("gpsimd", "v", v_gram)
    for p in range(8):
        cx.dma("gpsimd", lambda e, p=p: e.dma_start(
            out=ABD[p][0:64, 0:64], in_=GRAM[:, 64 * (2 * p):64 * (2 * p) + 64]), "mq")
        cx.dma("gpsimd", lambda e, p=p: e.dma_start(
            out=ABD[p][64:128, 64:128],
            in_=GRAM[:, 64 * (2 * p + 1):64 * (2 * p + 1) + 64]), "mq")
    abd_done = cx.val("mq")

    cx.wait("vector", "g", g_zero)
    cx.wait("vector", "v", v_cb)
    for p in range(8):
        cx.op("vector", lambda e, p=p: e.tensor_scalar(
            XST[0:64, 64 * p:64 * p + 64], I2S[0:64, :],
            CBC[0:64, 2 * p:2 * p + 1], None, ALU.mult))
        cx.op("vector", lambda e, p=p: e.tensor_scalar(
            XST[64:128, 64 * p:64 * p + 64], I2S[64:128, :],
            CBC[64:128, 2 * p + 1:2 * p + 2], None, ALU.mult))
    cx.op("vector", lambda e: e.nop(), inc=[("v", 1)])
    prev_v = cx.val("v")

    for it in range(NEWTON_ITERS):
        par = it % 2
        t1b = PS[:, par * 512:par * 512 + 512]
        x2b = PS[:, (2 + par) * 512:(2 + par) * 512 + 512]
        cx.wait("vector", "v", prev_v)
        for p in range(8):
            cx.op("vector", lambda e, p=p: e.tensor_copy(
                XBD[p][0:64, 0:64], XST[0:64, 64 * p:64 * p + 64]))
            cx.op("vector", lambda e, p=p: e.tensor_copy(
                XBD[p][64:128, 64:128], XST[64:128, 64 * p:64 * p + 64]))
        cx.op("vector", lambda e: e.nop(), inc=[("v", 1)])
        v_bd = cx.val("v")
        cx.wait("tensor", "v", v_bd)
        if it == 0:
            cx.wait("tensor", "mq", abd_done)
        for p in range(8):
            cx.op("tensor", lambda e, p=p, t1b=t1b: e.matmul(
                t1b[:, 64 * p:64 * p + 64], ABD[p][:, :],
                XST[:, 64 * p:64 * p + 64], start=True, stop=True),
                inc=[("t", 1)] if p == 7 else None)
        t_t1 = cx.val("t")
        cx.wait("vector", "t", t_t1)
        cx.op("vector", lambda e, t1b=t1b: e.scalar_tensor_tensor(
            YSB[:, :], t1b, -1.0, TWOI8[:, :], ALU.mult, ALU.add), inc=[("v", 1)])
        v_y = cx.val("v")
        cx.wait("tensor", "v", v_y)
        for p in range(8):
            cx.op("tensor", lambda e, p=p, x2b=x2b: e.matmul(
                x2b[:, 64 * p:64 * p + 64], XBD[p][:, :],
                YSB[:, 64 * p:64 * p + 64], start=True, stop=True),
                inc=[("t", 1)] if p == 7 else None)
        t_x2 = cx.val("t")
        cx.wait("vector", "t", t_x2)
        cx.op("vector", lambda e, x2b=x2b: e.tensor_copy(XST[:, :], x2b),
              inc=[("v", 1)])
        prev_v = cx.val("v")
    cx.wait("vector", "v", prev_v)
    cx.op("vector", lambda e: e.tensor_copy(XSTB[:, :], XST[:, :]), inc=[("v", 1)])
    v_xb = cx.val("v")
    return v_xb


def emit_qproj(cx, env, v_xb):
    PS, XSTB, STG = env["PS"], env["XSTB"], env["STG"]
    QTd, QPTd, WB = env["QTd"], env["QPTd"], env["WB"]
    qp_ev = {0: 0, 1: 0}
    qp_t = {}
    qp_mile = {}
    t0 = cx.val("t")
    cx.wait("sync", "pq1", cx.val("pq1"))
    for p in range(8):
        qbuf = WB[:, 4096 * (p % 2):4096 * (p % 2) + 4096]
        cx.wait("sync", "t", qp_t.get(p - 2, t0))
        cx.dma("sync", lambda e, p=p, qbuf=qbuf: e.dma_start(
            out=qbuf, in_=QTd[128 * p:128 * (p + 1), :]), "pq0")
        q_ld = cx.val("pq0")
        cx.wait("tensor", "pq0", q_ld)
        if p == 0:
            cx.wait("tensor", "v", v_xb)
        for h2 in range(2):
            for sc in range(8):
                par = sc % 2
                ob = PS[0:64, (4 + par) * 512:(4 + par) * 512 + 512]
                cx.wait("tensor", "v", qp_ev[par])
                lh = XSTB[64 * h2:64 * h2 + 64, 64 * p:64 * p + 64]
                cx.op("tensor", lambda e, lh=lh, ob=ob, h2=h2, sc=sc: e.matmul(
                    ob, lh, qbuf[64 * h2:64 * h2 + 64, 512 * sc:512 * (sc + 1)],
                    start=True, stop=True), inc=[("t", 1)])
                t_qp = cx.val("t")
                sl = (p * 16 + h2 * 8 + sc) % 4
                ssem = "sg%d" % sl
                stg = STG[sl][0:64, 0:512]
                cx.wait("vector", "t", t_qp)
                cx.wait("vector", ssem, cx.val(ssem))
                cx.op("vector", lambda e, stg=stg, ob=ob: e.tensor_copy(stg, ob),
                      inc=[("v", 1)])
                qp_ev[par] = cx.val("v")
                v_ev = cx.val("v")
                cx.wait("sync", "v", v_ev)
                cx.dma("sync", lambda e, p=p, h2=h2, sc=sc, stg=stg: e.dma_start(
                    out=QPTd[128 * p + 64 * h2:128 * p + 64 * h2 + 64,
                             512 * sc:512 * (sc + 1)], in_=stg), ssem)
        qp_t[p] = cx.val("t")
        qp_mile[p] = [("sg%d" % sl, cx.val("sg%d" % sl)) for sl in range(4)]
    return qp_mile


def emit_ffn(cx, env, w, rsrc, rdst):
    A1, WB, PS, STG, AR2, RQ = (env["A1"], env["WB"], env["PS"], env["STG"],
                                env["AR2"], env["RQ"])
    XT, MUR, MUK, DPREV, XPC, XPREV = (env["XT"], env["MUR"], env["MUK"],
                                       env["DPREV"], env["XPC"], env["XPREV"])

    def bankpair(i):
        return PS[:, 1024 * i:1024 * (i + 1)]

    t_entry = cx.val("t")

    for name, col0 in (("wr", 0), ("wkf", 8192), ("wvf", 16384)):
        cx.wait("gpsimd", "t", cx.wb_free_lo if name == "wr" else t_entry)
        src = w[name].rearrange("(ci p) o -> p ci o", p=128)
        dst = WB[:, col0:col0 + 8192].rearrange("p (ci o) -> p ci o", ci=8)
        cx.dma("gpsimd", lambda e, dst=dst, src=src: e.dma_start(out=dst, in_=src), "w")
    cx.dma("gpsimd", lambda e: e.dma_start(out=MUR[:, :], in_=w["mur"]), "w")
    cx.dma("gpsimd", lambda e: e.dma_start(out=MUK[:, :], in_=w["muk"]), "w")
    w_all = cx.val("w")
    cx.wait("tensor", "w", w_all)
    cx.wait("vector", "w", w_all)

    bank_free = {0: ("v", 0), 1: ("v", 0)}
    state = {"v_mix": 0, "v_mult_last": 0, "dprev_free": ("v", 0),
             "xprev_free": 0}

    def prep_quarter(q):
        t0, t1 = q * 8, (q + 1) * 8
        if q == 0:
            cx.op("gpsimd", lambda e: e.memset(XPC[:, :], 0.0), inc=[("g", 1)])
            cx.wait("vector", "g", cx.val("g"))
            cx.op("vector", lambda e: e.nop(), inc=[("v", 1)])
        else:
            cx.wait("vector", "v", state["v_mix"])
            for ci in range(NCH):
                cx.op("vector", lambda e, ci=ci: e.tensor_copy(
                    XPC[:, ci:ci + 1], A1[:, 1024 * ci + 1023:1024 * ci + 1024]))
            cx.op("vector", lambda e: e.nop(), inc=[("v", 1)])
        v_bnd = cx.val("v")
        emit_ln(cx, env, rsrc, t0, t1, act_sum=True)
        wait_all_sg(cx, "scalar")
        if q == 0:
            cx.wait("scalar", "t", t_entry)
        cx.wait("scalar", "v", v_bnd)
        for ci in range(NCH):
            dst = A1[:, 1024 * ci:1024 * ci + 1024]
            src = XT[t0 * 128:t1 * 128, 128 * ci:128 * ci + 128]
            cx.dma("scalar", lambda e, dst=dst, src=src: e.dma_start_transpose(
                dst, src), "tr")
        return cx.val("tr")

    def emit_mix_half(ci, mu, dst_off, tr_q, t_gate):
        xt = A1[:, 1024 * ci:1024 * ci + 1024]
        xd = A1[:, dst_off + 1024 * ci:dst_off + 1024 * ci + 1024]
        if ci == 0:
            cx.wait("vector", "tr", tr_q)
            cx.wait("vector", "t", t_gate)
            cx.wait("vector", "w", w_all)
        cx.wait("vector", "v", state["xprev_free"])
        cx.op("vector", lambda e, xt=xt: e.tensor_tensor(
            XPREV[:, 1:1024], xt[:, 1:1024], xt[:, 0:1023], ALU.subtract))
        cx.op("vector", lambda e, xt=xt, ci=ci: e.tensor_tensor(
            XPREV[:, 0:1], xt[:, 0:1], XPC[:, ci:ci + 1], ALU.subtract),
            inc=[("v", 1)])
        v_d = cx.val("v")
        cx.wait("vector", "v", v_d)
        cx.op("vector", lambda e, xd=xd, xt=xt, ci=ci, mu=mu: e.scalar_tensor_tensor(
            xd[:, 1:1024], XPREV[:, 1:1024], mu[:, ci:ci + 1], xt[:, 0:1023],
            ALU.mult, ALU.add))
        cx.op("vector", lambda e, xd=xd, ci=ci, mu=mu: e.scalar_tensor_tensor(
            xd[:, 0:1], XPREV[:, 0:1], mu[:, ci:ci + 1], XPC[:, ci:ci + 1],
            ALU.mult, ALU.add), inc=[("v", 1)])
        state["v_mix"] = cx.val("v")
        state["xprev_free"] = state["v_mix"]

    def r_pass(q, v_xr):
        for jl in range(8):
            par = jl % 2
            pp = bankpair(par)
            cx.wait("tensor", *bank_free[par])
            if jl == 0:
                cx.wait("tensor", "v", v_xr)
            for ci in range(NCH):
                lhsT = A1[:, 8192 + 1024 * ci + 128 * jl:
                          8192 + 1024 * ci + 128 * (jl + 1)]
                mm2(cx, lhsT, WB[:, 1024 * ci:1024 * ci + 512],
                    WB[:, 1024 * ci + 512:1024 * (ci + 1)],
                    pp[:, 0:512], pp[:, 512:1024],
                    start=(ci == 0), stop=(ci == NCH - 1), tinc=(ci == NCH - 1))
            t_r = cx.val("t")
            cx.wait("scalar", "t", t_r)
            if jl == 0:
                cx.wait("scalar", "v", state["v_mult_last"])
            cx.op("scalar", lambda e, jl=jl, pp=pp: e.activation(
                AR2[:, 1024 * jl:1024 * (jl + 1)], pp, AF.Sigmoid), inc=[("a", 1)])
            bank_free[par] = ("a", cx.val("a"))

    def k_pass(q, v_xk, mix_next):
        for dt in range(8):
            par = dt % 2
            pp = bankpair(par)
            cx.wait("tensor", *bank_free[par])
            if dt == 0:
                cx.wait("tensor", "v", v_xk)
            for ci in range(NCH):
                lhsT = WB[:, 8192 + 1024 * ci + 128 * dt:
                          8192 + 1024 * ci + 128 * dt + 128]
                rhs_a = A1[:, 16384 + 1024 * ci:16384 + 1024 * ci + 512]
                rhs_b = A1[:, 16384 + 1024 * ci + 512:16384 + 1024 * (ci + 1)]
                mm2(cx, lhsT, rhs_a, rhs_b, pp[:, 0:512], pp[:, 512:1024],
                    start=(ci == 0), stop=(ci == NCH - 1), tinc=(ci == NCH - 1))
            t_k = cx.val("t")
            cx.wait("scalar", "t", t_k)
            cx.wait("scalar", *state["dprev_free"])
            cx.op("scalar", lambda e, pp=pp: e.activation(
                DPREV[:, :], pp[:, 0:1024], AF.Relu), inc=[("a", 1)])
            a_rl = cx.val("a")
            bank_free[par] = ("a", a_rl)
            cx.wait("scalar", "a", a_rl)
            cx.op("scalar", lambda e, dt=dt: e.activation(
                A1[:, 24576 + 1024 * dt:24576 + 1024 * (dt + 1)], DPREV[:, :],
                AF.Square), inc=[("a", 1)])
            state["dprev_free"] = ("a", cx.val("a"))
            if mix_next is not None:
                emit_mix_half(dt, MUR, 8192, mix_next[0], mix_next[1])
        return cx.val("t"), cx.val("a")

    def v_pass(q, t_k_all, a_k_all, mix_next):
        t0 = q * 8
        for jl in range(8):
            i = t0 + jl
            par = jl % 2
            pp = bankpair(par)
            sl = i % 3
            rsem = "rr%d" % sl
            rt = RQ[sl]
            geng, gval = cx.rr_guard[sl]
            cx.wait("sync", geng, gval)
            cx.wait("sync", "str", cx.str_mile.get(i, 0))
            cx.dma("sync", lambda e, rt=rt, i=i: e.dma_start(
                out=rt[:, :], in_=rsrc[128 * i:128 * (i + 1), :]), rsem)
            r_ld = cx.val(rsem)
            cx.wait("tensor", *bank_free[par])
            if jl == 0:
                cx.wait("tensor", "a", a_k_all)
            for dt in range(NCH):
                lhsT = A1[:, 24576 + 1024 * dt + 128 * jl:
                          24576 + 1024 * dt + 128 * (jl + 1)]
                mm2(cx, lhsT, WB[:, 16384 + 1024 * dt:16384 + 1024 * dt + 512],
                    WB[:, 16384 + 1024 * dt + 512:16384 + 1024 * (dt + 1)],
                    pp[:, 0:512], pp[:, 512:1024],
                    start=(dt == 0), stop=(dt == NCH - 1), tinc=(dt == NCH - 1))
            t_v = cx.val("t")
            cx.wait("vector", "t", t_v)
            cx.wait("vector", *state["dprev_free"])
            cx.op("vector", lambda e, pp=pp, jl=jl: e.tensor_tensor(
                DPREV[:, :], pp[:, 0:1024], AR2[:, 1024 * jl:1024 * (jl + 1)],
                ALU.mult), inc=[("v", 1)])
            v_m = cx.val("v")
            bank_free[par] = ("v", v_m)
            cx.wait("vector", rsem, r_ld)
            cx.wait("vector", "v", v_m)
            cx.op("vector", lambda e, rt=rt: e.tensor_tensor(
                rt[:, :], rt[:, :], DPREV[:, :], ALU.add), inc=[("v", 1)])
            v_add = cx.val("v")
            state["dprev_free"] = ("v", v_add)
            state["v_mult_last"] = v_add
            cx.wait("scalar", "v", v_add)
            cx.dma("scalar", lambda e, rt=rt, i=i: e.dma_start(
                out=rdst[128 * i:128 * (i + 1), :], in_=rt[:, :]), "str")
            cx.rr_guard[sl] = ("str", cx.val("str"))
            cx.str_mile[i] = cx.val("str")
            if mix_next is not None:
                emit_mix_half(jl, MUK, 16384, mix_next[0], mix_next[1])

    tr0 = prep_quarter(0)
    for ci in range(NCH):
        emit_mix_half(ci, MUR, 8192, tr0, t_entry)
    v_xr = state["v_mix"]
    for ci in range(NCH):
        emit_mix_half(ci, MUK, 16384, tr0, t_entry)
    v_xk = state["v_mix"]
    for q in range(NQ):
        r_pass(q, v_xr)
        tr_next = prep_quarter(q + 1) if q < NQ - 1 else None
        t_r_all = cx.val("t")
        t_k_all, a_k_all = k_pass(q, v_xk,
                                  (tr_next, t_r_all) if q < NQ - 1 else None)
        if q == NQ - 1:
            cx.wb_free_lo = t_k_all
        v_xr = state["v_mix"]
        v_pass(q, t_k_all, a_k_all,
               (tr_next, t_k_all) if q < NQ - 1 else None)
        v_xk = state["v_mix"]



def _prep_weights(inputs):
    f = np.ascontiguousarray
    out = {}
    out[("attn0", 0)] = dict(
        wq=f(inputs["L0_attn_Wq"].transpose(1, 0, 2).reshape(C, C)),
        wk=f(inputs["L0_attn_Wk"].transpose(1, 0, 2).reshape(C, C)),
        wv=f(inputs["L0_attn_Wv"].transpose(1, 0, 2).reshape(C, C)),
        wo=f(inputs["L0_attn_Wo"].T),
        bo=f(inputs["L0_attn_bo"].reshape(1, C)),
    )
    out[("attn1", 1)] = dict(
        wq=f(inputs["L1_attn_Wq"].T),
        wk=f(inputs["L1_attn_Wk"].T),
        wv=f(inputs["L1_attn_Wv"].T),
        wo=f(inputs["L1_attn_Wfc"].T),
        bq=f(inputs["L1_attn_bq"].reshape(8, 128).T),
        bk=f(inputs["L1_attn_bk"].reshape(8, 128).T),
        bvr=f(inputs["L1_attn_bv"].reshape(1, C)),
        bkr=f(inputs["L1_attn_bk"].reshape(1, C)),
        bo=f(inputs["L1_attn_bfc"].reshape(1, C)),
    )
    for L in (0, 1):
        p = "L%d_ffn_" % L
        out[("ffn", L)] = dict(
            wr=f(inputs[p + "W_r"].T),
            wkf=f(inputs[p + "W_k"].T),
            wvf=f(inputs[p + "W_v"].T),
            mur=f(inputs[p + "mu_r"].reshape(8, 128).T),
            muk=f(inputs[p + "mu_k"].reshape(8, 128).T),
        )
    return out


FUSED = True


def kernel(**inputs):
    x = np.ascontiguousarray(np.asarray(inputs["L0_attn_x"], dtype=np.float32))
    wmaps = _prep_weights({k: np.asarray(v) for k, v in inputs.items()})
    B = x.shape[0]
    ranges = [(0, 4)] if FUSED else [(i, i + 1) for i in range(4)]
    cur = [np.ascontiguousarray(x[b]) for b in range(B)]
    for (lo, hi) in ranges:
        nc = build_program(lo, hi)
        in_maps = []
        for b in range(B):
            m = {"xin": cur[b]}
            for si, (kind, L) in enumerate(SUBS[lo:hi]):
                for k, arr in wmaps[(kind, L)].items():
                    m["s%d_%s" % (si, k)] = arr
            in_maps.append(m)
        res = run_bass_kernel_spmd(nc, in_maps, core_ids=list(range(8)))
        cur = [np.asarray(res.results[b]["xout"]) for b in range(B)]
    return np.stack(cur, axis=0)


# version 29
# speedup vs baseline: 1.0031x; 1.0031x over previous
import numpy as np
import concourse.bass as bass
import concourse.mybir as mybir
from concourse.bass_utils import run_bass_kernel_spmd

F32 = mybir.dt.float32
BF16 = mybir.dt.bfloat16
AF = mybir.ActivationFunctionType
ALU = mybir.AluOpType
AX = mybir.AxisListType

T = 4096
C = 1024
H = 16
D = 64
NT = T // 128
NCH = C // 128
SW = 512
NWIN = T // SW
NQ = 4
NEWTON_ITERS = 24
EPS = 1e-5

SUBS = [("attn0", 0), ("ffn", 0), ("attn1", 1), ("ffn", 1)]
TRUNC_LAST = "full"
INTERLEAVE_PROJ = False
NO_GRAM = False
NO_GRAMMM = False
NO_KSTTT = False
NO_GRMM = False
GRAM_SS = False
NO_BCAST = False

SEM_NAMES = (
    ["t", "v", "a", "g"]
    + ["rr%d" % i for i in range(4)]
    + ["sg%d" % i for i in range(4)]
    + ["pq0", "pq1", "pk0", "pk1", "pv0"]
    + ["tr", "w", "str", "dn", "mq"]
    + ["rx%d" % i for i in range(4)]
)


class Ctx:

    def __init__(self, me, eng, sems):
        self.me = me
        self.eng = eng
        self.sems = sems
        self.cnt = {k: 0 for k in sems}
        self.rr_guard = {i: ("v", 0) for i in range(3)}
        self.ln_state = {"var": {0: 0, 1: 0}, "s2": {0: 0, 1: 0},
                         "norm": {0: 0, 1: 0},
                         "ln": {0: 0, 1: 0}}
        self.str_mile = {}
        self.wb_free_lo = 0

    def inc(self, s, n=1):
        self.cnt[s] += n
        return self.cnt[s]

    def val(self, s):
        return self.cnt[s]

    def op(self, engine, builder, inc=None):
        ins = builder(self.eng) if self.me == engine else None
        if inc:
            for s, n in inc:
                self.inc(s, n)
                if ins is not None:
                    ins.then_inc(self.sems[s], n)
        return ins

    def wait(self, engine, s, v):
        if v <= 0:
            return
        if self.me == engine:
            self.eng.wait_ge(self.sems[s], v)

    def dma(self, engine, builder, sem):
        return self.op(engine, builder, inc=[(sem, 16)])


def build_program(sub_lo, sub_hi):
    nc = bass.Bass("TRN2", target_bir_lowering=False)

    xin = nc.dram_tensor("xin", [T, C], F32, kind="ExternalInput").ap()
    xout = nc.dram_tensor("xout", [T, C], F32, kind="ExternalOutput").ap()

    subs = SUBS[sub_lo:sub_hi]
    win = {}
    for si, (kind, L) in enumerate(subs):
        pre = "s%d_" % si
        d = {}
        if kind == "ffn":
            for nm, shp in (("wr", [C, C]), ("wkf", [C, C]), ("wvf", [C, C]),
                            ("mur", [128, 8]), ("muk", [128, 8])):
                d[nm] = nc.dram_tensor(pre + nm, shp, F32, kind="ExternalInput").ap()
        else:
            for nm in ("wq", "wk", "wv", "wo"):
                d[nm] = nc.dram_tensor(pre + nm, [C, C], F32, kind="ExternalInput").ap()
            d["bo"] = nc.dram_tensor(pre + "bo", [1, C], F32, kind="ExternalInput").ap()
            if kind == "attn1":
                for nm, shp in (("bq", [128, 8]), ("bk", [128, 8]),
                                ("bvr", [1, C]), ("bkr", [1, C])):
                    d[nm] = nc.dram_tensor(pre + nm, shp, F32, kind="ExternalInput").ap()
        win[si] = d

    XT = nc.dram_tensor("XTs", [T, C], BF16, kind="Internal").ap()
    QTd = nc.dram_tensor("QTs", [C, T], BF16, kind="Internal").ap()
    KTd = nc.dram_tensor("KTs", [C, T], BF16, kind="Internal").ap()
    QPTd = nc.dram_tensor("QPTs", [C, T], BF16, kind="Internal").ap()
    VSd = nc.dram_tensor("VSs", [T, H * 65], BF16, kind="Internal").ap()
    MQd = nc.dram_tensor("MQs", [3 * C], F32, kind="Internal").ap()
    MKd = nc.dram_tensor("MKs", [C], F32, kind="Internal").ap()
    RAd = nc.dram_tensor("RAs", [T, C], F32, kind="Internal").ap()
    RBd = nc.dram_tensor("RBs", [T, C], F32, kind="Internal").ap()
    APLd = nc.dram_tensor("APLs", [128, SW], F32, kind="Internal").ap()

    n_sub = len(subs)
    rsrc, rdst = [None] * n_sub, [None] * n_sub
    cur = xin
    for si in range(n_sub):
        rsrc[si] = cur
        rdst[si] = xout if si == n_sub - 1 else (RAd if si % 2 == 0 else RBd)
        cur = rdst[si]

    A1 = nc.alloc_sbuf_tensor("A1", [128, 32768], BF16).ap()
    WB = nc.alloc_sbuf_tensor("WB", [128, 3 * 8192], BF16).ap()
    RQ = [nc.alloc_sbuf_tensor("RQ%d" % i, [128, 1024], F32).ap() for i in range(3)]
    STG = [nc.alloc_sbuf_tensor("STG%d" % i, [128, 1040], BF16).ap() for i in range(4)]
    AR2 = nc.alloc_sbuf_tensor("AR2", [128, 8192], BF16).ap()
    STATS = nc.alloc_sbuf_tensor("STATS", [128, 16], F32).ap()
    DPREV = nc.alloc_sbuf_tensor("DPREV", [128, 1024], F32).ap()
    XPC = nc.alloc_sbuf_tensor("XPC", [128, 8], BF16).ap()
    XPREV = nc.alloc_sbuf_tensor("XPREV", [128, 1024], BF16).ap()
    MASKS = nc.alloc_sbuf_tensor("MASKS", [128, 4 * SW], BF16).ap()
    EB = [nc.alloc_sbuf_tensor("EB%d" % i, [128, 1024], BF16).ap() for i in range(2)]
    DEN = nc.alloc_sbuf_tensor("DEN", [128, SW], F32).ap()
    RECIP = nc.alloc_sbuf_tensor("RECIP", [128, SW], F32).ap()
    DSTG = [nc.alloc_sbuf_tensor("DSTG%d" % i, [1, SW], F32).ap() for i in range(2)]
    RSTG = [nc.alloc_sbuf_tensor("RSTG%d" % i, [1, SW], F32).ap() for i in range(2)]
    ONES1 = nc.alloc_sbuf_tensor("ONES1", [1, 128], F32).ap()
    ONES64 = nc.alloc_sbuf_tensor("ONES64", [64, 1], F32).ap()
    BROW = nc.alloc_sbuf_tensor("BROW", [1, C], F32).ap()
    BBC = nc.alloc_sbuf_tensor("BBC", [128, C], F32).ap()
    BBC2 = nc.alloc_sbuf_tensor("BBC2", [128, C], F32).ap()
    MQS = nc.alloc_sbuf_tensor("MQS", [128, 8], F32).ap()
    MQSQ = nc.alloc_sbuf_tensor("MQSQ", [64, 16], F32).ap()
    MKS = nc.alloc_sbuf_tensor("MKS", [128, 8], F32).ap()
    MAXR = nc.alloc_sbuf_tensor("MAXR", [1, 80], F32).ap()
    NEGB = nc.alloc_sbuf_tensor("NEGB", [128, 1], F32).ap()
    MUR = nc.alloc_sbuf_tensor("MUR", [128, 8], F32).ap()
    MUK = nc.alloc_sbuf_tensor("MUK", [128, 8], F32).ap()
    BQP = nc.alloc_sbuf_tensor("BQP", [128, 8], F32).ap()
    BKP = nc.alloc_sbuf_tensor("BKP", [128, 8], F32).ap()
    GRAM = nc.alloc_sbuf_tensor("GRAM", [64, H * 64], F32).ap()
    ABD = [nc.alloc_sbuf_tensor("ABD%d" % p, [128, 128], F32).ap() for p in range(8)]
    XBD = [nc.alloc_sbuf_tensor("XBD%d" % p, [128, 128], F32).ap() for p in range(8)]
    XST = nc.alloc_sbuf_tensor("XST", [128, 512], F32).ap()
    XSTB = nc.alloc_sbuf_tensor("XSTB", [128, 512], BF16).ap()
    YSB = nc.alloc_sbuf_tensor("YSB", [128, 512], F32).ap()
    I2S = nc.alloc_sbuf_tensor("I2S", [128, 64], F32).ap()
    TWOI8 = nc.alloc_sbuf_tensor("TWOI8", [128, 512], F32).ap()
    RS16 = nc.alloc_sbuf_tensor("RS16", [64, H], F32).ap()
    CIN = nc.alloc_sbuf_tensor("CIN", [1, H], F32).ap()
    CBC = nc.alloc_sbuf_tensor("CBC", [128, H], F32).ap()
    EPSB = nc.alloc_sbuf_tensor("EPSB", [128, 1], F32).ap()

    PS = nc.alloc_psum_tensor("PS", [128, 4096], F32).ap()

    sems = {s: nc.alloc_semaphore(s) for s in SEM_NAMES}

    env = dict(locals())

    def emit(me, eng):
        cx = Ctx(me, eng, sems)
        emit_prologue(cx, env)
        for si, (kind, L) in enumerate(subs):
            if kind == "ffn":
                emit_ffn(cx, env, win[si], rsrc[si], rdst[si])
            else:
                emit_attn(cx, env, kind, win[si], rsrc[si], rdst[si],
                          trunc=(TRUNC_LAST if si == len(subs) - 1 else "full"))
        cx.wait("sync", "str", cx.val("str"))

    with nc.Block() as block:
        block.tensor(lambda eng: emit("tensor", eng))
        block.vector(lambda eng: emit("vector", eng))
        block.scalar(lambda eng: emit("scalar", eng))
        block.gpsimd(lambda eng: emit("gpsimd", eng))
        block.sync(lambda eng: emit("sync", eng))

    return nc



def emit_prologue(cx, env):
    MASKS, ONES1, ONES64, I2S, TWOI8 = (env["MASKS"], env["ONES1"],
                                        env["ONES64"], env["I2S"], env["TWOI8"])
    for phi in range(4):
        m = MASKS[:, phi * SW:(phi + 1) * SW]
        cx.op("gpsimd", lambda e, m=m: e.memset(m, 1.0))
        cx.op("gpsimd", lambda e, m=m, phi=phi: e.affine_select(
            out=m, in_=m, compare_op=ALU.is_ge, fill=0.0,
            base=-128 * phi, pattern=[[1, SW]], channel_multiplier=-1))
    cx.op("gpsimd", lambda e: e.memset(ONES1[:, :], 1.0))
    cx.op("gpsimd", lambda e: e.memset(env["EPSB"][:, :], EPS))
    cx.op("gpsimd", lambda e: e.memset(ONES64[:, :], 1.0))
    cx.op("gpsimd", lambda e: e.memset(I2S[:, :], 1.0))
    cx.op("gpsimd", lambda e: e.affine_select(
        out=I2S[0:64, :], in_=I2S[0:64, :], compare_op=ALU.is_equal,
        fill=0.0, base=0, pattern=[[-1, 64]], channel_multiplier=1))
    cx.op("gpsimd", lambda e: e.affine_select(
        out=I2S[64:128, :], in_=I2S[64:128, :], compare_op=ALU.is_equal,
        fill=0.0, base=64, pattern=[[-1, 64]], channel_multiplier=1))
    for p in range(8):
        cx.op("gpsimd", lambda e, p=p: e.tensor_scalar_mul(
            TWOI8[:, 64 * p:64 * p + 64], I2S[:, :], 2.0))
    cx.op("gpsimd", lambda e: e.nop(), inc=[("g", 1)])


def emit_ln(cx, env, rsrc, t0, t1, window_cb=None, tile_cb=None,
            act_sum=False):
    RQ, STG, STATS, XT = (env["RQ"], env["STG"], env["STATS"], env["XT"])
    st = cx.ln_state
    vnorm = {}

    def emit_tr(i):
        ssl = i % 4
        stg = STG[ssl][:, 0:1024]
        cx.wait("scalar", "a", vnorm[i])
        cx.dma("scalar", lambda e, stg=stg, i=i: e.dma_start(
            out=XT[128 * i:128 * (i + 1), :], in_=stg), "sg%d" % ssl)
        if window_cb is not None and i % 8 == 7:
            window_cb(i)

    var_of = {}

    def stage_a(i):
        sl = i % 3
        rsem = "rr%d" % sl
        ssl = i % 4
        ssem = "sg%d" % ssl
        rt = RQ[sl]
        stg = STG[ssl][:, 0:1024]
        lane = (i - t0) % 2
        b = 8 * lane
        geng, gval = cx.rr_guard[sl]
        cx.wait("sync", geng, gval)
        cx.wait("sync", "str", cx.str_mile.get(i, 0))
        cx.dma("sync", lambda e, rt=rt, i=i: e.dma_start(
            out=rt[:, :], in_=rsrc[128 * i:128 * (i + 1), :]), rsem)
        v_ld = cx.val(rsem)
        cx.wait("scalar", "v", st["var"][lane])
        cx.wait("scalar", rsem, v_ld)
        cx.wait("scalar", ssem, cx.val(ssem))
        cx.op("scalar", lambda e, rt=rt, stg=stg, b=b: e.activation(
            stg, rt[:, :], AF.Square, accum_out=STATS[:, b + 1:b + 2]),
            inc=[("a", 1)])
        a_sq = cx.val("a")
        if act_sum:
            cx.wait("scalar", "v", st["s2"][lane])
            cx.op("scalar", lambda e, rt=rt, stg=stg, b=b: e.activation(
                stg, rt[:, :], AF.Identity, accum_out=STATS[:, b:b + 1]),
                inc=[("a", 1)])
            a_sum = cx.val("a")
            cx.wait("vector", "a", a_sum)
        else:
            cx.wait("vector", "v", st["s2"][lane])
            cx.wait("vector", rsem, v_ld)
            cx.op("vector", lambda e, rt=rt, b=b: e.reduce_sum(
                STATS[:, b:b + 1], rt[:, :], axis=AX.X), inc=[("v", 1)])
            v_red = cx.val("v")
            cx.wait("vector", "v", v_red)
        cx.wait("vector", "a", st["norm"][lane])
        cx.op("vector", lambda e, b=b: e.tensor_scalar_mul(
            STATS[:, b + 2:b + 3], STATS[:, b:b + 1], -1.0 / C))
        cx.op("vector", lambda e, b=b: e.tensor_tensor(
            STATS[:, b + 3:b + 4], STATS[:, b:b + 1], STATS[:, b:b + 1],
            ALU.mult), inc=[("v", 1)])
        st["s2"][lane] = v_s2 = cx.val("v")
        cx.wait("vector", "a", a_sq)
        cx.wait("vector", "a", st["ln"][lane])
        cx.wait("vector", "v", v_s2)
        cx.op("vector", lambda e, b=b: e.scalar_tensor_tensor(
            STATS[:, b + 4:b + 5], STATS[:, b + 3:b + 4], -1.0 / C,
            STATS[:, b + 1:b + 2], ALU.mult, ALU.add), inc=[("v", 1)])
        st["var"][lane] = var_of[i] = cx.val("v")

    def stage_b(i):
        sl = i % 3
        ssl = i % 4
        ssem = "sg%d" % ssl
        rt = RQ[sl]
        stg = STG[ssl][:, 0:1024]
        lane = (i - t0) % 2
        b = 8 * lane
        cx.wait("scalar", "v", var_of[i])
        cx.op("scalar", lambda e, b=b: e.activation(
            STATS[:, b + 5:b + 6], STATS[:, b + 4:b + 5], AF.Ln,
            bias=env["EPSB"][:, :], scale=1.0 / C), inc=[("a", 1)])
        a_ln = cx.val("a")
        st["ln"][lane] = a_ln
        cx.wait("scalar", "a", a_ln)
        cx.wait("scalar", "a", st["norm"][lane])
        cx.op("scalar", lambda e, b=b: e.activation(
            STATS[:, b + 6:b + 7], STATS[:, b + 5:b + 6], AF.Exp, scale=-0.5),
            inc=[("a", 1)])
        cx.op("scalar", lambda e, b=b: e.activation(
            STATS[:, b + 7:b + 8], STATS[:, b + 2:b + 3], AF.Identity,
            scale=STATS[:, b + 6:b + 7]), inc=[("a", 1)])
        a_bias = cx.val("a")
        cx.wait("scalar", "a", a_bias)
        cx.wait("scalar", ssem, cx.val(ssem))
        cx.op("scalar", lambda e, rt=rt, stg=stg, b=b: e.activation(
            stg, rt[:, :], AF.Identity, bias=STATS[:, b + 7:b + 8],
            scale=STATS[:, b + 6:b + 7]), inc=[("a", 1)])
        a_norm = cx.val("a")
        st["norm"][lane] = a_norm
        cx.rr_guard[sl] = ("a", a_norm)
        vnorm[i] = a_norm

    stage_a(t0)
    for i in range(t0 + 1, t1):
        stage_a(i)
        stage_b(i - 1)
        if i - 3 >= t0:
            emit_tr(i - 3)
        if tile_cb is not None:
            tile_cb(i)
    stage_b(t1 - 1)
    for i in (t1 - 3, t1 - 2, t1 - 1):
        if i >= t0:
            emit_tr(i)


def wait_all_sg(cx, engine):
    for sl in range(4):
        cx.wait(engine, "sg%d" % sl, cx.val("sg%d" % sl))
    cx.wait(engine, "pq1", cx.val("pq1"))


def mm2(cx, lhsT, rhs_a, rhs_b, out_a, out_b, start, stop, tinc=False):
    cx.op("tensor", lambda e: e.matmul(out_a, lhsT, rhs_a, start=start, stop=stop))
    cx.op("tensor", lambda e: e.matmul(out_b, lhsT, rhs_b, start=start, stop=stop),
          inc=[("t", 1)] if tinc else None)


def bcast_row(cx, env, row_ap, dst, banks=(4, 5)):
    PS, ONES1 = env["PS"], env["ONES1"]
    b0 = PS[:, 512 * banks[0]:512 * banks[0] + 512]
    b1 = PS[:, 512 * banks[1]:512 * banks[1] + 512]
    cx.op("tensor", lambda e: e.matmul(b0, ONES1[:, :], row_ap[:, 0:512],
                                       start=True, stop=True))
    cx.op("tensor", lambda e: e.matmul(b1, ONES1[:, :], row_ap[:, 512:1024],
                                       start=True, stop=True), inc=[("t", 1)])
    t_b = cx.val("t")
    cx.wait("vector", "t", t_b)
    cx.op("vector", lambda e: e.tensor_copy(dst[:, 0:512], b0))
    cx.op("vector", lambda e: e.tensor_copy(dst[:, 512:1024], b1), inc=[("v", 1)])
    v_b = cx.val("v")
    cx.wait("tensor", "v", v_b)
    return v_b


def emit_attn(cx, env, kind, w, rsrc, rdst, trunc="full"):
    A1, WB, PS, STG, RQ, AR2 = (env["A1"], env["WB"], env["PS"],
                                env["STG"], env["RQ"], env["AR2"])
    QTd, KTd, QPTd, VSd, XT = (env["QTd"], env["KTd"], env["QPTd"], env["VSd"],
                               env["XT"])
    MQS, MKS, MAXR, NEGB = env["MQS"], env["MKS"], env["MAXR"], env["NEGB"]
    MQd = env["MQd"]
    BROW, BBC, BBC2, BQP, BKP = (env["BROW"], env["BBC"], env["BBC2"],
                                 env["BQP"], env["BKP"])
    ONES1 = env["ONES1"]
    MASKS, EB, DEN, RECIP, DSTG, RSTG = (env["MASKS"], env["EB"], env["DEN"],
                                         env["RECIP"], env["DSTG"], env["RSTG"])
    l1 = (kind == "attn1")
    sm_scale = (1.0 / np.sqrt(D)) if not l1 else (1.0 / np.sqrt(C))

    def bank(i):
        return PS[:, 512 * i:512 * (i + 1)]

    def bankpair(i):
        return PS[:, 1024 * i:1024 * (i + 1)]

    t_entry = cx.val("t")

    for name, col0, buf in (("wq", 0, WB), ("wk", 8192, WB), ("wv", 16384, WB)):
        cx.wait("gpsimd", "t", t_entry if name == "wv" else cx.wb_free_lo)
        src = w[name].rearrange("(ci p) o -> p ci o", p=128)
        dst = buf[:, col0:col0 + 8192].rearrange("p (ci o) -> p ci o", ci=8)
        cx.dma("gpsimd", lambda e, dst=dst, src=src: e.dma_start(out=dst, in_=src), "w")
    if l1:
        cx.dma("gpsimd", lambda e: e.dma_start(out=BQP[:, :], in_=w["bq"]), "w")
        cx.dma("gpsimd", lambda e: e.dma_start(out=BKP[:, :], in_=w["bk"]), "w")
    w_all = cx.val("w")

    cx.op("gpsimd", lambda e: e.memset(MQS[:, :], 0.0))
    cx.op("gpsimd", lambda e: e.memset(MKS[:, :], 0.0), inc=[("g", 1)])
    g_z = cx.val("g")
    trw = {}

    def phase_a_window(j):
        w2 = j // 8
        if w2 == 0:
            cx.wait("scalar", "t", t_entry)
        wait_all_sg(cx, "scalar")
        for ci in range(NCH):
            dst = A1[:, ci * 4096 + 1024 * w2:ci * 4096 + 1024 * (w2 + 1)]
            src = XT[1024 * w2:1024 * (w2 + 1), 128 * ci:128 * ci + 128]
            cx.dma("scalar", lambda e, dst=dst, src=src: e.dma_start_transpose(
                dst, src), "tr")
        trw[w2] = cx.val("tr")

    evac_guard = {0: 0, 1: 0}

    def proj_pass_gen():
        cx.wait("tensor", "w", w_all)
        cx.wait("vector", "g", g_z)
        for wcol0, out_dram, bias_pp, track, mbuf in (
                (0, QTd, BQP if l1 else None, True, MQS),
                (8192, KTd, BKP if l1 else None, True, MKS)):
            for w2 in range(4):
                cx.wait("tensor", "tr", trw[w2])
                for dt in range(NCH):
                    it = w2 * 8 + dt
                    par = it % 2
                    pa, pb = bank(2 * par), bank(2 * par + 1)
                    cx.wait("tensor", "v", evac_guard[par])
                    for ci in range(NCH):
                        lhsT = WB[:, wcol0 + 1024 * ci + 128 * dt:
                                  wcol0 + 1024 * ci + 128 * dt + 128]
                        rhs_a = A1[:, ci * 4096 + 1024 * w2:
                                   ci * 4096 + 1024 * w2 + 512]
                        rhs_b = A1[:, ci * 4096 + 1024 * w2 + 512:
                                   ci * 4096 + 1024 * (w2 + 1)]
                        mm2(cx, lhsT, rhs_a, rhs_b, pa, pb,
                            start=(ci == 0), stop=(ci == NCH - 1),
                            tinc=(ci == NCH - 1))
                    t_done = cx.val("t")
                    sl = it % 8
                    stg = AR2[:, 1024 * sl:1024 * sl + 1024]
                    cx.wait("vector", "t", t_done)
                    cx.wait("vector", "pq1", max(0, cx.val("pq1") - 7 * 16))
                    if bias_pp is not None:
                        cx.op("vector", lambda e, stg=stg, dt=dt, pa=pa:
                              e.tensor_scalar(stg[:, 0:512], pa,
                                              bias_pp[:, dt:dt + 1], None,
                                              ALU.add))
                        cx.op("vector", lambda e, stg=stg, dt=dt, pb=pb:
                              e.tensor_scalar(stg[:, 512:1024], pb,
                                              bias_pp[:, dt:dt + 1], None,
                                              ALU.add), inc=[("v", 1)])
                    else:
                        cx.op("vector", lambda e, stg=stg, pa=pa: e.tensor_copy(
                            stg[:, 0:512], pa))
                        cx.op("vector", lambda e, stg=stg, pb=pb: e.tensor_copy(
                            stg[:, 512:1024], pb), inc=[("v", 1)])
                    v_evac = cx.val("v")
                    evac_guard[par] = v_evac
                    if track:
                        cx.wait("vector", "v", v_evac)
                        cx.op("vector", lambda e, stg=stg: e.reduce_max(
                            env["STATS"][:, 7:8], stg, axis=AX.X,
                            apply_absolute_value=True), inc=[("v", 1)])
                        v_r = cx.val("v")
                        cx.wait("vector", "v", v_r)
                        cx.op("vector", lambda e, dt=dt, mbuf=mbuf: e.tensor_tensor(
                            mbuf[:, dt:dt + 1], mbuf[:, dt:dt + 1],
                            env["STATS"][:, 7:8], ALU.max), inc=[("v", 1)])
                    cx.wait("gpsimd", "v", v_evac)
                    cx.dma("gpsimd", lambda e, stg=stg, dt=dt, w2=w2,
                           out_dram=out_dram: e.dma_start(
                        out=out_dram[128 * dt:128 * (dt + 1),
                                     1024 * w2:1024 * (w2 + 1)],
                        in_=stg), "pq1")
                    yield

    pgen = proj_pass_gen()
    tiles_seen = [0]

    def phase_a_tile(i):
        tiles_seen[0] = i
        if INTERLEAVE_PROJ and i >= 11:
            next(pgen, None)

    emit_ln(cx, env, rsrc, 0, NT, window_cb=phase_a_window,
            tile_cb=phase_a_tile)
    for _ in pgen:
        pass
    if trunc == "proj":
        return

    if l1 and not NO_BCAST:
        cx.dma("gpsimd", lambda e: e.dma_start(out=BROW[:, :], in_=w["bvr"]), "w")
        w_bv = cx.val("w")
        cx.wait("tensor", "w", w_bv)
        bcast_row(cx, env, BROW, BBC2, banks=(4, 5))
        t_after_bv = cx.val("t")
        cx.wait("gpsimd", "t", t_after_bv)
        cx.dma("gpsimd", lambda e: e.dma_start(out=BROW[:, :], in_=w["bkr"]), "w")
        w_bk = cx.val("w")
        cx.wait("tensor", "w", w_bk)
        bcast_row(cx, env, BROW, BBC, banks=(4, 5))

    evac_v = {0: evac_guard[0], 1: evac_guard[1]}
    kst_guard = 0
    l1g = l1 and not NO_GRAM
    if l1g:
        cx.wait("vector", "pq1", cx.val("pq1"))
    for j in range(NT):
        par = j % 2
        pv = bankpair(par)
        pkv = bankpair(2) if l1g else None
        cx.wait("tensor", "v", evac_v[par])
        for ci in range(NCH):
            lhsT = A1[:, ci * 4096 + 128 * j:ci * 4096 + 128 * (j + 1)]
            cx.op("tensor", lambda e, lhsT=lhsT, pv=pv, ci=ci: e.matmul(
                pv[:, 0:512], lhsT, WB[:, 16384 + 1024 * ci:16384 + 1024 * ci + 512],
                start=(ci == 0), stop=(ci == NCH - 1)))
            cx.op("tensor", lambda e, lhsT=lhsT, pv=pv, ci=ci: e.matmul(
                pv[:, 512:1024], lhsT,
                WB[:, 16384 + 1024 * ci + 512:16384 + 1024 * (ci + 1)],
                start=(ci == 0), stop=(ci == NCH - 1)),
                inc=[("t", 1)] if (not l1g and ci == NCH - 1) else None)
            if l1g:
                cx.op("tensor", lambda e, lhsT=lhsT, pkv=pkv, ci=ci: e.matmul(
                    pkv[:, 0:512], lhsT, WB[:, 8192 + 1024 * ci:8192 + 1024 * ci + 512],
                    start=(ci == 0), stop=(ci == NCH - 1)))
                cx.op("tensor", lambda e, lhsT=lhsT, pkv=pkv, ci=ci: e.matmul(
                    pkv[:, 512:1024], lhsT,
                    WB[:, 8192 + 1024 * ci + 512:8192 + 1024 * (ci + 1)],
                    start=(ci == 0), stop=(ci == NCH - 1)),
                    inc=[("t", 1)] if ci == NCH - 1 else None)
        t_v = cx.val("t")
        sl = j % 4
        ssem = "sg%d" % sl
        stg = STG[sl]
        cx.wait("vector", "t", t_v)
        cx.wait("vector", ssem, cx.val(ssem))
        stg3 = stg.rearrange("p (h d) -> p h d", h=H)
        pv3 = pv.rearrange("p (h d) -> p h d", h=H)
        if l1 and not NO_BCAST:
            cx.op("vector", lambda e, stg3=stg3, pv3=pv3: e.tensor_tensor(
                stg3[:, :, 0:64], pv3,
                BBC2[:, :].rearrange("p (h d) -> p h d", h=H), ALU.add),
                inc=[("v", 1)])
        else:
            cx.op("vector", lambda e, stg3=stg3, pv3=pv3: e.tensor_copy(
                stg3[:, :, 0:64], pv3), inc=[("v", 1)])
        v_evac = cx.val("v")
        cx.wait("vector", "v", v_evac)
        cx.op("vector", lambda e, stg3=stg3: e.memset(stg3[:, :, 64:65], 1.0),
              inc=[("v", 1)])
        v_ones = cx.val("v")
        evac_v[par] = v_ones
        if l1g and not NO_GRAMMM:
            kst = env["AR2"][:, 0:1024]
            if not NO_KSTTT:
                cx.wait("vector", "t", kst_guard)
                cx.op("vector", lambda e, kst=kst, pkv=pkv: e.tensor_tensor(
                    kst, pkv, BBC[:, :], ALU.add), inc=[("v", 1)])
                v_kst = cx.val("v")
                cx.wait("tensor", "v", v_kst)
            if not NO_GRMM:
                for h in range(H):
                    lh = kst[:, 64 * h:64 * (h + 1)]
                    st = (j == 0 and (h == 0 or h == 8)) if not GRAM_SS else True
                    sp = False if not GRAM_SS else True
                    cx.op("tensor", lambda e, lh=lh, st=st, sp=sp, h=h: e.matmul(
                        PS[0:64, 3072 + 64 * h:3072 + 64 * (h + 1)], lh, lh,
                        start=st, stop=sp),
                        inc=[("t", 1)] if h == H - 1 else None)
                kst_guard = cx.val("t")
        cx.wait("sync", "v", v_ones)
        cx.dma("sync", lambda e, stg=stg, j=j: e.dma_start(
            out=VSd[128 * j:128 * (j + 1), :], in_=stg[:, :]), ssem)

    cx.wait("gpsimd", "t", cx.val("t"))
    srco = w["wo"].rearrange("(ci p) o -> p ci o", p=128)
    dsto = WB[:, 12352:20544].rearrange("p (ci o) -> p ci o", ci=8)
    cx.dma("gpsimd", lambda e: e.dma_start(out=dsto, in_=srco), "w")
    w_wo = cx.val("w")

    if trunc == "vpass":
        return
    if l1:
        v_xb = emit_newton(cx, env)
        cx.wait("vector", "v", v_xb)
        cx.op("vector", lambda e: e.reduce_max(
            env["STATS"][:, 15:16], env["XST"][:, :], axis=AX.X,
            apply_absolute_value=True), inc=[("v", 1)])
        v_gm = cx.val("v")
        MROW3 = env["DPREV"][0:1, 0:128]
        cx.wait("gpsimd", "v", v_gm)
        cx.dma("gpsimd", lambda e: e.dma_start(
            out=MQd[2048:2176].rearrange("(a c) -> a c", c=1),
            in_=env["STATS"][:, 15:16]), "mq")
        vq3 = cx.val("mq")
        cx.wait("gpsimd", "mq", vq3)
        cx.dma("gpsimd", lambda e, MROW3=MROW3: e.dma_start(
            out=MROW3, in_=MQd[2048:2176].rearrange("(a c) -> a c", a=1)), "mq")
        mg_done = cx.val("mq")
        cx.wait("vector", "mq", mg_done)
        cx.op("vector", lambda e, MROW3=MROW3: e.reduce_max(
            MAXR[0:1, 67:68], MROW3, axis=AX.X), inc=[("v", 1)])
    if trunc == "newton":
        return

    MROW = env["BBC2"][0:1, :]
    MROW2 = env["BBC2"][64:65, :]

    def fold_max(mbuf, row, col):
        cx.wait("gpsimd", "v", cx.val("v"))
        cx.dma("gpsimd", lambda e, mbuf=mbuf, col=col: e.dma_start(
            out=MQd[1024 * col:1024 * (col + 1)], in_=mbuf[:, :]), "mq")
        vq = cx.val("mq")
        cx.wait("gpsimd", "mq", vq)
        cx.dma("gpsimd", lambda e, row=row, col=col: e.dma_start(
            out=row, in_=MQd[1024 * col:1024 * (col + 1)]
            .rearrange("(a c) -> a c", a=1)), "mq")
        return cx.val("mq")

    fold_max(MQS, MROW, 0)
    mq_done = fold_max(MKS, MROW2, 1)
    cx.wait("vector", "mq", mq_done)
    cx.op("vector", lambda e: e.reduce_max(
        MAXR[0:1, 64:65], MROW, axis=AX.X))
    cx.op("vector", lambda e: e.reduce_max(
        MAXR[0:1, 65:66], MROW2, axis=AX.X), inc=[("v", 1)])
    v_hm = cx.val("v")
    cx.wait("vector", "v", v_hm)
    cx.op("vector", lambda e: e.tensor_tensor(
        MAXR[0:1, 66:67], MAXR[0:1, 64:65], MAXR[0:1, 65:66], ALU.mult),
        inc=[("v", 1)])
    v_b1 = cx.val("v")
    cx.wait("vector", "v", v_b1)
    if l1:
        cx.op("vector", lambda e: e.tensor_tensor(
            MAXR[0:1, 66:67], MAXR[0:1, 66:67], MAXR[0:1, 67:68], ALU.mult),
            inc=[("v", 1)])
        cx.wait("vector", "v", cx.val("v"))
    cx.op("vector", lambda e: e.tensor_scalar(
        MAXR[0:1, 66:67], MAXR[0:1, 66:67],
        (64.0 * 64.0 if l1 else 64.0) * sm_scale, 80.0,
        ALU.mult, ALU.min), inc=[("v", 1)])
    v_b2 = cx.val("v")
    cx.wait("vector", "v", v_b2)
    cx.op("vector", lambda e: e.tensor_scalar_mul(
        MAXR[0:1, 66:67], MAXR[0:1, 66:67], -1.0), inc=[("v", 1)])
    v_b3 = cx.val("v")
    cx.wait("tensor", "v", v_b3)
    cx.op("tensor", lambda e: e.matmul(PS[:, 0:1], ONES1[:, :], MAXR[0:1, 66:67],
                                       start=True, stop=True), inc=[("t", 1)])
    t_nb = cx.val("t")
    cx.wait("vector", "t", t_nb)
    cx.op("vector", lambda e: e.tensor_copy(NEGB[:, :], PS[:, 0:1]), inc=[("v", 1)])
    v_nb = cx.val("v")
    cx.wait("tensor", "v", v_nb)
    qp_mile = emit_qproj(cx, env, v_xb) if l1 else None

    qsrc = QPTd if l1 else QTd
    dn_base = cx.val("dn")
    pair_done = {}
    s_war = {0: 0, 1: 0}
    eb_war = {0: 0, 1: 0}
    o_ev = {0: 0, 1: 0}
    for p in range(8):
        qb = WB[:, 0:4096]
        kb = WB[:, 4096:8192]
        vb = WB[:, 8192:8192 + 4160]
        tp_guard = cx.val("t") if p == 0 else pair_done[p - 1]
        cx.wait("sync", "pq1", cx.val("pq1"))
        if qp_mile is not None:
            for snm, sval in qp_mile[p]:
                cx.wait("sync", snm, sval)
        cx.wait("sync", "t", tp_guard)
        q_lds = []
        for c, sem in ((0, "pq0"), (1, "pk1")):
            cx.dma("sync", lambda e, qb=qb, p=p, c=c: e.dma_start(
                out=qb[:, 2048 * c:2048 * (c + 1)],
                in_=qsrc[128 * p:128 * (p + 1), 2048 * c:2048 * (c + 1)]), sem)
            q_lds.append((sem, cx.val(sem)))
        cx.wait("scalar", "pq1", cx.val("pq1"))
        cx.wait("scalar", "t", tp_guard)
        k_lds = []
        for c, sem in ((0, "pk0"), (1, "rx0")):
            cx.dma("scalar", lambda e, kb=kb, p=p, c=c: e.dma_start(
                out=kb[:, 2048 * c:2048 * (c + 1)],
                in_=KTd[128 * p:128 * (p + 1), 2048 * c:2048 * (c + 1)]), sem)
            k_lds.append((sem, cx.val(sem)))
        wait_all_sg(cx, "gpsimd")
        cx.wait("gpsimd", "t", tp_guard)
        v_lds = {}
        for g in range(4):
            sem = "pv0" if g < 2 else "rx1"
            cx.dma("gpsimd", lambda e, vb=vb, p=p, g=g: e.dma_start(
                out=vb.rearrange("q (j d) -> q j d", d=130)[:, 8 * g:8 * (g + 1), :],
                in_=VSd[:, 65 * 2 * p:65 * 2 * p + 130]
                .rearrange("(j q) d -> q j d", q=128)[:, 8 * g:8 * (g + 1), :]),
                sem)
            v_lds[g] = (sem, cx.val(sem))
        q_seen, k_seen, v_seen = set(), set(), set()

        gates = {}

        def opair(wi):
            ob = 4 if wi % 2 == 0 else 6
            return (PS[0:65, ob * 512:(ob + 1) * 512],
                    PS[0:65, (ob + 1) * 512:(ob + 2) * 512])

        def emit_S(wi, j, jm, qb=qb, kb=kb):
            par = j % 2
            s01 = PS[:, 1024 * par:1024 * (par + 1)]
            cx.wait("tensor", "a", s_war[par])
            qc = wi // 4
            if qc not in q_seen:
                q_seen.add(qc)
                cx.wait("tensor", *q_lds[qc])
            kc = j // 16
            if kc not in k_seen:
                k_seen.add(kc)
                cx.wait("tensor", *k_lds[kc])
            cx.op("tensor", lambda e, kb=kb, qb=qb, s01=s01, j=j, wi=wi:
                  e.matmul(s01[:, 0:512], kb[0:64, 128 * j:128 * (j + 1)],
                           qb[0:64, SW * wi:SW * (wi + 1)],
                           start=True, stop=True))
            cx.op("tensor", lambda e, kb=kb, qb=qb, s01=s01, j=j, wi=wi:
                  e.matmul(s01[:, 512:1024], kb[64:128, 128 * j:128 * (j + 1)],
                           qb[64:128, SW * wi:SW * (wi + 1)],
                           start=True, stop=True), inc=[("t", 1)])
            t_s = cx.val("t")
            cx.wait("scalar", "t", t_s)
            cx.wait("scalar", "t", eb_war[par])
            cx.op("scalar", lambda e, s01=s01, par=par: e.activation(
                EB[par][:, :], s01, AF.Exp, bias=NEGB[:, 0:1],
                scale=sm_scale), inc=[("a", 1)])
            a_exp = cx.val("a")
            s_war[par] = a_exp
            masked = (not l1) and (j >= 4 * wi)
            if masked:
                phi = j - 4 * wi
                cx.wait("vector", "a", a_exp)
                cx.op("vector", lambda e, par=par, phi=phi: e.tensor_tensor(
                    EB[par][:, 0:512], EB[par][:, 0:512],
                    MASKS[:, SW * phi:SW * (phi + 1)], ALU.mult))
                cx.op("vector", lambda e, par=par, phi=phi: e.tensor_tensor(
                    EB[par][:, 512:1024], EB[par][:, 512:1024],
                    MASKS[:, SW * phi:SW * (phi + 1)], ALU.mult),
                    inc=[("v", 1)])
                gates[(wi, j)] = ("v", cx.val("v"))
            else:
                gates[(wi, j)] = ("a", a_exp)

        def emit_AV(wi, j, jm, vb=vb):
            par = j % 2
            o0, o1 = opair(wi)
            g_eng, g_val = gates[(wi, j)]
            cx.wait("tensor", g_eng, g_val)
            if j == 0:
                cx.wait("tensor", "v", o_ev[wi % 2])
            vg = j // 16
            if vg not in v_seen:
                v_seen.add(vg)
                cx.wait("tensor", *v_lds[2 * vg + 1])
            cx.op("tensor", lambda e, vb=vb, o0=o0, j=j, jm=jm, par=par:
                  e.matmul(o0, vb[:, 130 * j:130 * j + 65],
                           EB[par][:, 0:512],
                           start=(j == 0), stop=(j == jm - 1)))
            cx.op("tensor", lambda e, vb=vb, o1=o1, j=j, jm=jm, par=par:
                  e.matmul(o1, vb[:, 130 * j + 65:130 * j + 130],
                           EB[par][:, 512:1024],
                           start=(j == 0), stop=(j == jm - 1)),
                  inc=[("t", 1)])
            eb_war[par] = cx.val("t")
            if j == jm - 1:
                emit_evac(wi)

        def emit_evac(wi):
            o0, o1 = opair(wi)
            t_av = cx.val("t")
            idx = p * NWIN + wi
            sl0, sl1 = 0, 1
            cx.wait("vector", "t", t_av)
            cx.op("vector", lambda e, p=p, wi=wi, o0=o0: e.tensor_copy(
                A1[0:64, 4096 * p + SW * wi:4096 * p + SW * (wi + 1)], o0[0:64, :]))
            cx.op("vector", lambda e, p=p, wi=wi, o1=o1: e.tensor_copy(
                A1[64:128, 4096 * p + SW * wi:4096 * p + SW * (wi + 1)], o1[0:64, :]),
                inc=[("v", 1)])
            cx.wait("vector", "dn", dn_base + 16 * (2 * idx))
            cx.op("vector", lambda e, sl0=sl0, o0=o0: e.tensor_copy(
                DSTG[sl0][:, :], o0[64:65, :]))
            cx.op("vector", lambda e, sl1=sl1, o1=o1: e.tensor_copy(
                DSTG[sl1][:, :], o1[64:65, :]), inc=[("v", 1)])
            v_den = cx.val("v")
            cx.wait("gpsimd", "v", v_den)
            cx.dma("gpsimd", lambda e, sl0=sl0, p=p, wi=wi: e.dma_start(
                out=DEN[(2 * p) * 8 + wi:(2 * p) * 8 + wi + 1, :],
                in_=DSTG[sl0][:, :]), "dn")
            cx.dma("gpsimd", lambda e, sl1=sl1, p=p, wi=wi: e.dma_start(
                out=DEN[(2 * p + 1) * 8 + wi:(2 * p + 1) * 8 + wi + 1, :],
                in_=DSTG[sl1][:, :]), "dn")
            o_ev[wi % 2] = v_den

        seq = []
        for wi in range(NWIN):
            jm = NT if l1 else 4 * wi + 4
            for j in range(jm):
                seq.append((wi, j, jm))
        emit_S(*seq[0])
        for idx in range(1, len(seq)):
            emit_S(*seq[idx])
            emit_AV(*seq[idx - 1])
        emit_AV(*seq[-1])
        pair_done[p] = cx.val("t")
        if trunc == "c1" and p == 0:
            return

    APLd = env["APLd"]
    DPREV, BBC2v = env["DPREV"], env["BBC2"]
    aslot = [DPREV[0:1, 0:1024], BBC2[0:1, 0:1024], env["BBC"][0:1, 0:1024]]
    cx.wait("vector", "dn", cx.val("dn"))
    cx.op("vector", lambda e: e.reciprocal(RECIP[:, :], DEN[:, :]), inc=[("v", 1)])
    v_rec = cx.val("v")
    cx.wait("gpsimd", "v", v_rec)
    cx.dma("gpsimd", lambda e: e.dma_start(out=APLd, in_=RECIP[:, :]), "mq")
    r_full = cx.val("mq")
    cx.wait("gpsimd", "mq", r_full)
    apply_done = {}
    slot_free = {0: 0, 1: 0, 2: 0}
    for g in range(64):
        h = g // 4
        k = g % 4
        s = g % 3
        rsem = "rx%d" % (g % 2)
        cx.wait("gpsimd", "t", slot_free[s])
        cx.dma("gpsimd", lambda e, h=h, k=k, s=s: e.dma_start(
            out=aslot[s],
            in_=APLd[8 * h + 2 * k:8 * h + 2 * k + 2, :]
            .rearrange("(a r) c -> a (r c)", a=1)), rsem)
        r_mv = cx.val(rsem)
        bp = 2 + 2 * s
        bb = PS[0:64, bp * 512:(bp + 2) * 512]
        if g >= 3:
            cx.wait("tensor", "v", apply_done[g - 3])
        cx.wait("tensor", rsem, r_mv)
        cx.op("tensor", lambda e, bb=bb, s=s: e.matmul(
            bb[:, 0:512], ONES1[:, 0:64], aslot[s][:, 0:512],
            start=True, stop=True))
        cx.op("tensor", lambda e, bb=bb, s=s: e.matmul(
            bb[:, 512:1024], ONES1[:, 0:64], aslot[s][:, 512:1024],
            start=True, stop=True), inc=[("t", 1)])
        slot_free[s] = t_bc = cx.val("t")
        pi = h // 2
        half = h % 2
        ao = A1[64 * half:64 * half + 64,
                4096 * pi + 1024 * k:4096 * pi + 1024 * (k + 1)]
        cx.wait("vector", "t", t_bc)
        cx.op("vector", lambda e, ao=ao, bb=bb: e.tensor_tensor(
            ao, ao, bb, ALU.mult), inc=[("v", 1)])
        apply_done[g] = cx.val("v")
    v_apply = cx.val("v")
    cx.wb_free_lo = cx.val("t")
    if trunc == "apply":
        return

    cx.wait("gpsimd", "t", cx.val("t"))
    cx.dma("gpsimd", lambda e: e.dma_start(out=BROW[:, :], in_=w["bo"]), "w")
    w_bo = cx.val("w")
    cx.wait("tensor", "v", v_apply)
    cx.wait("tensor", "w", w_bo)
    cx.wait("tensor", "w", w_wo)
    bcast_row(cx, env, BROW, BBC, banks=(6, 7))
    v_bo = cx.val("v")
    evac_d = {0: 0, 1: 0}
    for i in range(NT):
        par = i % 2
        pd = bankpair(par)
        sl = i % 3
        rsem = "rr%d" % sl
        rt = RQ[sl]
        geng, gval = cx.rr_guard[sl]
        cx.wait("sync", geng, gval)
        cx.wait("sync", "str", cx.str_mile.get(i, 0))
        cx.dma("sync", lambda e, rt=rt, i=i: e.dma_start(
            out=rt[:, :], in_=rsrc[128 * i:128 * (i + 1), :]), rsem)
        r_ld = cx.val(rsem)
        cx.wait("tensor", "v", evac_d[par])
        for ci in range(NCH):
            lhsT = A1[:, 4096 * ci + 128 * i:4096 * ci + 128 * (i + 1)]
            mm2(cx, lhsT, WB[:, 12352 + 1024 * ci:12352 + 1024 * ci + 512],
                WB[:, 12352 + 1024 * ci + 512:12352 + 1024 * (ci + 1)],
                pd[:, 0:512], pd[:, 512:1024],
                start=(ci == 0), stop=(ci == NCH - 1), tinc=(ci == NCH - 1))
        t_wo = cx.val("t")
        cx.wait("vector", "t", t_wo)
        cx.wait("vector", rsem, r_ld)
        cx.wait("vector", "v", v_bo)
        cx.op("vector", lambda e, rt=rt, pd=pd: e.tensor_tensor(
            rt[:, :], rt[:, :], pd[:, :], ALU.add), inc=[("v", 1)])
        v_add1 = cx.val("v")
        evac_d[par] = v_add1
        cx.wait("vector", "v", v_add1)
        cx.op("vector", lambda e, rt=rt: e.tensor_tensor(
            rt[:, :], rt[:, :], BBC[:, :], ALU.add), inc=[("v", 1)])
        v_add2 = cx.val("v")
        cx.wait("scalar", "v", v_add2)
        cx.dma("scalar", lambda e, rt=rt, i=i: e.dma_start(
            out=rdst[128 * i:128 * (i + 1), :], in_=rt[:, :]), "str")
        cx.rr_guard[sl] = ("str", cx.val("str"))
        cx.str_mile[i] = cx.val("str")


def emit_newton(cx, env):
    PS, GRAM, ABD, XBD, XST, XSTB, YSB, I2S, TWOI8 = (
        env["PS"], env["GRAM"], env["ABD"], env["XBD"], env["XST"], env["XSTB"],
        env["YSB"], env["I2S"], env["TWOI8"])
    RS16, CIN, CBC, ONES64, ONES1, MQS, STG = (
        env["RS16"], env["CIN"], env["CBC"], env["ONES64"], env["ONES1"],
        env["MQS"], env["STG"])
    QTd, QPTd, WB = env["QTd"], env["QPTd"], env["WB"]

    t_gram = cx.val("t")
    cx.wait("vector", "t", t_gram)
    cx.op("vector", lambda e: e.tensor_copy(GRAM[:, 0:512], PS[0:64, 3072:3584]))
    cx.op("vector", lambda e: e.tensor_copy(GRAM[:, 512:1024], PS[0:64, 3584:4096]),
          inc=[("v", 1)])
    v_gram = cx.val("v")
    cx.wait("vector", "v", v_gram)
    cx.op("vector", lambda e: e.reduce_sum(
        RS16[:, :], GRAM[:, :].rearrange("p (h d) -> p h d", h=H), axis=AX.X,
        apply_absolute_value=True), inc=[("v", 1)])
    v_rs = cx.val("v")
    cx.wait("tensor", "v", v_rs)
    cx.op("tensor", lambda e: e.matmul(PS[0:1, 0:H], ONES64[:, :], RS16[:, :],
                                       start=True, stop=True), inc=[("t", 1)])
    t_rs = cx.val("t")
    cx.wait("vector", "t", t_rs)
    cx.op("vector", lambda e: e.tensor_copy(CIN[:, :], PS[0:1, 0:H]), inc=[("v", 1)])
    v_ci = cx.val("v")
    cx.wait("vector", "v", v_ci)
    cx.op("vector", lambda e: e.reciprocal(CIN[:, :], CIN[:, :]), inc=[("v", 1)])
    v_cr = cx.val("v")
    cx.wait("tensor", "v", v_cr)
    cx.op("tensor", lambda e: e.matmul(PS[:, 0:H], ONES1[:, :], CIN[:, :],
                                       start=True, stop=True), inc=[("t", 1)])
    t_cb = cx.val("t")
    cx.wait("vector", "t", t_cb)
    cx.op("vector", lambda e: e.tensor_copy(CBC[:, :], PS[:, 0:H]), inc=[("v", 1)])
    v_cb = cx.val("v")
    cx.wait("tensor", "v", v_cb)

    for p in range(8):
        cx.op("gpsimd", lambda e, p=p: e.memset(ABD[p][:, :], 0.0))
        cx.op("gpsimd", lambda e, p=p: e.memset(XBD[p][:, :], 0.0))
    cx.op("gpsimd", lambda e: e.nop(), inc=[("g", 1)])
    g_zero = cx.val("g")
    cx.wait("gpsimd", "v", v_gram)
    for p in range(8):
        cx.dma("gpsimd", lambda e, p=p: e.dma_start(
            out=ABD[p][0:64, 0:64], in_=GRAM[:, 64 * (2 * p):64 * (2 * p) + 64]), "mq")
        cx.dma("gpsimd", lambda e, p=p: e.dma_start(
            out=ABD[p][64:128, 64:128],
            in_=GRAM[:, 64 * (2 * p + 1):64 * (2 * p + 1) + 64]), "mq")
    abd_done = cx.val("mq")

    cx.wait("vector", "g", g_zero)
    cx.wait("vector", "v", v_cb)
    for p in range(8):
        cx.op("vector", lambda e, p=p: e.tensor_scalar(
            XST[0:64, 64 * p:64 * p + 64], I2S[0:64, :],
            CBC[0:64, 2 * p:2 * p + 1], None, ALU.mult))
        cx.op("vector", lambda e, p=p: e.tensor_scalar(
            XST[64:128, 64 * p:64 * p + 64], I2S[64:128, :],
            CBC[64:128, 2 * p + 1:2 * p + 2], None, ALU.mult))
    cx.op("vector", lambda e: e.nop(), inc=[("v", 1)])
    prev_v = cx.val("v")

    for it in range(NEWTON_ITERS):
        par = it % 2
        t1b = PS[:, par * 512:par * 512 + 512]
        x2b = PS[:, (2 + par) * 512:(2 + par) * 512 + 512]
        cx.wait("vector", "v", prev_v)
        for p in range(8):
            cx.op("vector", lambda e, p=p: e.tensor_copy(
                XBD[p][0:64, 0:64], XST[0:64, 64 * p:64 * p + 64]))
            cx.op("vector", lambda e, p=p: e.tensor_copy(
                XBD[p][64:128, 64:128], XST[64:128, 64 * p:64 * p + 64]))
        cx.op("vector", lambda e: e.nop(), inc=[("v", 1)])
        v_bd = cx.val("v")
        cx.wait("tensor", "v", v_bd)
        if it == 0:
            cx.wait("tensor", "mq", abd_done)
        for p in range(8):
            cx.op("tensor", lambda e, p=p, t1b=t1b: e.matmul(
                t1b[:, 64 * p:64 * p + 64], ABD[p][:, :],
                XST[:, 64 * p:64 * p + 64], start=True, stop=True),
                inc=[("t", 1)] if p == 7 else None)
        t_t1 = cx.val("t")
        cx.wait("vector", "t", t_t1)
        cx.op("vector", lambda e, t1b=t1b: e.scalar_tensor_tensor(
            YSB[:, :], t1b, -1.0, TWOI8[:, :], ALU.mult, ALU.add), inc=[("v", 1)])
        v_y = cx.val("v")
        cx.wait("tensor", "v", v_y)
        for p in range(8):
            cx.op("tensor", lambda e, p=p, x2b=x2b: e.matmul(
                x2b[:, 64 * p:64 * p + 64], XBD[p][:, :],
                YSB[:, 64 * p:64 * p + 64], start=True, stop=True),
                inc=[("t", 1)] if p == 7 else None)
        t_x2 = cx.val("t")
        cx.wait("vector", "t", t_x2)
        cx.op("vector", lambda e, x2b=x2b: e.tensor_copy(XST[:, :], x2b),
              inc=[("v", 1)])
        prev_v = cx.val("v")
    cx.wait("vector", "v", prev_v)
    cx.op("vector", lambda e: e.tensor_copy(XSTB[:, :], XST[:, :]), inc=[("v", 1)])
    v_xb = cx.val("v")
    return v_xb


def emit_qproj(cx, env, v_xb):
    PS, XSTB, STG = env["PS"], env["XSTB"], env["STG"]
    QTd, QPTd, WB = env["QTd"], env["QPTd"], env["WB"]
    qp_ev = {0: 0, 1: 0}
    qp_t = {}
    qp_mile = {}
    t0 = cx.val("t")
    cx.wait("sync", "pq1", cx.val("pq1"))
    for p in range(8):
        qbuf = WB[:, 4096 * (p % 2):4096 * (p % 2) + 4096]
        cx.wait("sync", "t", qp_t.get(p - 2, t0))
        cx.dma("sync", lambda e, p=p, qbuf=qbuf: e.dma_start(
            out=qbuf, in_=QTd[128 * p:128 * (p + 1), :]), "pq0")
        q_ld = cx.val("pq0")
        cx.wait("tensor", "pq0", q_ld)
        if p == 0:
            cx.wait("tensor", "v", v_xb)
        for h2 in range(2):
            for sc in range(8):
                par = sc % 2
                ob = PS[0:64, (4 + par) * 512:(4 + par) * 512 + 512]
                cx.wait("tensor", "v", qp_ev[par])
                lh = XSTB[64 * h2:64 * h2 + 64, 64 * p:64 * p + 64]
                cx.op("tensor", lambda e, lh=lh, ob=ob, h2=h2, sc=sc: e.matmul(
                    ob, lh, qbuf[64 * h2:64 * h2 + 64, 512 * sc:512 * (sc + 1)],
                    start=True, stop=True), inc=[("t", 1)])
                t_qp = cx.val("t")
                sl = (p * 16 + h2 * 8 + sc) % 4
                ssem = "sg%d" % sl
                stg = STG[sl][0:64, 0:512]
                cx.wait("vector", "t", t_qp)
                cx.wait("vector", ssem, cx.val(ssem))
                cx.op("vector", lambda e, stg=stg, ob=ob: e.tensor_copy(stg, ob),
                      inc=[("v", 1)])
                qp_ev[par] = cx.val("v")
                v_ev = cx.val("v")
                cx.wait("sync", "v", v_ev)
                cx.dma("sync", lambda e, p=p, h2=h2, sc=sc, stg=stg: e.dma_start(
                    out=QPTd[128 * p + 64 * h2:128 * p + 64 * h2 + 64,
                             512 * sc:512 * (sc + 1)], in_=stg), ssem)
        qp_t[p] = cx.val("t")
        qp_mile[p] = [("sg%d" % sl, cx.val("sg%d" % sl)) for sl in range(4)]
    return qp_mile


def emit_ffn(cx, env, w, rsrc, rdst):
    A1, WB, PS, STG, AR2, RQ = (env["A1"], env["WB"], env["PS"], env["STG"],
                                env["AR2"], env["RQ"])
    XT, MUR, MUK, DPREV, XPC, XPREV = (env["XT"], env["MUR"], env["MUK"],
                                       env["DPREV"], env["XPC"], env["XPREV"])

    def bankpair(i):
        return PS[:, 1024 * i:1024 * (i + 1)]

    t_entry = cx.val("t")

    for name, col0 in (("wr", 0), ("wkf", 8192), ("wvf", 16384)):
        cx.wait("gpsimd", "t", cx.wb_free_lo if name == "wr" else t_entry)
        src = w[name].rearrange("(ci p) o -> p ci o", p=128)
        dst = WB[:, col0:col0 + 8192].rearrange("p (ci o) -> p ci o", ci=8)
        cx.dma("gpsimd", lambda e, dst=dst, src=src: e.dma_start(out=dst, in_=src), "w")
    cx.dma("gpsimd", lambda e: e.dma_start(out=MUR[:, :], in_=w["mur"]), "w")
    cx.dma("gpsimd", lambda e: e.dma_start(out=MUK[:, :], in_=w["muk"]), "w")
    w_all = cx.val("w")
    cx.wait("tensor", "w", w_all)
    cx.wait("vector", "w", w_all)

    bank_free = {0: ("v", 0), 1: ("v", 0)}
    state = {"v_mix": 0, "v_mult_last": 0, "dprev_free": ("v", 0),
             "xprev_free": 0}

    def prep_quarter(q):
        t0, t1 = q * 8, (q + 1) * 8
        if q == 0:
            cx.op("gpsimd", lambda e: e.memset(XPC[:, :], 0.0), inc=[("g", 1)])
            cx.wait("vector", "g", cx.val("g"))
            cx.op("vector", lambda e: e.nop(), inc=[("v", 1)])
        else:
            cx.wait("vector", "v", state["v_mix"])
            for ci in range(NCH):
                cx.op("vector", lambda e, ci=ci: e.tensor_copy(
                    XPC[:, ci:ci + 1], A1[:, 1024 * ci + 1023:1024 * ci + 1024]))
            cx.op("vector", lambda e: e.nop(), inc=[("v", 1)])
        v_bnd = cx.val("v")
        emit_ln(cx, env, rsrc, t0, t1, act_sum=True)
        wait_all_sg(cx, "scalar")
        if q == 0:
            cx.wait("scalar", "t", t_entry)
        cx.wait("scalar", "v", v_bnd)
        for ci in range(NCH):
            dst = A1[:, 1024 * ci:1024 * ci + 1024]
            src = XT[t0 * 128:t1 * 128, 128 * ci:128 * ci + 128]
            cx.dma("scalar", lambda e, dst=dst, src=src: e.dma_start_transpose(
                dst, src), "tr")
        return cx.val("tr")

    def emit_mix_half(ci, mu, dst_off, tr_q, t_gate):
        xt = A1[:, 1024 * ci:1024 * ci + 1024]
        xd = A1[:, dst_off + 1024 * ci:dst_off + 1024 * ci + 1024]
        if ci == 0:
            cx.wait("vector", "tr", tr_q)
            cx.wait("vector", "t", t_gate)
            cx.wait("vector", "w", w_all)
        cx.wait("vector", "v", state["xprev_free"])
        cx.op("vector", lambda e, xt=xt: e.tensor_tensor(
            XPREV[:, 1:1024], xt[:, 1:1024], xt[:, 0:1023], ALU.subtract))
        cx.op("vector", lambda e, xt=xt, ci=ci: e.tensor_tensor(
            XPREV[:, 0:1], xt[:, 0:1], XPC[:, ci:ci + 1], ALU.subtract),
            inc=[("v", 1)])
        v_d = cx.val("v")
        cx.wait("vector", "v", v_d)
        cx.op("vector", lambda e, xd=xd, xt=xt, ci=ci, mu=mu: e.scalar_tensor_tensor(
            xd[:, 1:1024], XPREV[:, 1:1024], mu[:, ci:ci + 1], xt[:, 0:1023],
            ALU.mult, ALU.add))
        cx.op("vector", lambda e, xd=xd, ci=ci, mu=mu: e.scalar_tensor_tensor(
            xd[:, 0:1], XPREV[:, 0:1], mu[:, ci:ci + 1], XPC[:, ci:ci + 1],
            ALU.mult, ALU.add), inc=[("v", 1)])
        state["v_mix"] = cx.val("v")
        state["xprev_free"] = state["v_mix"]

    def r_pass(q, v_xr):
        for jl in range(8):
            par = jl % 2
            pp = bankpair(par)
            cx.wait("tensor", *bank_free[par])
            if jl == 0:
                cx.wait("tensor", "v", v_xr)
            for ci in range(NCH):
                lhsT = A1[:, 8192 + 1024 * ci + 128 * jl:
                          8192 + 1024 * ci + 128 * (jl + 1)]
                mm2(cx, lhsT, WB[:, 1024 * ci:1024 * ci + 512],
                    WB[:, 1024 * ci + 512:1024 * (ci + 1)],
                    pp[:, 0:512], pp[:, 512:1024],
                    start=(ci == 0), stop=(ci == NCH - 1), tinc=(ci == NCH - 1))
            t_r = cx.val("t")
            cx.wait("scalar", "t", t_r)
            if jl == 0:
                cx.wait("scalar", "v", state["v_mult_last"])
            cx.op("scalar", lambda e, jl=jl, pp=pp: e.activation(
                AR2[:, 1024 * jl:1024 * (jl + 1)], pp, AF.Sigmoid), inc=[("a", 1)])
            bank_free[par] = ("a", cx.val("a"))

    def k_pass(q, v_xk, mix_next):
        for dt in range(8):
            par = dt % 2
            pp = bankpair(par)
            cx.wait("tensor", *bank_free[par])
            if dt == 0:
                cx.wait("tensor", "v", v_xk)
            for ci in range(NCH):
                lhsT = WB[:, 8192 + 1024 * ci + 128 * dt:
                          8192 + 1024 * ci + 128 * dt + 128]
                rhs_a = A1[:, 16384 + 1024 * ci:16384 + 1024 * ci + 512]
                rhs_b = A1[:, 16384 + 1024 * ci + 512:16384 + 1024 * (ci + 1)]
                mm2(cx, lhsT, rhs_a, rhs_b, pp[:, 0:512], pp[:, 512:1024],
                    start=(ci == 0), stop=(ci == NCH - 1), tinc=(ci == NCH - 1))
            t_k = cx.val("t")
            cx.wait("scalar", "t", t_k)
            cx.wait("scalar", *state["dprev_free"])
            cx.op("scalar", lambda e, pp=pp: e.activation(
                DPREV[:, :], pp[:, 0:1024], AF.Relu), inc=[("a", 1)])
            a_rl = cx.val("a")
            bank_free[par] = ("a", a_rl)
            cx.wait("scalar", "a", a_rl)
            cx.op("scalar", lambda e, dt=dt: e.activation(
                A1[:, 24576 + 1024 * dt:24576 + 1024 * (dt + 1)], DPREV[:, :],
                AF.Square), inc=[("a", 1)])
            state["dprev_free"] = ("a", cx.val("a"))
            if mix_next is not None:
                emit_mix_half(dt, MUR, 8192, mix_next[0], mix_next[1])
        return cx.val("t"), cx.val("a")

    def v_pass(q, t_k_all, a_k_all, mix_next):
        t0 = q * 8
        for jl in range(8):
            i = t0 + jl
            par = jl % 2
            pp = bankpair(par)
            sl = i % 3
            rsem = "rr%d" % sl
            rt = RQ[sl]
            geng, gval = cx.rr_guard[sl]
            cx.wait("sync", geng, gval)
            cx.wait("sync", "str", cx.str_mile.get(i, 0))
            cx.dma("sync", lambda e, rt=rt, i=i: e.dma_start(
                out=rt[:, :], in_=rsrc[128 * i:128 * (i + 1), :]), rsem)
            r_ld = cx.val(rsem)
            cx.wait("tensor", *bank_free[par])
            if jl == 0:
                cx.wait("tensor", "a", a_k_all)
            for dt in range(NCH):
                lhsT = A1[:, 24576 + 1024 * dt + 128 * jl:
                          24576 + 1024 * dt + 128 * (jl + 1)]
                mm2(cx, lhsT, WB[:, 16384 + 1024 * dt:16384 + 1024 * dt + 512],
                    WB[:, 16384 + 1024 * dt + 512:16384 + 1024 * (dt + 1)],
                    pp[:, 0:512], pp[:, 512:1024],
                    start=(dt == 0), stop=(dt == NCH - 1), tinc=(dt == NCH - 1))
            t_v = cx.val("t")
            cx.wait("vector", "t", t_v)
            cx.wait("vector", *state["dprev_free"])
            cx.op("vector", lambda e, pp=pp, jl=jl: e.tensor_tensor(
                DPREV[:, :], pp[:, 0:1024], AR2[:, 1024 * jl:1024 * (jl + 1)],
                ALU.mult), inc=[("v", 1)])
            v_m = cx.val("v")
            bank_free[par] = ("v", v_m)
            cx.wait("vector", rsem, r_ld)
            cx.wait("vector", "v", v_m)
            cx.op("vector", lambda e, rt=rt: e.tensor_tensor(
                rt[:, :], rt[:, :], DPREV[:, :], ALU.add), inc=[("v", 1)])
            v_add = cx.val("v")
            state["dprev_free"] = ("v", v_add)
            state["v_mult_last"] = v_add
            cx.wait("scalar", "v", v_add)
            cx.dma("scalar", lambda e, rt=rt, i=i: e.dma_start(
                out=rdst[128 * i:128 * (i + 1), :], in_=rt[:, :]), "str")
            cx.rr_guard[sl] = ("str", cx.val("str"))
            cx.str_mile[i] = cx.val("str")
            if mix_next is not None:
                emit_mix_half(jl, MUK, 16384, mix_next[0], mix_next[1])

    tr0 = prep_quarter(0)
    for ci in range(NCH):
        emit_mix_half(ci, MUR, 8192, tr0, t_entry)
    v_xr = state["v_mix"]
    for ci in range(NCH):
        emit_mix_half(ci, MUK, 16384, tr0, t_entry)
    v_xk = state["v_mix"]
    for q in range(NQ):
        r_pass(q, v_xr)
        tr_next = prep_quarter(q + 1) if q < NQ - 1 else None
        t_r_all = cx.val("t")
        t_k_all, a_k_all = k_pass(q, v_xk,
                                  (tr_next, t_r_all) if q < NQ - 1 else None)
        if q == NQ - 1:
            cx.wb_free_lo = t_k_all
        v_xr = state["v_mix"]
        v_pass(q, t_k_all, a_k_all,
               (tr_next, t_k_all) if q < NQ - 1 else None)
        v_xk = state["v_mix"]



def _prep_weights(inputs):
    f = np.ascontiguousarray
    out = {}
    out[("attn0", 0)] = dict(
        wq=f(inputs["L0_attn_Wq"].transpose(1, 0, 2).reshape(C, C)),
        wk=f(inputs["L0_attn_Wk"].transpose(1, 0, 2).reshape(C, C)),
        wv=f(inputs["L0_attn_Wv"].transpose(1, 0, 2).reshape(C, C)),
        wo=f(inputs["L0_attn_Wo"].T),
        bo=f(inputs["L0_attn_bo"].reshape(1, C)),
    )
    out[("attn1", 1)] = dict(
        wq=f(inputs["L1_attn_Wq"].T),
        wk=f(inputs["L1_attn_Wk"].T),
        wv=f(inputs["L1_attn_Wv"].T),
        wo=f(inputs["L1_attn_Wfc"].T),
        bq=f(inputs["L1_attn_bq"].reshape(8, 128).T),
        bk=f(inputs["L1_attn_bk"].reshape(8, 128).T),
        bvr=f(inputs["L1_attn_bv"].reshape(1, C)),
        bkr=f(inputs["L1_attn_bk"].reshape(1, C)),
        bo=f(inputs["L1_attn_bfc"].reshape(1, C)),
    )
    for L in (0, 1):
        p = "L%d_ffn_" % L
        out[("ffn", L)] = dict(
            wr=f(inputs[p + "W_r"].T),
            wkf=f(inputs[p + "W_k"].T),
            wvf=f(inputs[p + "W_v"].T),
            mur=f(inputs[p + "mu_r"].reshape(8, 128).T),
            muk=f(inputs[p + "mu_k"].reshape(8, 128).T),
        )
    return out


FUSED = True


def kernel(**inputs):
    x = np.ascontiguousarray(np.asarray(inputs["L0_attn_x"], dtype=np.float32))
    wmaps = _prep_weights({k: np.asarray(v) for k, v in inputs.items()})
    B = x.shape[0]
    ranges = [(0, 4)] if FUSED else [(i, i + 1) for i in range(4)]
    cur = [np.ascontiguousarray(x[b]) for b in range(B)]
    for (lo, hi) in ranges:
        nc = build_program(lo, hi)
        in_maps = []
        for b in range(B):
            m = {"xin": cur[b]}
            for si, (kind, L) in enumerate(SUBS[lo:hi]):
                for k, arr in wmaps[(kind, L)].items():
                    m["s%d_%s" % (si, k)] = arr
            in_maps.append(m)
        res = run_bass_kernel_spmd(nc, in_maps, core_ids=list(range(8)))
        cur = [np.asarray(res.results[b]["xout"]) for b in range(B)]
    return np.stack(cur, axis=0)


# version 30
# speedup vs baseline: 1.1839x; 1.1803x over previous
import numpy as np
import concourse.bass as bass
import concourse.mybir as mybir
from concourse.bass_utils import run_bass_kernel_spmd

F32 = mybir.dt.float32
BF16 = mybir.dt.bfloat16
AF = mybir.ActivationFunctionType
ALU = mybir.AluOpType
AX = mybir.AxisListType

T = 4096
C = 1024
H = 16
D = 64
NT = T // 128
NCH = C // 128
SW = 512
NWIN = T // SW
NQ = 4
NEWTON_ITERS = 24
EPS = 1e-5

SUBS = [("attn0", 0), ("ffn", 0), ("attn1", 1), ("ffn", 1)]
TRUNC_LAST = "full"
INTERLEAVE_PROJ = False
NO_GRAM = False
NO_GRAMMM = False
NO_KSTTT = False
NO_GRMM = False
GRAM_SS = False
NO_BCAST = False

SEM_NAMES = (
    ["t", "v", "a", "g"]
    + ["rr%d" % i for i in range(4)]
    + ["sg%d" % i for i in range(4)]
    + ["pq0", "pq1", "pk0", "pk1", "pv0"]
    + ["tr", "w", "str", "dn", "mq"]
    + ["rx%d" % i for i in range(4)]
)


class Ctx:

    def __init__(self, me, eng, sems):
        self.me = me
        self.eng = eng
        self.sems = sems
        self.cnt = {k: 0 for k in sems}
        self.rr_guard = {i: ("v", 0) for i in range(3)}
        self.ln_state = {"var": {0: 0, 1: 0}, "s2": {0: 0, 1: 0},
                         "norm": {0: 0, 1: 0},
                         "ln": {0: 0, 1: 0}}
        self.str_mile = {}
        self.wb_free_lo = 0

    def inc(self, s, n=1):
        self.cnt[s] += n
        return self.cnt[s]

    def val(self, s):
        return self.cnt[s]

    def op(self, engine, builder, inc=None):
        ins = builder(self.eng) if self.me == engine else None
        if inc:
            for s, n in inc:
                self.inc(s, n)
                if ins is not None:
                    ins.then_inc(self.sems[s], n)
        return ins

    def wait(self, engine, s, v):
        if v <= 0:
            return
        if self.me == engine:
            self.eng.wait_ge(self.sems[s], v)

    def dma(self, engine, builder, sem):
        return self.op(engine, builder, inc=[(sem, 16)])


def build_program(sub_lo, sub_hi):
    nc = bass.Bass("TRN2", target_bir_lowering=False)

    xin = nc.dram_tensor("xin", [T, C], F32, kind="ExternalInput").ap()
    xout = nc.dram_tensor("xout", [T, C], F32, kind="ExternalOutput").ap()

    subs = SUBS[sub_lo:sub_hi]
    win = {}
    for si, (kind, L) in enumerate(subs):
        pre = "s%d_" % si
        d = {}
        if kind == "ffn":
            for nm, shp in (("wr", [C, C]), ("wkf", [C, C]), ("wvf", [C, C]),
                            ("mur", [128, 8]), ("muk", [128, 8])):
                d[nm] = nc.dram_tensor(pre + nm, shp, F32, kind="ExternalInput").ap()
        else:
            for nm in ("wq", "wk", "wv", "wo"):
                d[nm] = nc.dram_tensor(pre + nm, [C, C], F32, kind="ExternalInput").ap()
            d["bo"] = nc.dram_tensor(pre + "bo", [1, C], F32, kind="ExternalInput").ap()
            if kind == "attn1":
                for nm, shp in (("bq", [128, 8]), ("bk", [128, 8]),
                                ("bvr", [1, C]), ("bkr", [1, C])):
                    d[nm] = nc.dram_tensor(pre + nm, shp, F32, kind="ExternalInput").ap()
        win[si] = d

    XT = nc.dram_tensor("XTs", [T, C], BF16, kind="Internal").ap()
    QTd = nc.dram_tensor("QTs", [C, T], BF16, kind="Internal").ap()
    KTd = nc.dram_tensor("KTs", [C, T], BF16, kind="Internal").ap()
    QPTd = nc.dram_tensor("QPTs", [C, T], BF16, kind="Internal").ap()
    VSd = nc.dram_tensor("VSs", [T, H * 65], BF16, kind="Internal").ap()
    MQd = nc.dram_tensor("MQs", [3 * C], F32, kind="Internal").ap()
    MKd = nc.dram_tensor("MKs", [C], F32, kind="Internal").ap()
    RAd = nc.dram_tensor("RAs", [T, C], F32, kind="Internal").ap()
    RBd = nc.dram_tensor("RBs", [T, C], F32, kind="Internal").ap()
    APLd = nc.dram_tensor("APLs", [128, SW], F32, kind="Internal").ap()

    n_sub = len(subs)
    rsrc, rdst = [None] * n_sub, [None] * n_sub
    cur = xin
    for si in range(n_sub):
        rsrc[si] = cur
        rdst[si] = xout if si == n_sub - 1 else (RAd if si % 2 == 0 else RBd)
        cur = rdst[si]

    A1 = nc.alloc_sbuf_tensor("A1", [128, 32768], BF16).ap()
    WB = nc.alloc_sbuf_tensor("WB", [128, 3 * 8192], BF16).ap()
    RQ = [nc.alloc_sbuf_tensor("RQ%d" % i, [128, 1024], F32).ap() for i in range(3)]
    STG = [nc.alloc_sbuf_tensor("STG%d" % i, [128, 1040], BF16).ap() for i in range(4)]
    AR2 = nc.alloc_sbuf_tensor("AR2", [128, 8192], BF16).ap()
    STATS = nc.alloc_sbuf_tensor("STATS", [128, 16], F32).ap()
    DPREV = nc.alloc_sbuf_tensor("DPREV", [128, 1024], F32).ap()
    XPC = nc.alloc_sbuf_tensor("XPC", [128, 8], BF16).ap()
    XPREV = nc.alloc_sbuf_tensor("XPREV", [128, 1024], BF16).ap()
    MASKS = nc.alloc_sbuf_tensor("MASKS", [128, 4 * SW], BF16).ap()
    EB = [nc.alloc_sbuf_tensor("EB%d" % i, [128, 1024], BF16).ap() for i in range(2)]
    DEN = nc.alloc_sbuf_tensor("DEN", [128, SW], F32).ap()
    RECIP = nc.alloc_sbuf_tensor("RECIP", [128, SW], F32).ap()
    DSTG = [nc.alloc_sbuf_tensor("DSTG%d" % i, [1, SW], F32).ap() for i in range(2)]
    RSTG = [nc.alloc_sbuf_tensor("RSTG%d" % i, [1, SW], F32).ap() for i in range(2)]
    ONES1 = nc.alloc_sbuf_tensor("ONES1", [1, 128], F32).ap()
    ONES64 = nc.alloc_sbuf_tensor("ONES64", [64, 1], F32).ap()
    BROW = nc.alloc_sbuf_tensor("BROW", [1, C], F32).ap()
    BBC = nc.alloc_sbuf_tensor("BBC", [128, C], F32).ap()
    BBC2 = nc.alloc_sbuf_tensor("BBC2", [128, C], F32).ap()
    MQS = nc.alloc_sbuf_tensor("MQS", [128, 8], F32).ap()
    MQSQ = nc.alloc_sbuf_tensor("MQSQ", [64, 16], F32).ap()
    MKS = nc.alloc_sbuf_tensor("MKS", [128, 8], F32).ap()
    MAXR = nc.alloc_sbuf_tensor("MAXR", [1, 80], F32).ap()
    NEGB = nc.alloc_sbuf_tensor("NEGB", [128, 1], F32).ap()
    MUR = nc.alloc_sbuf_tensor("MUR", [128, 8], F32).ap()
    MUK = nc.alloc_sbuf_tensor("MUK", [128, 8], F32).ap()
    BQP = nc.alloc_sbuf_tensor("BQP", [128, 8], F32).ap()
    BKP = nc.alloc_sbuf_tensor("BKP", [128, 8], F32).ap()
    GRAM = nc.alloc_sbuf_tensor("GRAM", [64, H * 64], F32).ap()
    ABD = [nc.alloc_sbuf_tensor("ABD%d" % p, [128, 128], F32).ap() for p in range(8)]
    XBD = [nc.alloc_sbuf_tensor("XBD%d" % p, [128, 128], F32).ap() for p in range(8)]
    XST = nc.alloc_sbuf_tensor("XST", [128, 512], F32).ap()
    XSTB = nc.alloc_sbuf_tensor("XSTB", [128, 512], BF16).ap()
    YSB = nc.alloc_sbuf_tensor("YSB", [128, 512], F32).ap()
    I2S = nc.alloc_sbuf_tensor("I2S", [128, 64], F32).ap()
    TWOI8 = nc.alloc_sbuf_tensor("TWOI8", [128, 512], F32).ap()
    RS16 = nc.alloc_sbuf_tensor("RS16", [64, H], F32).ap()
    CIN = nc.alloc_sbuf_tensor("CIN", [1, H], F32).ap()
    CBC = nc.alloc_sbuf_tensor("CBC", [128, H], F32).ap()
    EPSB = nc.alloc_sbuf_tensor("EPSB", [128, 1], F32).ap()

    PS = nc.alloc_psum_tensor("PS", [128, 4096], F32).ap()

    sems = {s: nc.alloc_semaphore(s) for s in SEM_NAMES}

    env = dict(locals())

    def emit(me, eng):
        cx = Ctx(me, eng, sems)
        emit_prologue(cx, env)
        for si, (kind, L) in enumerate(subs):
            if kind == "ffn":
                emit_ffn(cx, env, win[si], rsrc[si], rdst[si])
            else:
                emit_attn(cx, env, kind, win[si], rsrc[si], rdst[si],
                          trunc=(TRUNC_LAST if si == len(subs) - 1 else "full"))
        cx.wait("sync", "str", cx.val("str"))

    with nc.Block() as block:
        block.tensor(lambda eng: emit("tensor", eng))
        block.vector(lambda eng: emit("vector", eng))
        block.scalar(lambda eng: emit("scalar", eng))
        block.gpsimd(lambda eng: emit("gpsimd", eng))
        block.sync(lambda eng: emit("sync", eng))

    return nc



def emit_prologue(cx, env):
    MASKS, ONES1, ONES64, I2S, TWOI8 = (env["MASKS"], env["ONES1"],
                                        env["ONES64"], env["I2S"], env["TWOI8"])
    for phi in range(4):
        m = MASKS[:, phi * SW:(phi + 1) * SW]
        cx.op("gpsimd", lambda e, m=m: e.memset(m, 1.0))
        cx.op("gpsimd", lambda e, m=m, phi=phi: e.affine_select(
            out=m, in_=m, compare_op=ALU.is_ge, fill=0.0,
            base=-128 * phi, pattern=[[1, SW]], channel_multiplier=-1))
    cx.op("gpsimd", lambda e: e.memset(ONES1[:, :], 1.0))
    cx.op("gpsimd", lambda e: e.memset(env["EPSB"][:, :], EPS))
    cx.op("gpsimd", lambda e: e.memset(ONES64[:, :], 1.0))
    cx.op("gpsimd", lambda e: e.memset(I2S[:, :], 1.0))
    cx.op("gpsimd", lambda e: e.affine_select(
        out=I2S[0:64, :], in_=I2S[0:64, :], compare_op=ALU.is_equal,
        fill=0.0, base=0, pattern=[[-1, 64]], channel_multiplier=1))
    cx.op("gpsimd", lambda e: e.affine_select(
        out=I2S[64:128, :], in_=I2S[64:128, :], compare_op=ALU.is_equal,
        fill=0.0, base=64, pattern=[[-1, 64]], channel_multiplier=1))
    for p in range(8):
        cx.op("gpsimd", lambda e, p=p: e.tensor_scalar_mul(
            TWOI8[:, 64 * p:64 * p + 64], I2S[:, :], 2.0))
    cx.op("gpsimd", lambda e: e.nop(), inc=[("g", 1)])


def emit_ln(cx, env, rsrc, t0, t1, window_cb=None, tile_cb=None,
            act_sum=False):
    RQ, STG, STATS, XT = (env["RQ"], env["STG"], env["STATS"], env["XT"])
    st = cx.ln_state
    vnorm = {}

    def emit_tr(i):
        ssl = i % 4
        stg = STG[ssl][:, 0:1024]
        cx.wait("scalar", "a", vnorm[i])
        cx.dma("scalar", lambda e, stg=stg, i=i: e.dma_start(
            out=XT[128 * i:128 * (i + 1), :], in_=stg), "sg%d" % ssl)
        if window_cb is not None and i % 8 == 7:
            window_cb(i)

    var_of = {}

    def stage_a(i):
        sl = i % 3
        rsem = "rr%d" % sl
        ssl = i % 4
        ssem = "sg%d" % ssl
        rt = RQ[sl]
        stg = STG[ssl][:, 0:1024]
        lane = (i - t0) % 2
        b = 8 * lane
        geng, gval = cx.rr_guard[sl]
        cx.wait("sync", geng, gval)
        cx.wait("sync", "str", cx.str_mile.get(i, 0))
        cx.dma("sync", lambda e, rt=rt, i=i: e.dma_start(
            out=rt[:, :], in_=rsrc[128 * i:128 * (i + 1), :]), rsem)
        v_ld = cx.val(rsem)
        cx.wait("scalar", "v", st["var"][lane])
        cx.wait("scalar", rsem, v_ld)
        cx.wait("scalar", ssem, cx.val(ssem))
        cx.op("scalar", lambda e, rt=rt, stg=stg, b=b: e.activation(
            stg, rt[:, :], AF.Square, accum_out=STATS[:, b + 1:b + 2]),
            inc=[("a", 1)])
        a_sq = cx.val("a")
        if act_sum:
            cx.wait("scalar", "v", st["s2"][lane])
            cx.op("scalar", lambda e, rt=rt, stg=stg, b=b: e.activation(
                stg, rt[:, :], AF.Identity, accum_out=STATS[:, b:b + 1]),
                inc=[("a", 1)])
            a_sum = cx.val("a")
            cx.wait("vector", "a", a_sum)
        else:
            cx.wait("vector", "v", st["s2"][lane])
            cx.wait("vector", rsem, v_ld)
            cx.op("vector", lambda e, rt=rt, b=b: e.reduce_sum(
                STATS[:, b:b + 1], rt[:, :], axis=AX.X), inc=[("v", 1)])
            v_red = cx.val("v")
            cx.wait("vector", "v", v_red)
        cx.wait("vector", "a", st["norm"][lane])
        cx.op("vector", lambda e, b=b: e.tensor_scalar_mul(
            STATS[:, b + 2:b + 3], STATS[:, b:b + 1], -1.0 / C))
        cx.op("vector", lambda e, b=b: e.tensor_tensor(
            STATS[:, b + 3:b + 4], STATS[:, b:b + 1], STATS[:, b:b + 1],
            ALU.mult), inc=[("v", 1)])
        st["s2"][lane] = v_s2 = cx.val("v")
        cx.wait("vector", "a", a_sq)
        cx.wait("vector", "a", st["ln"][lane])
        cx.wait("vector", "v", v_s2)
        cx.op("vector", lambda e, b=b: e.scalar_tensor_tensor(
            STATS[:, b + 4:b + 5], STATS[:, b + 3:b + 4], -1.0 / C,
            STATS[:, b + 1:b + 2], ALU.mult, ALU.add), inc=[("v", 1)])
        st["var"][lane] = var_of[i] = cx.val("v")

    def stage_b(i):
        sl = i % 3
        ssl = i % 4
        ssem = "sg%d" % ssl
        rt = RQ[sl]
        stg = STG[ssl][:, 0:1024]
        lane = (i - t0) % 2
        b = 8 * lane
        cx.wait("scalar", "v", var_of[i])
        cx.op("scalar", lambda e, b=b: e.activation(
            STATS[:, b + 5:b + 6], STATS[:, b + 4:b + 5], AF.Ln,
            bias=env["EPSB"][:, :], scale=1.0 / C), inc=[("a", 1)])
        a_ln = cx.val("a")
        st["ln"][lane] = a_ln
        cx.wait("scalar", "a", a_ln)
        cx.wait("scalar", "a", st["norm"][lane])
        cx.op("scalar", lambda e, b=b: e.activation(
            STATS[:, b + 6:b + 7], STATS[:, b + 5:b + 6], AF.Exp, scale=-0.5),
            inc=[("a", 1)])
        cx.op("scalar", lambda e, b=b: e.activation(
            STATS[:, b + 7:b + 8], STATS[:, b + 2:b + 3], AF.Identity,
            scale=STATS[:, b + 6:b + 7]), inc=[("a", 1)])
        a_bias = cx.val("a")
        cx.wait("scalar", "a", a_bias)
        cx.wait("scalar", ssem, cx.val(ssem))
        cx.op("scalar", lambda e, rt=rt, stg=stg, b=b: e.activation(
            stg, rt[:, :], AF.Identity, bias=STATS[:, b + 7:b + 8],
            scale=STATS[:, b + 6:b + 7]), inc=[("a", 1)])
        a_norm = cx.val("a")
        st["norm"][lane] = a_norm
        cx.rr_guard[sl] = ("a", a_norm)
        vnorm[i] = a_norm

    stage_a(t0)
    for i in range(t0 + 1, t1):
        stage_a(i)
        stage_b(i - 1)
        if i - 3 >= t0:
            emit_tr(i - 3)
        if tile_cb is not None:
            tile_cb(i)
    stage_b(t1 - 1)
    for i in (t1 - 3, t1 - 2, t1 - 1):
        if i >= t0:
            emit_tr(i)


def wait_all_sg(cx, engine):
    for sl in range(4):
        cx.wait(engine, "sg%d" % sl, cx.val("sg%d" % sl))
    cx.wait(engine, "pq1", cx.val("pq1"))


def mm2(cx, lhsT, rhs_a, rhs_b, out_a, out_b, start, stop, tinc=False):
    cx.op("tensor", lambda e: e.matmul(out_a, lhsT, rhs_a, start=start, stop=stop))
    cx.op("tensor", lambda e: e.matmul(out_b, lhsT, rhs_b, start=start, stop=stop),
          inc=[("t", 1)] if tinc else None)


def bcast_row(cx, env, row_ap, dst, banks=(4, 5)):
    PS, ONES1 = env["PS"], env["ONES1"]
    b0 = PS[:, 512 * banks[0]:512 * banks[0] + 512]
    b1 = PS[:, 512 * banks[1]:512 * banks[1] + 512]
    cx.op("tensor", lambda e: e.matmul(b0, ONES1[:, :], row_ap[:, 0:512],
                                       start=True, stop=True))
    cx.op("tensor", lambda e: e.matmul(b1, ONES1[:, :], row_ap[:, 512:1024],
                                       start=True, stop=True), inc=[("t", 1)])
    t_b = cx.val("t")
    cx.wait("vector", "t", t_b)
    cx.op("vector", lambda e: e.tensor_copy(dst[:, 0:512], b0))
    cx.op("vector", lambda e: e.tensor_copy(dst[:, 512:1024], b1), inc=[("v", 1)])
    v_b = cx.val("v")
    cx.wait("tensor", "v", v_b)
    return v_b


def emit_attn(cx, env, kind, w, rsrc, rdst, trunc="full"):
    A1, WB, PS, STG, RQ, AR2 = (env["A1"], env["WB"], env["PS"],
                                env["STG"], env["RQ"], env["AR2"])
    QTd, KTd, QPTd, VSd, XT = (env["QTd"], env["KTd"], env["QPTd"], env["VSd"],
                               env["XT"])
    MQS, MKS, MAXR, NEGB = env["MQS"], env["MKS"], env["MAXR"], env["NEGB"]
    MQd = env["MQd"]
    BROW, BBC, BBC2, BQP, BKP = (env["BROW"], env["BBC"], env["BBC2"],
                                 env["BQP"], env["BKP"])
    ONES1 = env["ONES1"]
    MASKS, EB, DEN, RECIP, DSTG, RSTG = (env["MASKS"], env["EB"], env["DEN"],
                                         env["RECIP"], env["DSTG"], env["RSTG"])
    l1 = (kind == "attn1")
    sm_scale = (1.0 / np.sqrt(D)) if not l1 else (1.0 / np.sqrt(C))

    def bank(i):
        return PS[:, 512 * i:512 * (i + 1)]

    def bankpair(i):
        return PS[:, 1024 * i:1024 * (i + 1)]

    t_entry = cx.val("t")

    for name, col0, buf in (("wq", 0, WB), ("wk", 8192, WB), ("wv", 16384, WB)):
        cx.wait("gpsimd", "t", t_entry if name == "wv" else cx.wb_free_lo)
        src = w[name].rearrange("(ci p) o -> p ci o", p=128)
        dst = buf[:, col0:col0 + 8192].rearrange("p (ci o) -> p ci o", ci=8)
        cx.dma("gpsimd", lambda e, dst=dst, src=src: e.dma_start(out=dst, in_=src), "w")
    if l1:
        cx.dma("gpsimd", lambda e: e.dma_start(out=BQP[:, :], in_=w["bq"]), "w")
        cx.dma("gpsimd", lambda e: e.dma_start(out=BKP[:, :], in_=w["bk"]), "w")
    w_all = cx.val("w")

    cx.op("gpsimd", lambda e: e.memset(MQS[:, :], 0.0))
    cx.op("gpsimd", lambda e: e.memset(MKS[:, :], 0.0), inc=[("g", 1)])
    g_z = cx.val("g")
    trw = {}

    def phase_a_window(j):
        w2 = j // 8
        if w2 == 0:
            cx.wait("scalar", "t", t_entry)
        wait_all_sg(cx, "scalar")
        for ci in range(NCH):
            dst = A1[:, ci * 4096 + 1024 * w2:ci * 4096 + 1024 * (w2 + 1)]
            src = XT[1024 * w2:1024 * (w2 + 1), 128 * ci:128 * ci + 128]
            cx.dma("scalar", lambda e, dst=dst, src=src: e.dma_start_transpose(
                dst, src), "tr")
        trw[w2] = cx.val("tr")

    evac_guard = {0: 0, 1: 0}

    def proj_pass_gen():
        cx.wait("tensor", "w", w_all)
        cx.wait("vector", "g", g_z)
        for wcol0, out_dram, bias_pp, track, mbuf in (
                (0, QTd, BQP if l1 else None, True, MQS),
                (8192, KTd, BKP if l1 else None, True, MKS)):
            for w2 in range(4):
                cx.wait("tensor", "tr", trw[w2])
                for dt in range(NCH):
                    it = w2 * 8 + dt
                    par = it % 2
                    pa, pb = bank(2 * par), bank(2 * par + 1)
                    cx.wait("tensor", "v", evac_guard[par])
                    for ci in range(NCH):
                        lhsT = WB[:, wcol0 + 1024 * ci + 128 * dt:
                                  wcol0 + 1024 * ci + 128 * dt + 128]
                        rhs_a = A1[:, ci * 4096 + 1024 * w2:
                                   ci * 4096 + 1024 * w2 + 512]
                        rhs_b = A1[:, ci * 4096 + 1024 * w2 + 512:
                                   ci * 4096 + 1024 * (w2 + 1)]
                        mm2(cx, lhsT, rhs_a, rhs_b, pa, pb,
                            start=(ci == 0), stop=(ci == NCH - 1),
                            tinc=(ci == NCH - 1))
                    t_done = cx.val("t")
                    sl = it % 8
                    stg = AR2[:, 1024 * sl:1024 * sl + 1024]
                    cx.wait("vector", "t", t_done)
                    cx.wait("vector", "pq1", max(0, cx.val("pq1") - 7 * 16))
                    if bias_pp is not None:
                        cx.op("vector", lambda e, stg=stg, dt=dt, pa=pa:
                              e.tensor_scalar(stg[:, 0:512], pa,
                                              bias_pp[:, dt:dt + 1], None,
                                              ALU.add))
                        cx.op("vector", lambda e, stg=stg, dt=dt, pb=pb:
                              e.tensor_scalar(stg[:, 512:1024], pb,
                                              bias_pp[:, dt:dt + 1], None,
                                              ALU.add), inc=[("v", 1)])
                    else:
                        cx.op("vector", lambda e, stg=stg, pa=pa: e.tensor_copy(
                            stg[:, 0:512], pa))
                        cx.op("vector", lambda e, stg=stg, pb=pb: e.tensor_copy(
                            stg[:, 512:1024], pb), inc=[("v", 1)])
                    v_evac = cx.val("v")
                    evac_guard[par] = v_evac
                    if track:
                        cx.wait("vector", "v", v_evac)
                        cx.op("vector", lambda e, stg=stg: e.reduce_max(
                            env["STATS"][:, 7:8], stg, axis=AX.X,
                            apply_absolute_value=True), inc=[("v", 1)])
                        v_r = cx.val("v")
                        cx.wait("vector", "v", v_r)
                        cx.op("vector", lambda e, dt=dt, mbuf=mbuf: e.tensor_tensor(
                            mbuf[:, dt:dt + 1], mbuf[:, dt:dt + 1],
                            env["STATS"][:, 7:8], ALU.max), inc=[("v", 1)])
                    cx.wait("gpsimd", "v", v_evac)
                    cx.dma("gpsimd", lambda e, stg=stg, dt=dt, w2=w2,
                           out_dram=out_dram: e.dma_start(
                        out=out_dram[128 * dt:128 * (dt + 1),
                                     1024 * w2:1024 * (w2 + 1)],
                        in_=stg), "pq1")
                    yield

    pgen = proj_pass_gen()
    tiles_seen = [0]

    def phase_a_tile(i):
        tiles_seen[0] = i
        if INTERLEAVE_PROJ and i >= 11:
            next(pgen, None)

    emit_ln(cx, env, rsrc, 0, NT, window_cb=phase_a_window,
            tile_cb=phase_a_tile)
    for _ in pgen:
        pass
    if trunc == "proj":
        return

    if l1 and not NO_BCAST:
        cx.dma("gpsimd", lambda e: e.dma_start(out=BROW[:, :], in_=w["bvr"]), "w")
        w_bv = cx.val("w")
        cx.wait("tensor", "w", w_bv)
        bcast_row(cx, env, BROW, BBC2, banks=(4, 5))
        t_after_bv = cx.val("t")
        cx.wait("gpsimd", "t", t_after_bv)
        cx.dma("gpsimd", lambda e: e.dma_start(out=BROW[:, :], in_=w["bkr"]), "w")
        w_bk = cx.val("w")
        cx.wait("tensor", "w", w_bk)
        bcast_row(cx, env, BROW, BBC, banks=(4, 5))

    evac_v = {0: evac_guard[0], 1: evac_guard[1]}
    kst_guard = 0
    l1g = l1 and not NO_GRAM
    if l1g:
        cx.wait("vector", "pq1", cx.val("pq1"))
    for j in range(NT):
        par = j % 2
        pv = bankpair(par)
        pkv = bankpair(2) if l1g else None
        cx.wait("tensor", "v", evac_v[par])
        for ci in range(NCH):
            lhsT = A1[:, ci * 4096 + 128 * j:ci * 4096 + 128 * (j + 1)]
            cx.op("tensor", lambda e, lhsT=lhsT, pv=pv, ci=ci: e.matmul(
                pv[:, 0:512], lhsT, WB[:, 16384 + 1024 * ci:16384 + 1024 * ci + 512],
                start=(ci == 0), stop=(ci == NCH - 1)))
            cx.op("tensor", lambda e, lhsT=lhsT, pv=pv, ci=ci: e.matmul(
                pv[:, 512:1024], lhsT,
                WB[:, 16384 + 1024 * ci + 512:16384 + 1024 * (ci + 1)],
                start=(ci == 0), stop=(ci == NCH - 1)),
                inc=[("t", 1)] if (not l1g and ci == NCH - 1) else None)
            if l1g:
                cx.op("tensor", lambda e, lhsT=lhsT, pkv=pkv, ci=ci: e.matmul(
                    pkv[:, 0:512], lhsT, WB[:, 8192 + 1024 * ci:8192 + 1024 * ci + 512],
                    start=(ci == 0), stop=(ci == NCH - 1)))
                cx.op("tensor", lambda e, lhsT=lhsT, pkv=pkv, ci=ci: e.matmul(
                    pkv[:, 512:1024], lhsT,
                    WB[:, 8192 + 1024 * ci + 512:8192 + 1024 * (ci + 1)],
                    start=(ci == 0), stop=(ci == NCH - 1)),
                    inc=[("t", 1)] if ci == NCH - 1 else None)
        t_v = cx.val("t")
        sl = j % 4
        ssem = "sg%d" % sl
        stg = STG[sl]
        cx.wait("vector", "t", t_v)
        cx.wait("vector", ssem, cx.val(ssem))
        stg3 = stg.rearrange("p (h d) -> p h d", h=H)
        pv3 = pv.rearrange("p (h d) -> p h d", h=H)
        if l1 and not NO_BCAST:
            cx.op("vector", lambda e, stg3=stg3, pv3=pv3: e.tensor_tensor(
                stg3[:, :, 0:64], pv3,
                BBC2[:, :].rearrange("p (h d) -> p h d", h=H), ALU.add),
                inc=[("v", 1)])
        else:
            cx.op("vector", lambda e, stg3=stg3, pv3=pv3: e.tensor_copy(
                stg3[:, :, 0:64], pv3), inc=[("v", 1)])
        v_evac = cx.val("v")
        cx.wait("vector", "v", v_evac)
        cx.op("vector", lambda e, stg3=stg3: e.memset(stg3[:, :, 64:65], 1.0),
              inc=[("v", 1)])
        v_ones = cx.val("v")
        evac_v[par] = v_ones
        if l1g and not NO_GRAMMM:
            kst = env["AR2"][:, 0:1024]
            if not NO_KSTTT:
                cx.wait("vector", "t", kst_guard)
                cx.op("vector", lambda e, kst=kst, pkv=pkv: e.tensor_tensor(
                    kst, pkv, BBC[:, :], ALU.add), inc=[("v", 1)])
                v_kst = cx.val("v")
                cx.wait("tensor", "v", v_kst)
            if not NO_GRMM:
                for h in range(H):
                    lh = kst[:, 64 * h:64 * (h + 1)]
                    st = (j == 0 and (h == 0 or h == 8)) if not GRAM_SS else True
                    sp = False if not GRAM_SS else True
                    cx.op("tensor", lambda e, lh=lh, st=st, sp=sp, h=h: e.matmul(
                        PS[0:64, 3072 + 64 * h:3072 + 64 * (h + 1)], lh, lh,
                        start=st, stop=sp),
                        inc=[("t", 1)] if h == H - 1 else None)
                kst_guard = cx.val("t")
        cx.wait("sync", "v", v_ones)
        cx.dma("sync", lambda e, stg=stg, j=j: e.dma_start(
            out=VSd[128 * j:128 * (j + 1), :], in_=stg[:, :]), ssem)

    cx.wait("gpsimd", "t", cx.val("t"))
    srco = w["wo"].rearrange("(ci p) o -> p ci o", p=128)
    dsto = WB[:, 12352:20544].rearrange("p (ci o) -> p ci o", ci=8)
    cx.dma("gpsimd", lambda e: e.dma_start(out=dsto, in_=srco), "w")
    w_wo = cx.val("w")

    if trunc == "vpass":
        return
    if l1:
        v_xb = emit_newton(cx, env)
        cx.wait("vector", "v", v_xb)
        cx.op("vector", lambda e: e.reduce_max(
            env["STATS"][:, 15:16], env["XST"][:, :], axis=AX.X,
            apply_absolute_value=True), inc=[("v", 1)])
        v_gm = cx.val("v")
        MROW3 = env["DPREV"][0:1, 0:128]
        cx.wait("gpsimd", "v", v_gm)
        cx.dma("gpsimd", lambda e: e.dma_start(
            out=MQd[2048:2176].rearrange("(a c) -> a c", c=1),
            in_=env["STATS"][:, 15:16]), "mq")
        vq3 = cx.val("mq")
        cx.wait("gpsimd", "mq", vq3)
        cx.dma("gpsimd", lambda e, MROW3=MROW3: e.dma_start(
            out=MROW3, in_=MQd[2048:2176].rearrange("(a c) -> a c", a=1)), "mq")
        mg_done = cx.val("mq")
        cx.wait("vector", "mq", mg_done)
        cx.op("vector", lambda e, MROW3=MROW3: e.reduce_max(
            MAXR[0:1, 67:68], MROW3, axis=AX.X), inc=[("v", 1)])
    if trunc == "newton":
        return

    MROW = env["BBC2"][0:1, :]
    MROW2 = env["BBC2"][64:65, :]

    def fold_max(mbuf, row, col):
        cx.wait("gpsimd", "v", cx.val("v"))
        cx.dma("gpsimd", lambda e, mbuf=mbuf, col=col: e.dma_start(
            out=MQd[1024 * col:1024 * (col + 1)], in_=mbuf[:, :]), "mq")
        vq = cx.val("mq")
        cx.wait("gpsimd", "mq", vq)
        cx.dma("gpsimd", lambda e, row=row, col=col: e.dma_start(
            out=row, in_=MQd[1024 * col:1024 * (col + 1)]
            .rearrange("(a c) -> a c", a=1)), "mq")
        return cx.val("mq")

    fold_max(MQS, MROW, 0)
    mq_done = fold_max(MKS, MROW2, 1)
    cx.wait("vector", "mq", mq_done)
    cx.op("vector", lambda e: e.reduce_max(
        MAXR[0:1, 64:65], MROW, axis=AX.X))
    cx.op("vector", lambda e: e.reduce_max(
        MAXR[0:1, 65:66], MROW2, axis=AX.X), inc=[("v", 1)])
    v_hm = cx.val("v")
    cx.wait("vector", "v", v_hm)
    cx.op("vector", lambda e: e.tensor_tensor(
        MAXR[0:1, 66:67], MAXR[0:1, 64:65], MAXR[0:1, 65:66], ALU.mult),
        inc=[("v", 1)])
    v_b1 = cx.val("v")
    cx.wait("vector", "v", v_b1)
    if l1:
        cx.op("vector", lambda e: e.tensor_tensor(
            MAXR[0:1, 66:67], MAXR[0:1, 66:67], MAXR[0:1, 67:68], ALU.mult),
            inc=[("v", 1)])
        cx.wait("vector", "v", cx.val("v"))
    cx.op("vector", lambda e: e.tensor_scalar(
        MAXR[0:1, 66:67], MAXR[0:1, 66:67],
        (64.0 * 64.0 if l1 else 64.0) * sm_scale, 80.0,
        ALU.mult, ALU.min), inc=[("v", 1)])
    v_b2 = cx.val("v")
    cx.wait("vector", "v", v_b2)
    cx.op("vector", lambda e: e.tensor_scalar_mul(
        MAXR[0:1, 66:67], MAXR[0:1, 66:67], -1.0), inc=[("v", 1)])
    v_b3 = cx.val("v")
    cx.wait("tensor", "v", v_b3)
    cx.op("tensor", lambda e: e.matmul(PS[:, 0:1], ONES1[:, :], MAXR[0:1, 66:67],
                                       start=True, stop=True), inc=[("t", 1)])
    t_nb = cx.val("t")
    cx.wait("vector", "t", t_nb)
    cx.op("vector", lambda e: e.tensor_copy(NEGB[:, :], PS[:, 0:1]), inc=[("v", 1)])
    v_nb = cx.val("v")
    cx.wait("tensor", "v", v_nb)
    qp_mile = emit_qproj(cx, env, v_xb) if l1 else None

    qsrc = QPTd if l1 else QTd
    dn_base = cx.val("dn")
    pair_done = {}
    s_war = {0: 0, 1: 0}
    eb_war = {0: 0, 1: 0}
    o_ev = {0: 0, 1: 0}
    for p in range(8):
        qb = WB[:, 0:4096]
        kb = WB[:, 4096:8192]
        vb = WB[:, 8192:8192 + 4160]
        tp_guard = cx.val("t") if p == 0 else pair_done[p - 1]
        cx.wait("sync", "pq1", cx.val("pq1"))
        if qp_mile is not None:
            for snm, sval in qp_mile[p]:
                cx.wait("sync", snm, sval)
        cx.wait("sync", "t", tp_guard)
        q_lds = []
        for c, sem in ((0, "pq0"), (1, "pk1")):
            cx.dma("sync", lambda e, qb=qb, p=p, c=c: e.dma_start(
                out=qb[:, 2048 * c:2048 * (c + 1)],
                in_=qsrc[128 * p:128 * (p + 1), 2048 * c:2048 * (c + 1)]), sem)
            q_lds.append((sem, cx.val(sem)))
        cx.wait("scalar", "pq1", cx.val("pq1"))
        cx.wait("scalar", "t", tp_guard)
        k_lds = []
        for c, sem in ((0, "pk0"), (1, "rx0")):
            cx.dma("scalar", lambda e, kb=kb, p=p, c=c: e.dma_start(
                out=kb[:, 2048 * c:2048 * (c + 1)],
                in_=KTd[128 * p:128 * (p + 1), 2048 * c:2048 * (c + 1)]), sem)
            k_lds.append((sem, cx.val(sem)))
        wait_all_sg(cx, "gpsimd")
        cx.wait("gpsimd", "t", tp_guard)
        v_lds = {}
        for g in range(4):
            sem = "pv0" if g < 2 else "rx1"
            cx.dma("gpsimd", lambda e, vb=vb, p=p, g=g: e.dma_start(
                out=vb.rearrange("q (j d) -> q j d", d=130)[:, 8 * g:8 * (g + 1), :],
                in_=VSd[:, 65 * 2 * p:65 * 2 * p + 130]
                .rearrange("(j q) d -> q j d", q=128)[:, 8 * g:8 * (g + 1), :]),
                sem)
            v_lds[g] = (sem, cx.val(sem))
        q_seen, k_seen, v_seen = set(), set(), set()

        gates = {}

        def opair(wi):
            ob = 4 if wi % 2 == 0 else 6
            return (PS[0:65, ob * 512:(ob + 1) * 512],
                    PS[0:65, (ob + 1) * 512:(ob + 2) * 512])

        def emit_S(wi, j, jm, qb=qb, kb=kb):
            par = j % 2
            s01 = PS[:, 1024 * par:1024 * (par + 1)]
            cx.wait("tensor", "a", s_war[par])
            qc = wi // 4
            if qc not in q_seen:
                q_seen.add(qc)
                cx.wait("tensor", *q_lds[qc])
            kc = j // 16
            if kc not in k_seen:
                k_seen.add(kc)
                cx.wait("tensor", *k_lds[kc])
            cx.op("tensor", lambda e, kb=kb, qb=qb, s01=s01, j=j, wi=wi:
                  e.matmul(s01[:, 0:512], kb[0:64, 128 * j:128 * (j + 1)],
                           qb[0:64, SW * wi:SW * (wi + 1)],
                           start=True, stop=True))
            cx.op("tensor", lambda e, kb=kb, qb=qb, s01=s01, j=j, wi=wi:
                  e.matmul(s01[:, 512:1024], kb[64:128, 128 * j:128 * (j + 1)],
                           qb[64:128, SW * wi:SW * (wi + 1)],
                           start=True, stop=True), inc=[("t", 1)])
            t_s = cx.val("t")
            cx.wait("scalar", "t", t_s)
            cx.wait("scalar", "t", eb_war[par])
            cx.op("scalar", lambda e, s01=s01, par=par: e.activation(
                EB[par][:, :], s01, AF.Exp, bias=NEGB[:, 0:1],
                scale=sm_scale), inc=[("a", 1)])
            a_exp = cx.val("a")
            s_war[par] = a_exp
            masked = (not l1) and (j >= 4 * wi)
            if masked:
                phi = j - 4 * wi
                cx.wait("vector", "a", a_exp)
                cx.op("vector", lambda e, par=par, phi=phi: e.tensor_tensor(
                    EB[par][:, 0:512], EB[par][:, 0:512],
                    MASKS[:, SW * phi:SW * (phi + 1)], ALU.mult))
                cx.op("vector", lambda e, par=par, phi=phi: e.tensor_tensor(
                    EB[par][:, 512:1024], EB[par][:, 512:1024],
                    MASKS[:, SW * phi:SW * (phi + 1)], ALU.mult),
                    inc=[("v", 1)])
                gates[(wi, j)] = ("v", cx.val("v"))
            else:
                gates[(wi, j)] = ("a", a_exp)

        def emit_AV(wi, j, jm, vb=vb):
            par = j % 2
            o0, o1 = opair(wi)
            g_eng, g_val = gates[(wi, j)]
            cx.wait("tensor", g_eng, g_val)
            if j == 0:
                cx.wait("tensor", "v", o_ev[wi % 2])
            vg = j // 16
            if vg not in v_seen:
                v_seen.add(vg)
                cx.wait("tensor", *v_lds[2 * vg + 1])
            cx.op("tensor", lambda e, vb=vb, o0=o0, j=j, jm=jm, par=par:
                  e.matmul(o0, vb[:, 130 * j:130 * j + 65],
                           EB[par][:, 0:512],
                           start=(j == 0), stop=(j == jm - 1)))
            cx.op("tensor", lambda e, vb=vb, o1=o1, j=j, jm=jm, par=par:
                  e.matmul(o1, vb[:, 130 * j + 65:130 * j + 130],
                           EB[par][:, 512:1024],
                           start=(j == 0), stop=(j == jm - 1)),
                  inc=[("t", 1)])
            eb_war[par] = cx.val("t")
            if j == jm - 1:
                emit_evac(wi)

        def emit_evac(wi):
            o0, o1 = opair(wi)
            t_av = cx.val("t")
            idx = p * NWIN + wi
            sl0, sl1 = 0, 1
            cx.wait("vector", "t", t_av)
            cx.op("vector", lambda e, p=p, wi=wi, o0=o0: e.tensor_copy(
                A1[0:64, 4096 * p + SW * wi:4096 * p + SW * (wi + 1)], o0[0:64, :]))
            cx.op("vector", lambda e, p=p, wi=wi, o1=o1: e.tensor_copy(
                A1[64:128, 4096 * p + SW * wi:4096 * p + SW * (wi + 1)], o1[0:64, :]),
                inc=[("v", 1)])
            cx.wait("vector", "dn", dn_base + 16 * (2 * idx))
            cx.op("vector", lambda e, sl0=sl0, o0=o0: e.tensor_copy(
                DSTG[sl0][:, :], o0[64:65, :]))
            cx.op("vector", lambda e, sl1=sl1, o1=o1: e.tensor_copy(
                DSTG[sl1][:, :], o1[64:65, :]), inc=[("v", 1)])
            v_den = cx.val("v")
            cx.wait("gpsimd", "v", v_den)
            cx.dma("gpsimd", lambda e, sl0=sl0, p=p, wi=wi: e.dma_start(
                out=DEN[(2 * p) * 8 + wi:(2 * p) * 8 + wi + 1, :],
                in_=DSTG[sl0][:, :]), "dn")
            cx.dma("gpsimd", lambda e, sl1=sl1, p=p, wi=wi: e.dma_start(
                out=DEN[(2 * p + 1) * 8 + wi:(2 * p + 1) * 8 + wi + 1, :],
                in_=DSTG[sl1][:, :]), "dn")
            o_ev[wi % 2] = v_den

        seq = []
        for wi in range(NWIN):
            jm = NT if l1 else 4 * wi + 4
            for j in range(jm):
                seq.append((wi, j, jm))
        emit_S(*seq[0])
        for idx in range(1, len(seq)):
            emit_S(*seq[idx])
            emit_AV(*seq[idx - 1])
        emit_AV(*seq[-1])
        pair_done[p] = cx.val("t")
        if trunc == "c1" and p == 0:
            return

    APLd = env["APLd"]
    DPREV, BBC2v = env["DPREV"], env["BBC2"]
    aslot = [DPREV[0:1, 0:1024], BBC2[0:1, 0:1024], env["BBC"][0:1, 0:1024]]
    cx.wait("vector", "dn", cx.val("dn"))
    cx.op("vector", lambda e: e.reciprocal(RECIP[:, :], DEN[:, :]), inc=[("v", 1)])
    v_rec = cx.val("v")
    cx.wait("gpsimd", "v", v_rec)
    cx.dma("gpsimd", lambda e: e.dma_start(out=APLd, in_=RECIP[:, :]), "mq")
    r_full = cx.val("mq")
    cx.wait("gpsimd", "mq", r_full)
    apply_done = {}
    slot_free = {0: 0, 1: 0, 2: 0}
    for g in range(64):
        h = g // 4
        k = g % 4
        s = g % 3
        rsem = "rx%d" % (g % 2)
        cx.wait("gpsimd", "t", slot_free[s])
        cx.dma("gpsimd", lambda e, h=h, k=k, s=s: e.dma_start(
            out=aslot[s],
            in_=APLd[8 * h + 2 * k:8 * h + 2 * k + 2, :]
            .rearrange("(a r) c -> a (r c)", a=1)), rsem)
        r_mv = cx.val(rsem)
        bp = 2 + 2 * s
        bb = PS[0:64, bp * 512:(bp + 2) * 512]
        if g >= 3:
            cx.wait("tensor", "v", apply_done[g - 3])
        cx.wait("tensor", rsem, r_mv)
        cx.op("tensor", lambda e, bb=bb, s=s: e.matmul(
            bb[:, 0:512], ONES1[:, 0:64], aslot[s][:, 0:512],
            start=True, stop=True))
        cx.op("tensor", lambda e, bb=bb, s=s: e.matmul(
            bb[:, 512:1024], ONES1[:, 0:64], aslot[s][:, 512:1024],
            start=True, stop=True), inc=[("t", 1)])
        slot_free[s] = t_bc = cx.val("t")
        pi = h // 2
        half = h % 2
        ao = A1[64 * half:64 * half + 64,
                4096 * pi + 1024 * k:4096 * pi + 1024 * (k + 1)]
        cx.wait("vector", "t", t_bc)
        cx.op("vector", lambda e, ao=ao, bb=bb: e.tensor_tensor(
            ao, ao, bb, ALU.mult), inc=[("v", 1)])
        apply_done[g] = cx.val("v")
    v_apply = cx.val("v")
    cx.wb_free_lo = cx.val("t")
    if trunc == "apply":
        return

    cx.wait("gpsimd", "t", cx.val("t"))
    cx.dma("gpsimd", lambda e: e.dma_start(out=BROW[:, :], in_=w["bo"]), "w")
    w_bo = cx.val("w")
    cx.wait("tensor", "v", v_apply)
    cx.wait("tensor", "w", w_bo)
    cx.wait("tensor", "w", w_wo)
    bcast_row(cx, env, BROW, BBC, banks=(6, 7))
    v_bo = cx.val("v")
    evac_d = {0: 0, 1: 0}
    for i in range(NT):
        par = i % 2
        pd = bankpair(par)
        sl = i % 3
        rsem = "rr%d" % sl
        rt = RQ[sl]
        geng, gval = cx.rr_guard[sl]
        cx.wait("sync", geng, gval)
        cx.wait("sync", "str", cx.str_mile.get(i, 0))
        cx.dma("sync", lambda e, rt=rt, i=i: e.dma_start(
            out=rt[:, :], in_=rsrc[128 * i:128 * (i + 1), :]), rsem)
        r_ld = cx.val(rsem)
        cx.wait("tensor", "v", evac_d[par])
        for ci in range(NCH):
            lhsT = A1[:, 4096 * ci + 128 * i:4096 * ci + 128 * (i + 1)]
            mm2(cx, lhsT, WB[:, 12352 + 1024 * ci:12352 + 1024 * ci + 512],
                WB[:, 12352 + 1024 * ci + 512:12352 + 1024 * (ci + 1)],
                pd[:, 0:512], pd[:, 512:1024],
                start=(ci == 0), stop=(ci == NCH - 1), tinc=(ci == NCH - 1))
        t_wo = cx.val("t")
        cx.wait("vector", "t", t_wo)
        cx.wait("vector", rsem, r_ld)
        cx.wait("vector", "v", v_bo)
        cx.op("vector", lambda e, rt=rt, pd=pd: e.tensor_tensor(
            rt[:, :], rt[:, :], pd[:, :], ALU.add), inc=[("v", 1)])
        v_add1 = cx.val("v")
        evac_d[par] = v_add1
        cx.wait("vector", "v", v_add1)
        cx.op("vector", lambda e, rt=rt: e.tensor_tensor(
            rt[:, :], rt[:, :], BBC[:, :], ALU.add), inc=[("v", 1)])
        v_add2 = cx.val("v")
        cx.wait("scalar", "v", v_add2)
        cx.dma("scalar", lambda e, rt=rt, i=i: e.dma_start(
            out=rdst[128 * i:128 * (i + 1), :], in_=rt[:, :]), "str")
        cx.rr_guard[sl] = ("str", cx.val("str"))
        cx.str_mile[i] = cx.val("str")


def emit_newton(cx, env):
    PS, GRAM, ABD, XBD, XST, XSTB, YSB, I2S, TWOI8 = (
        env["PS"], env["GRAM"], env["ABD"], env["XBD"], env["XST"], env["XSTB"],
        env["YSB"], env["I2S"], env["TWOI8"])
    RS16, CIN, CBC, ONES64, ONES1, MQS, STG = (
        env["RS16"], env["CIN"], env["CBC"], env["ONES64"], env["ONES1"],
        env["MQS"], env["STG"])
    QTd, QPTd, WB = env["QTd"], env["QPTd"], env["WB"]

    t_gram = cx.val("t")
    cx.wait("vector", "t", t_gram)
    cx.op("vector", lambda e: e.tensor_copy(GRAM[:, 0:512], PS[0:64, 3072:3584]))
    cx.op("vector", lambda e: e.tensor_copy(GRAM[:, 512:1024], PS[0:64, 3584:4096]),
          inc=[("v", 1)])
    v_gram = cx.val("v")
    cx.wait("vector", "v", v_gram)
    cx.op("vector", lambda e: e.reduce_sum(
        RS16[:, :], GRAM[:, :].rearrange("p (h d) -> p h d", h=H), axis=AX.X,
        apply_absolute_value=True), inc=[("v", 1)])
    v_rs = cx.val("v")
    cx.wait("tensor", "v", v_rs)
    cx.op("tensor", lambda e: e.matmul(PS[0:1, 0:H], ONES64[:, :], RS16[:, :],
                                       start=True, stop=True), inc=[("t", 1)])
    t_rs = cx.val("t")
    cx.wait("vector", "t", t_rs)
    cx.op("vector", lambda e: e.tensor_copy(CIN[:, :], PS[0:1, 0:H]), inc=[("v", 1)])
    v_ci = cx.val("v")
    cx.wait("vector", "v", v_ci)
    cx.op("vector", lambda e: e.reciprocal(CIN[:, :], CIN[:, :]), inc=[("v", 1)])
    v_cr = cx.val("v")
    cx.wait("tensor", "v", v_cr)
    cx.op("tensor", lambda e: e.matmul(PS[:, 0:H], ONES1[:, :], CIN[:, :],
                                       start=True, stop=True), inc=[("t", 1)])
    t_cb = cx.val("t")
    cx.wait("vector", "t", t_cb)
    cx.op("vector", lambda e: e.tensor_copy(CBC[:, :], PS[:, 0:H]), inc=[("v", 1)])
    v_cb = cx.val("v")
    cx.wait("tensor", "v", v_cb)

    for p in range(8):
        cx.op("gpsimd", lambda e, p=p: e.memset(ABD[p][:, :], 0.0))
        cx.op("gpsimd", lambda e, p=p: e.memset(XBD[p][:, :], 0.0))
    cx.op("gpsimd", lambda e: e.nop(), inc=[("g", 1)])
    g_zero = cx.val("g")
    cx.wait("gpsimd", "v", v_gram)
    for p in range(8):
        cx.dma("gpsimd", lambda e, p=p: e.dma_start(
            out=ABD[p][0:64, 0:64], in_=GRAM[:, 64 * (2 * p):64 * (2 * p) + 64]), "mq")
        cx.dma("gpsimd", lambda e, p=p: e.dma_start(
            out=ABD[p][64:128, 64:128],
            in_=GRAM[:, 64 * (2 * p + 1):64 * (2 * p + 1) + 64]), "mq")
    abd_done = cx.val("mq")

    cx.wait("vector", "g", g_zero)
    cx.wait("vector", "v", v_cb)
    for p in range(8):
        cx.op("vector", lambda e, p=p: e.tensor_scalar(
            XST[0:64, 64 * p:64 * p + 64], I2S[0:64, :],
            CBC[0:64, 2 * p:2 * p + 1], None, ALU.mult))
        cx.op("vector", lambda e, p=p: e.tensor_scalar(
            XST[64:128, 64 * p:64 * p + 64], I2S[64:128, :],
            CBC[64:128, 2 * p + 1:2 * p + 2], None, ALU.mult))
    cx.op("vector", lambda e: e.nop(), inc=[("v", 1)])
    prev_v = cx.val("v")

    for it in range(NEWTON_ITERS):
        par = it % 2
        t1b = PS[:, par * 512:par * 512 + 512]
        x2b = PS[:, (2 + par) * 512:(2 + par) * 512 + 512]
        cx.wait("vector", "v", prev_v)
        for p in range(8):
            cx.op("vector", lambda e, p=p: e.tensor_copy(
                XBD[p][0:64, 0:64], XST[0:64, 64 * p:64 * p + 64]))
            cx.op("vector", lambda e, p=p: e.tensor_copy(
                XBD[p][64:128, 64:128], XST[64:128, 64 * p:64 * p + 64]))
        cx.op("vector", lambda e: e.nop(), inc=[("v", 1)])
        v_bd = cx.val("v")
        cx.wait("tensor", "v", v_bd)
        if it == 0:
            cx.wait("tensor", "mq", abd_done)
        for p in range(8):
            cx.op("tensor", lambda e, p=p, t1b=t1b: e.matmul(
                t1b[:, 64 * p:64 * p + 64], ABD[p][:, :],
                XST[:, 64 * p:64 * p + 64], start=True, stop=True),
                inc=[("t", 1)] if p == 7 else None)
        t_t1 = cx.val("t")
        cx.wait("vector", "t", t_t1)
        cx.op("vector", lambda e, t1b=t1b: e.scalar_tensor_tensor(
            YSB[:, :], t1b, -1.0, TWOI8[:, :], ALU.mult, ALU.add), inc=[("v", 1)])
        v_y = cx.val("v")
        cx.wait("tensor", "v", v_y)
        for p in range(8):
            cx.op("tensor", lambda e, p=p, x2b=x2b: e.matmul(
                x2b[:, 64 * p:64 * p + 64], XBD[p][:, :],
                YSB[:, 64 * p:64 * p + 64], start=True, stop=True),
                inc=[("t", 1)] if p == 7 else None)
        t_x2 = cx.val("t")
        cx.wait("vector", "t", t_x2)
        cx.op("vector", lambda e, x2b=x2b: e.tensor_copy(XST[:, :], x2b),
              inc=[("v", 1)])
        prev_v = cx.val("v")
    cx.wait("vector", "v", prev_v)
    cx.op("vector", lambda e: e.tensor_copy(XSTB[:, :], XST[:, :]), inc=[("v", 1)])
    v_xb = cx.val("v")
    return v_xb


def emit_qproj(cx, env, v_xb):
    PS, XSTB, STG = env["PS"], env["XSTB"], env["STG"]
    QTd, QPTd, WB = env["QTd"], env["QPTd"], env["WB"]
    qp_ev = {0: 0, 1: 0}
    qp_t = {}
    qp_mile = {}
    t0 = cx.val("t")
    cx.wait("sync", "pq1", cx.val("pq1"))
    for p in range(8):
        qbuf = WB[:, 4096 * (p % 2):4096 * (p % 2) + 4096]
        cx.wait("sync", "t", qp_t.get(p - 2, t0))
        cx.dma("sync", lambda e, p=p, qbuf=qbuf: e.dma_start(
            out=qbuf, in_=QTd[128 * p:128 * (p + 1), :]), "pq0")
        q_ld = cx.val("pq0")
        cx.wait("tensor", "pq0", q_ld)
        if p == 0:
            cx.wait("tensor", "v", v_xb)
        for h2 in range(2):
            for sc in range(8):
                par = sc % 2
                ob = PS[0:64, (4 + par) * 512:(4 + par) * 512 + 512]
                cx.wait("tensor", "v", qp_ev[par])
                lh = XSTB[64 * h2:64 * h2 + 64, 64 * p:64 * p + 64]
                cx.op("tensor", lambda e, lh=lh, ob=ob, h2=h2, sc=sc: e.matmul(
                    ob, lh, qbuf[64 * h2:64 * h2 + 64, 512 * sc:512 * (sc + 1)],
                    start=True, stop=True), inc=[("t", 1)])
                t_qp = cx.val("t")
                sl = (p * 16 + h2 * 8 + sc) % 4
                ssem = "sg%d" % sl
                stg = STG[sl][0:64, 0:512]
                cx.wait("vector", "t", t_qp)
                cx.wait("vector", ssem, cx.val(ssem))
                cx.op("vector", lambda e, stg=stg, ob=ob: e.tensor_copy(stg, ob),
                      inc=[("v", 1)])
                qp_ev[par] = cx.val("v")
                v_ev = cx.val("v")
                cx.wait("sync", "v", v_ev)
                cx.dma("sync", lambda e, p=p, h2=h2, sc=sc, stg=stg: e.dma_start(
                    out=QPTd[128 * p + 64 * h2:128 * p + 64 * h2 + 64,
                             512 * sc:512 * (sc + 1)], in_=stg), ssem)
        qp_t[p] = cx.val("t")
        qp_mile[p] = [("sg%d" % sl, cx.val("sg%d" % sl)) for sl in range(4)]
    return qp_mile


def emit_ffn(cx, env, w, rsrc, rdst):
    A1, WB, PS, STG, AR2, RQ = (env["A1"], env["WB"], env["PS"], env["STG"],
                                env["AR2"], env["RQ"])
    XT, MUR, MUK, DPREV, XPC, XPREV = (env["XT"], env["MUR"], env["MUK"],
                                       env["DPREV"], env["XPC"], env["XPREV"])

    def bankpair(i):
        return PS[:, 1024 * i:1024 * (i + 1)]

    t_entry = cx.val("t")

    for name, col0 in (("wr", 0), ("wkf", 8192), ("wvf", 16384)):
        cx.wait("gpsimd", "t", cx.wb_free_lo if name == "wr" else t_entry)
        src = w[name].rearrange("(ci p) o -> p ci o", p=128)
        dst = WB[:, col0:col0 + 8192].rearrange("p (ci o) -> p ci o", ci=8)
        cx.dma("gpsimd", lambda e, dst=dst, src=src: e.dma_start(out=dst, in_=src), "w")
    cx.dma("gpsimd", lambda e: e.dma_start(out=MUR[:, :], in_=w["mur"]), "w")
    cx.dma("gpsimd", lambda e: e.dma_start(out=MUK[:, :], in_=w["muk"]), "w")
    w_all = cx.val("w")
    cx.wait("tensor", "w", w_all)
    cx.wait("vector", "w", w_all)

    bank_free = {0: ("v", 0), 1: ("v", 0)}
    state = {"v_mix": 0, "v_mult_last": 0, "dprev_free": ("v", 0),
             "xprev_free": 0}

    def prep_quarter(q):
        t0, t1 = q * 8, (q + 1) * 8
        if q == 0:
            cx.op("gpsimd", lambda e: e.memset(XPC[:, :], 0.0), inc=[("g", 1)])
            cx.wait("vector", "g", cx.val("g"))
            cx.op("vector", lambda e: e.nop(), inc=[("v", 1)])
        else:
            cx.wait("vector", "v", state["v_mix"])
            for ci in range(NCH):
                cx.op("vector", lambda e, ci=ci: e.tensor_copy(
                    XPC[:, ci:ci + 1], A1[:, 1024 * ci + 1023:1024 * ci + 1024]))
            cx.op("vector", lambda e: e.nop(), inc=[("v", 1)])
        v_bnd = cx.val("v")
        emit_ln(cx, env, rsrc, t0, t1, act_sum=(q == 0))
        wait_all_sg(cx, "scalar")
        if q == 0:
            cx.wait("scalar", "t", t_entry)
        cx.wait("scalar", "v", v_bnd)
        for ci in range(NCH):
            dst = A1[:, 1024 * ci:1024 * ci + 1024]
            src = XT[t0 * 128:t1 * 128, 128 * ci:128 * ci + 128]
            cx.dma("scalar", lambda e, dst=dst, src=src: e.dma_start_transpose(
                dst, src), "tr")
        return cx.val("tr")

    def emit_mix_half(ci, mu, dst_off, tr_q, t_gate):
        xt = A1[:, 1024 * ci:1024 * ci + 1024]
        xd = A1[:, dst_off + 1024 * ci:dst_off + 1024 * ci + 1024]
        if ci == 0:
            cx.wait("vector", "tr", tr_q)
            cx.wait("vector", "t", t_gate)
            cx.wait("vector", "w", w_all)
        cx.wait("vector", "v", state["xprev_free"])
        cx.op("vector", lambda e, xt=xt: e.tensor_tensor(
            XPREV[:, 1:1024], xt[:, 1:1024], xt[:, 0:1023], ALU.subtract))
        cx.op("vector", lambda e, xt=xt, ci=ci: e.tensor_tensor(
            XPREV[:, 0:1], xt[:, 0:1], XPC[:, ci:ci + 1], ALU.subtract),
            inc=[("v", 1)])
        v_d = cx.val("v")
        cx.wait("vector", "v", v_d)
        cx.op("vector", lambda e, xd=xd, xt=xt, ci=ci, mu=mu: e.scalar_tensor_tensor(
            xd[:, 1:1024], XPREV[:, 1:1024], mu[:, ci:ci + 1], xt[:, 0:1023],
            ALU.mult, ALU.add))
        cx.op("vector", lambda e, xd=xd, ci=ci, mu=mu: e.scalar_tensor_tensor(
            xd[:, 0:1], XPREV[:, 0:1], mu[:, ci:ci + 1], XPC[:, ci:ci + 1],
            ALU.mult, ALU.add), inc=[("v", 1)])
        state["v_mix"] = cx.val("v")
        state["xprev_free"] = state["v_mix"]

    def r_pass(q, v_xr):
        for jl in range(8):
            par = jl % 2
            pp = bankpair(par)
            cx.wait("tensor", *bank_free[par])
            if jl == 0:
                cx.wait("tensor", "v", v_xr)
            for ci in range(NCH):
                lhsT = A1[:, 8192 + 1024 * ci + 128 * jl:
                          8192 + 1024 * ci + 128 * (jl + 1)]
                mm2(cx, lhsT, WB[:, 1024 * ci:1024 * ci + 512],
                    WB[:, 1024 * ci + 512:1024 * (ci + 1)],
                    pp[:, 0:512], pp[:, 512:1024],
                    start=(ci == 0), stop=(ci == NCH - 1), tinc=(ci == NCH - 1))
            t_r = cx.val("t")
            cx.wait("scalar", "t", t_r)
            if jl == 0:
                cx.wait("scalar", "v", state["v_mult_last"])
            cx.op("scalar", lambda e, jl=jl, pp=pp: e.activation(
                AR2[:, 1024 * jl:1024 * (jl + 1)], pp, AF.Sigmoid), inc=[("a", 1)])
            bank_free[par] = ("a", cx.val("a"))

    def k_pass(q, v_xk, mix_next):
        for dt in range(8):
            par = dt % 2
            pp = bankpair(par)
            cx.wait("tensor", *bank_free[par])
            if dt == 0:
                cx.wait("tensor", "v", v_xk)
            for ci in range(NCH):
                lhsT = WB[:, 8192 + 1024 * ci + 128 * dt:
                          8192 + 1024 * ci + 128 * dt + 128]
                rhs_a = A1[:, 16384 + 1024 * ci:16384 + 1024 * ci + 512]
                rhs_b = A1[:, 16384 + 1024 * ci + 512:16384 + 1024 * (ci + 1)]
                mm2(cx, lhsT, rhs_a, rhs_b, pp[:, 0:512], pp[:, 512:1024],
                    start=(ci == 0), stop=(ci == NCH - 1), tinc=(ci == NCH - 1))
            t_k = cx.val("t")
            cx.wait("scalar", "t", t_k)
            cx.wait("scalar", *state["dprev_free"])
            cx.op("scalar", lambda e, pp=pp: e.activation(
                DPREV[:, :], pp[:, 0:1024], AF.Relu), inc=[("a", 1)])
            a_rl = cx.val("a")
            bank_free[par] = ("a", a_rl)
            cx.wait("scalar", "a", a_rl)
            cx.op("scalar", lambda e, dt=dt: e.activation(
                A1[:, 24576 + 1024 * dt:24576 + 1024 * (dt + 1)], DPREV[:, :],
                AF.Square), inc=[("a", 1)])
            state["dprev_free"] = ("a", cx.val("a"))
            if mix_next is not None:
                emit_mix_half(dt, MUR, 8192, mix_next[0], mix_next[1])
        return cx.val("t"), cx.val("a")

    def v_pass(q, t_k_all, a_k_all, mix_next):
        t0 = q * 8
        for jl in range(8):
            i = t0 + jl
            par = jl % 2
            pp = bankpair(par)
            sl = i % 3
            rsem = "rr%d" % sl
            rt = RQ[sl]
            geng, gval = cx.rr_guard[sl]
            cx.wait("sync", geng, gval)
            cx.wait("sync", "str", cx.str_mile.get(i, 0))
            cx.dma("sync", lambda e, rt=rt, i=i: e.dma_start(
                out=rt[:, :], in_=rsrc[128 * i:128 * (i + 1), :]), rsem)
            r_ld = cx.val(rsem)
            cx.wait("tensor", *bank_free[par])
            if jl == 0:
                cx.wait("tensor", "a", a_k_all)
            for dt in range(NCH):
                lhsT = A1[:, 24576 + 1024 * dt + 128 * jl:
                          24576 + 1024 * dt + 128 * (jl + 1)]
                mm2(cx, lhsT, WB[:, 16384 + 1024 * dt:16384 + 1024 * dt + 512],
                    WB[:, 16384 + 1024 * dt + 512:16384 + 1024 * (dt + 1)],
                    pp[:, 0:512], pp[:, 512:1024],
                    start=(dt == 0), stop=(dt == NCH - 1), tinc=(dt == NCH - 1))
            t_v = cx.val("t")
            cx.wait("vector", "t", t_v)
            cx.wait("vector", *state["dprev_free"])
            cx.op("vector", lambda e, pp=pp, jl=jl: e.tensor_tensor(
                DPREV[:, :], pp[:, 0:1024], AR2[:, 1024 * jl:1024 * (jl + 1)],
                ALU.mult), inc=[("v", 1)])
            v_m = cx.val("v")
            bank_free[par] = ("v", v_m)
            cx.wait("vector", rsem, r_ld)
            cx.wait("vector", "v", v_m)
            cx.op("vector", lambda e, rt=rt: e.tensor_tensor(
                rt[:, :], rt[:, :], DPREV[:, :], ALU.add), inc=[("v", 1)])
            v_add = cx.val("v")
            state["dprev_free"] = ("v", v_add)
            state["v_mult_last"] = v_add
            cx.wait("scalar", "v", v_add)
            cx.dma("scalar", lambda e, rt=rt, i=i: e.dma_start(
                out=rdst[128 * i:128 * (i + 1), :], in_=rt[:, :]), "str")
            cx.rr_guard[sl] = ("str", cx.val("str"))
            cx.str_mile[i] = cx.val("str")
            if mix_next is not None:
                emit_mix_half(jl, MUK, 16384, mix_next[0], mix_next[1])

    tr0 = prep_quarter(0)
    for ci in range(NCH):
        emit_mix_half(ci, MUR, 8192, tr0, t_entry)
    v_xr = state["v_mix"]
    for ci in range(NCH):
        emit_mix_half(ci, MUK, 16384, tr0, t_entry)
    v_xk = state["v_mix"]
    for q in range(NQ):
        r_pass(q, v_xr)
        tr_next = prep_quarter(q + 1) if q < NQ - 1 else None
        t_r_all = cx.val("t")
        t_k_all, a_k_all = k_pass(q, v_xk,
                                  (tr_next, t_r_all) if q < NQ - 1 else None)
        if q == NQ - 1:
            cx.wb_free_lo = t_k_all
        v_xr = state["v_mix"]
        v_pass(q, t_k_all, a_k_all,
               (tr_next, t_k_all) if q < NQ - 1 else None)
        v_xk = state["v_mix"]



def _prep_weights(inputs):
    f = np.ascontiguousarray
    out = {}
    out[("attn0", 0)] = dict(
        wq=f(inputs["L0_attn_Wq"].transpose(1, 0, 2).reshape(C, C)),
        wk=f(inputs["L0_attn_Wk"].transpose(1, 0, 2).reshape(C, C)),
        wv=f(inputs["L0_attn_Wv"].transpose(1, 0, 2).reshape(C, C)),
        wo=f(inputs["L0_attn_Wo"].T),
        bo=f(inputs["L0_attn_bo"].reshape(1, C)),
    )
    out[("attn1", 1)] = dict(
        wq=f(inputs["L1_attn_Wq"].T),
        wk=f(inputs["L1_attn_Wk"].T),
        wv=f(inputs["L1_attn_Wv"].T),
        wo=f(inputs["L1_attn_Wfc"].T),
        bq=f(inputs["L1_attn_bq"].reshape(8, 128).T),
        bk=f(inputs["L1_attn_bk"].reshape(8, 128).T),
        bvr=f(inputs["L1_attn_bv"].reshape(1, C)),
        bkr=f(inputs["L1_attn_bk"].reshape(1, C)),
        bo=f(inputs["L1_attn_bfc"].reshape(1, C)),
    )
    for L in (0, 1):
        p = "L%d_ffn_" % L
        out[("ffn", L)] = dict(
            wr=f(inputs[p + "W_r"].T),
            wkf=f(inputs[p + "W_k"].T),
            wvf=f(inputs[p + "W_v"].T),
            mur=f(inputs[p + "mu_r"].reshape(8, 128).T),
            muk=f(inputs[p + "mu_k"].reshape(8, 128).T),
        )
    return out


FUSED = True


def kernel(**inputs):
    x = np.ascontiguousarray(np.asarray(inputs["L0_attn_x"], dtype=np.float32))
    wmaps = _prep_weights({k: np.asarray(v) for k, v in inputs.items()})
    B = x.shape[0]
    ranges = [(0, 4)] if FUSED else [(i, i + 1) for i in range(4)]
    cur = [np.ascontiguousarray(x[b]) for b in range(B)]
    for (lo, hi) in ranges:
        nc = build_program(lo, hi)
        in_maps = []
        for b in range(B):
            m = {"xin": cur[b]}
            for si, (kind, L) in enumerate(SUBS[lo:hi]):
                for k, arr in wmaps[(kind, L)].items():
                    m["s%d_%s" % (si, k)] = arr
            in_maps.append(m)
        res = run_bass_kernel_spmd(nc, in_maps, core_ids=list(range(8)))
        cur = [np.asarray(res.results[b]["xout"]) for b in range(B)]
    return np.stack(cur, axis=0)


# version 31 (final)
# speedup vs baseline: 1.1866x; 1.0023x over previous
import numpy as np
import concourse.bass as bass
import concourse.mybir as mybir
from concourse.bass_utils import run_bass_kernel_spmd

F32 = mybir.dt.float32
BF16 = mybir.dt.bfloat16
AF = mybir.ActivationFunctionType
ALU = mybir.AluOpType
AX = mybir.AxisListType

T = 4096
C = 1024
H = 16
D = 64
NT = T // 128
NCH = C // 128
SW = 512
NWIN = T // SW
NQ = 4
NEWTON_ITERS = 24
EPS = 1e-5

SUBS = [("attn0", 0), ("ffn", 0), ("attn1", 1), ("ffn", 1)]
TRUNC_LAST = "full"
INTERLEAVE_PROJ = False
NO_GRAM = False
NO_GRAMMM = False
NO_KSTTT = False
NO_GRMM = False
GRAM_SS = False
NO_BCAST = False

SEM_NAMES = (
    ["t", "v", "a", "g"]
    + ["rr%d" % i for i in range(4)]
    + ["sg%d" % i for i in range(4)]
    + ["pq0", "pq1", "pk0", "pk1", "pv0"]
    + ["tr", "w", "str", "dn", "mq"]
    + ["rx%d" % i for i in range(4)]
)


class Ctx:

    def __init__(self, me, eng, sems):
        self.me = me
        self.eng = eng
        self.sems = sems
        self.cnt = {k: 0 for k in sems}
        self.rr_guard = {i: ("v", 0) for i in range(3)}
        self.ln_state = {"var": {0: 0, 1: 0}, "s2": {0: 0, 1: 0},
                         "norm": {0: 0, 1: 0},
                         "ln": {0: 0, 1: 0}}
        self.str_mile = {}
        self.wb_free_lo = 0

    def inc(self, s, n=1):
        self.cnt[s] += n
        return self.cnt[s]

    def val(self, s):
        return self.cnt[s]

    def op(self, engine, builder, inc=None):
        ins = builder(self.eng) if self.me == engine else None
        if inc:
            for s, n in inc:
                self.inc(s, n)
                if ins is not None:
                    ins.then_inc(self.sems[s], n)
        return ins

    def wait(self, engine, s, v):
        if v <= 0:
            return
        if self.me == engine:
            self.eng.wait_ge(self.sems[s], v)

    def dma(self, engine, builder, sem):
        return self.op(engine, builder, inc=[(sem, 16)])


def build_program(sub_lo, sub_hi):
    nc = bass.Bass("TRN2", target_bir_lowering=False)

    xin = nc.dram_tensor("xin", [T, C], F32, kind="ExternalInput").ap()
    xout = nc.dram_tensor("xout", [T, C], F32, kind="ExternalOutput").ap()

    subs = SUBS[sub_lo:sub_hi]
    win = {}
    for si, (kind, L) in enumerate(subs):
        pre = "s%d_" % si
        d = {}
        if kind == "ffn":
            for nm, shp in (("wr", [C, C]), ("wkf", [C, C]), ("wvf", [C, C]),
                            ("mur", [128, 8]), ("muk", [128, 8])):
                d[nm] = nc.dram_tensor(pre + nm, shp, F32, kind="ExternalInput").ap()
        else:
            for nm in ("wq", "wk", "wv", "wo"):
                d[nm] = nc.dram_tensor(pre + nm, [C, C], F32, kind="ExternalInput").ap()
            d["bo"] = nc.dram_tensor(pre + "bo", [1, C], F32, kind="ExternalInput").ap()
            if kind == "attn1":
                for nm, shp in (("bq", [128, 8]), ("bk", [128, 8]),
                                ("bvr", [1, C]), ("bkr", [1, C])):
                    d[nm] = nc.dram_tensor(pre + nm, shp, F32, kind="ExternalInput").ap()
        win[si] = d

    XT = nc.dram_tensor("XTs", [T, C], BF16, kind="Internal").ap()
    QTd = nc.dram_tensor("QTs", [C, T], BF16, kind="Internal").ap()
    KTd = nc.dram_tensor("KTs", [C, T], BF16, kind="Internal").ap()
    QPTd = nc.dram_tensor("QPTs", [C, T], BF16, kind="Internal").ap()
    VSd = nc.dram_tensor("VSs", [T, H * 65], BF16, kind="Internal").ap()
    MQd = nc.dram_tensor("MQs", [3 * C], F32, kind="Internal").ap()
    MKd = nc.dram_tensor("MKs", [C], F32, kind="Internal").ap()
    RAd = nc.dram_tensor("RAs", [T, C], F32, kind="Internal").ap()
    RBd = nc.dram_tensor("RBs", [T, C], F32, kind="Internal").ap()
    APLd = nc.dram_tensor("APLs", [128, SW], F32, kind="Internal").ap()

    n_sub = len(subs)
    rsrc, rdst = [None] * n_sub, [None] * n_sub
    cur = xin
    for si in range(n_sub):
        rsrc[si] = cur
        rdst[si] = xout if si == n_sub - 1 else (RAd if si % 2 == 0 else RBd)
        cur = rdst[si]

    A1 = nc.alloc_sbuf_tensor("A1", [128, 32768], BF16).ap()
    WB = nc.alloc_sbuf_tensor("WB", [128, 3 * 8192], BF16).ap()
    RQ = [nc.alloc_sbuf_tensor("RQ%d" % i, [128, 1024], F32).ap() for i in range(3)]
    STG = [nc.alloc_sbuf_tensor("STG%d" % i, [128, 1040], BF16).ap() for i in range(4)]
    AR2 = nc.alloc_sbuf_tensor("AR2", [128, 8192], BF16).ap()
    STATS = nc.alloc_sbuf_tensor("STATS", [128, 16], F32).ap()
    DPREV = nc.alloc_sbuf_tensor("DPREV", [128, 1024], F32).ap()
    XPC = nc.alloc_sbuf_tensor("XPC", [128, 8], BF16).ap()
    XPREV = nc.alloc_sbuf_tensor("XPREV", [128, 1024], BF16).ap()
    MASKS = nc.alloc_sbuf_tensor("MASKS", [128, 4 * SW], BF16).ap()
    EB = [nc.alloc_sbuf_tensor("EB%d" % i, [128, 1024], BF16).ap() for i in range(2)]
    DEN = nc.alloc_sbuf_tensor("DEN", [128, SW], F32).ap()
    RECIP = nc.alloc_sbuf_tensor("RECIP", [128, SW], F32).ap()
    DSTG = [nc.alloc_sbuf_tensor("DSTG%d" % i, [1, SW], F32).ap() for i in range(2)]
    RSTG = [nc.alloc_sbuf_tensor("RSTG%d" % i, [1, SW], F32).ap() for i in range(2)]
    ONES1 = nc.alloc_sbuf_tensor("ONES1", [1, 128], F32).ap()
    ONES64 = nc.alloc_sbuf_tensor("ONES64", [64, 1], F32).ap()
    BROW = nc.alloc_sbuf_tensor("BROW", [1, C], F32).ap()
    BBC = nc.alloc_sbuf_tensor("BBC", [128, C], F32).ap()
    BBC2 = nc.alloc_sbuf_tensor("BBC2", [128, C], F32).ap()
    MQS = nc.alloc_sbuf_tensor("MQS", [128, 8], F32).ap()
    MQSQ = nc.alloc_sbuf_tensor("MQSQ", [64, 16], F32).ap()
    MKS = nc.alloc_sbuf_tensor("MKS", [128, 8], F32).ap()
    MAXR = nc.alloc_sbuf_tensor("MAXR", [1, 80], F32).ap()
    NEGB = nc.alloc_sbuf_tensor("NEGB", [128, 1], F32).ap()
    MUR = nc.alloc_sbuf_tensor("MUR", [128, 8], F32).ap()
    MUK = nc.alloc_sbuf_tensor("MUK", [128, 8], F32).ap()
    BQP = nc.alloc_sbuf_tensor("BQP", [128, 8], F32).ap()
    BKP = nc.alloc_sbuf_tensor("BKP", [128, 8], F32).ap()
    GRAM = nc.alloc_sbuf_tensor("GRAM", [64, H * 64], F32).ap()
    ABD = [nc.alloc_sbuf_tensor("ABD%d" % p, [128, 128], F32).ap() for p in range(8)]
    XBD = [nc.alloc_sbuf_tensor("XBD%d" % p, [128, 128], F32).ap() for p in range(8)]
    XST = nc.alloc_sbuf_tensor("XST", [128, 512], F32).ap()
    XSTB = nc.alloc_sbuf_tensor("XSTB", [128, 512], BF16).ap()
    YSB = nc.alloc_sbuf_tensor("YSB", [128, 512], F32).ap()
    I2S = nc.alloc_sbuf_tensor("I2S", [128, 64], F32).ap()
    TWOI8 = nc.alloc_sbuf_tensor("TWOI8", [128, 512], F32).ap()
    RS16 = nc.alloc_sbuf_tensor("RS16", [64, H], F32).ap()
    CIN = nc.alloc_sbuf_tensor("CIN", [1, H], F32).ap()
    CBC = nc.alloc_sbuf_tensor("CBC", [128, H], F32).ap()
    EPSB = nc.alloc_sbuf_tensor("EPSB", [128, 1], F32).ap()

    PS = nc.alloc_psum_tensor("PS", [128, 4096], F32).ap()

    sems = {s: nc.alloc_semaphore(s) for s in SEM_NAMES}

    env = dict(locals())

    def emit(me, eng):
        cx = Ctx(me, eng, sems)
        emit_prologue(cx, env)
        for si, (kind, L) in enumerate(subs):
            if kind == "ffn":
                emit_ffn(cx, env, win[si], rsrc[si], rdst[si])
            else:
                emit_attn(cx, env, kind, win[si], rsrc[si], rdst[si],
                          trunc=(TRUNC_LAST if si == len(subs) - 1 else "full"))
        cx.wait("sync", "str", cx.val("str"))

    with nc.Block() as block:
        block.tensor(lambda eng: emit("tensor", eng))
        block.vector(lambda eng: emit("vector", eng))
        block.scalar(lambda eng: emit("scalar", eng))
        block.gpsimd(lambda eng: emit("gpsimd", eng))
        block.sync(lambda eng: emit("sync", eng))

    return nc



def emit_prologue(cx, env):
    MASKS, ONES1, ONES64, I2S, TWOI8 = (env["MASKS"], env["ONES1"],
                                        env["ONES64"], env["I2S"], env["TWOI8"])
    for phi in range(4):
        m = MASKS[:, phi * SW:(phi + 1) * SW]
        cx.op("gpsimd", lambda e, m=m: e.memset(m, 1.0))
        cx.op("gpsimd", lambda e, m=m, phi=phi: e.affine_select(
            out=m, in_=m, compare_op=ALU.is_ge, fill=0.0,
            base=-128 * phi, pattern=[[1, SW]], channel_multiplier=-1))
    cx.op("gpsimd", lambda e: e.memset(ONES1[:, :], 1.0))
    cx.op("gpsimd", lambda e: e.memset(env["EPSB"][:, :], EPS))
    cx.op("gpsimd", lambda e: e.memset(ONES64[:, :], 1.0))
    cx.op("gpsimd", lambda e: e.memset(I2S[:, :], 1.0))
    cx.op("gpsimd", lambda e: e.affine_select(
        out=I2S[0:64, :], in_=I2S[0:64, :], compare_op=ALU.is_equal,
        fill=0.0, base=0, pattern=[[-1, 64]], channel_multiplier=1))
    cx.op("gpsimd", lambda e: e.affine_select(
        out=I2S[64:128, :], in_=I2S[64:128, :], compare_op=ALU.is_equal,
        fill=0.0, base=64, pattern=[[-1, 64]], channel_multiplier=1))
    for p in range(8):
        cx.op("gpsimd", lambda e, p=p: e.tensor_scalar_mul(
            TWOI8[:, 64 * p:64 * p + 64], I2S[:, :], 2.0))
    cx.op("gpsimd", lambda e: e.nop(), inc=[("g", 1)])


def emit_ln(cx, env, rsrc, t0, t1, window_cb=None, tile_cb=None,
            act_sum=False):
    RQ, STG, STATS, XT = (env["RQ"], env["STG"], env["STATS"], env["XT"])
    st = cx.ln_state
    vnorm = {}

    def emit_tr(i):
        ssl = i % 4
        stg = STG[ssl][:, 0:1024]
        cx.wait("scalar", "a", vnorm[i])
        cx.dma("scalar", lambda e, stg=stg, i=i: e.dma_start(
            out=XT[128 * i:128 * (i + 1), :], in_=stg), "sg%d" % ssl)
        if window_cb is not None and i % 8 == 7:
            window_cb(i)

    var_of = {}

    def stage_a(i):
        sl = i % 3
        rsem = "rr%d" % sl
        ssl = i % 4
        ssem = "sg%d" % ssl
        rt = RQ[sl]
        stg = STG[ssl][:, 0:1024]
        lane = (i - t0) % 2
        b = 8 * lane
        geng, gval = cx.rr_guard[sl]
        cx.wait("sync", geng, gval)
        cx.wait("sync", "str", cx.str_mile.get(i, 0))
        cx.dma("sync", lambda e, rt=rt, i=i: e.dma_start(
            out=rt[:, :], in_=rsrc[128 * i:128 * (i + 1), :]), rsem)
        v_ld = cx.val(rsem)
        cx.wait("scalar", "v", st["var"][lane])
        cx.wait("scalar", rsem, v_ld)
        cx.wait("scalar", ssem, cx.val(ssem))
        cx.op("scalar", lambda e, rt=rt, stg=stg, b=b: e.activation(
            stg, rt[:, :], AF.Square, accum_out=STATS[:, b + 1:b + 2]),
            inc=[("a", 1)])
        a_sq = cx.val("a")
        if act_sum:
            cx.wait("scalar", "v", st["s2"][lane])
            cx.op("scalar", lambda e, rt=rt, stg=stg, b=b: e.activation(
                stg, rt[:, :], AF.Identity, accum_out=STATS[:, b:b + 1]),
                inc=[("a", 1)])
            a_sum = cx.val("a")
            cx.wait("vector", "a", a_sum)
        else:
            cx.wait("vector", "v", st["s2"][lane])
            cx.wait("vector", rsem, v_ld)
            cx.op("vector", lambda e, rt=rt, b=b: e.reduce_sum(
                STATS[:, b:b + 1], rt[:, :], axis=AX.X), inc=[("v", 1)])
            v_red = cx.val("v")
            cx.wait("vector", "v", v_red)
        cx.wait("vector", "a", st["norm"][lane])
        cx.op("vector", lambda e, b=b: e.tensor_scalar_mul(
            STATS[:, b + 2:b + 3], STATS[:, b:b + 1], -1.0 / C))
        cx.op("vector", lambda e, b=b: e.tensor_tensor(
            STATS[:, b + 3:b + 4], STATS[:, b:b + 1], STATS[:, b:b + 1],
            ALU.mult), inc=[("v", 1)])
        st["s2"][lane] = v_s2 = cx.val("v")
        cx.wait("vector", "a", a_sq)
        cx.wait("vector", "a", st["ln"][lane])
        cx.wait("vector", "v", v_s2)
        cx.op("vector", lambda e, b=b: e.scalar_tensor_tensor(
            STATS[:, b + 4:b + 5], STATS[:, b + 3:b + 4], -1.0 / C,
            STATS[:, b + 1:b + 2], ALU.mult, ALU.add), inc=[("v", 1)])
        st["var"][lane] = var_of[i] = cx.val("v")

    def stage_b(i):
        sl = i % 3
        ssl = i % 4
        ssem = "sg%d" % ssl
        rt = RQ[sl]
        stg = STG[ssl][:, 0:1024]
        lane = (i - t0) % 2
        b = 8 * lane
        cx.wait("scalar", "v", var_of[i])
        cx.op("scalar", lambda e, b=b: e.activation(
            STATS[:, b + 5:b + 6], STATS[:, b + 4:b + 5], AF.Ln,
            bias=env["EPSB"][:, :], scale=1.0 / C), inc=[("a", 1)])
        a_ln = cx.val("a")
        st["ln"][lane] = a_ln
        cx.wait("scalar", "a", a_ln)
        cx.wait("scalar", "a", st["norm"][lane])
        cx.op("scalar", lambda e, b=b: e.activation(
            STATS[:, b + 6:b + 7], STATS[:, b + 5:b + 6], AF.Exp, scale=-0.5),
            inc=[("a", 1)])
        cx.op("scalar", lambda e, b=b: e.activation(
            STATS[:, b + 7:b + 8], STATS[:, b + 2:b + 3], AF.Identity,
            scale=STATS[:, b + 6:b + 7]), inc=[("a", 1)])
        a_bias = cx.val("a")
        cx.wait("scalar", "a", a_bias)
        cx.wait("scalar", ssem, cx.val(ssem))
        cx.op("scalar", lambda e, rt=rt, stg=stg, b=b: e.activation(
            stg, rt[:, :], AF.Identity, bias=STATS[:, b + 7:b + 8],
            scale=STATS[:, b + 6:b + 7]), inc=[("a", 1)])
        a_norm = cx.val("a")
        st["norm"][lane] = a_norm
        cx.rr_guard[sl] = ("a", a_norm)
        vnorm[i] = a_norm

    stage_a(t0)
    for i in range(t0 + 1, t1):
        stage_a(i)
        stage_b(i - 1)
        if i - 3 >= t0:
            emit_tr(i - 3)
        if tile_cb is not None:
            tile_cb(i)
    stage_b(t1 - 1)
    for i in (t1 - 3, t1 - 2, t1 - 1):
        if i >= t0:
            emit_tr(i)


def wait_all_sg(cx, engine):
    for sl in range(4):
        cx.wait(engine, "sg%d" % sl, cx.val("sg%d" % sl))
    cx.wait(engine, "pq1", cx.val("pq1"))


def mm2(cx, lhsT, rhs_a, rhs_b, out_a, out_b, start, stop, tinc=False):
    cx.op("tensor", lambda e: e.matmul(out_a, lhsT, rhs_a, start=start, stop=stop))
    cx.op("tensor", lambda e: e.matmul(out_b, lhsT, rhs_b, start=start, stop=stop),
          inc=[("t", 1)] if tinc else None)


def bcast_row(cx, env, row_ap, dst, banks=(4, 5)):
    PS, ONES1 = env["PS"], env["ONES1"]
    b0 = PS[:, 512 * banks[0]:512 * banks[0] + 512]
    b1 = PS[:, 512 * banks[1]:512 * banks[1] + 512]
    cx.op("tensor", lambda e: e.matmul(b0, ONES1[:, :], row_ap[:, 0:512],
                                       start=True, stop=True))
    cx.op("tensor", lambda e: e.matmul(b1, ONES1[:, :], row_ap[:, 512:1024],
                                       start=True, stop=True), inc=[("t", 1)])
    t_b = cx.val("t")
    cx.wait("vector", "t", t_b)
    cx.op("vector", lambda e: e.tensor_copy(dst[:, 0:512], b0))
    cx.op("vector", lambda e: e.tensor_copy(dst[:, 512:1024], b1), inc=[("v", 1)])
    v_b = cx.val("v")
    cx.wait("tensor", "v", v_b)
    return v_b


def emit_attn(cx, env, kind, w, rsrc, rdst, trunc="full"):
    A1, WB, PS, STG, RQ, AR2 = (env["A1"], env["WB"], env["PS"],
                                env["STG"], env["RQ"], env["AR2"])
    QTd, KTd, QPTd, VSd, XT = (env["QTd"], env["KTd"], env["QPTd"], env["VSd"],
                               env["XT"])
    MQS, MKS, MAXR, NEGB = env["MQS"], env["MKS"], env["MAXR"], env["NEGB"]
    MQd = env["MQd"]
    BROW, BBC, BBC2, BQP, BKP = (env["BROW"], env["BBC"], env["BBC2"],
                                 env["BQP"], env["BKP"])
    ONES1 = env["ONES1"]
    MASKS, EB, DEN, RECIP, DSTG, RSTG = (env["MASKS"], env["EB"], env["DEN"],
                                         env["RECIP"], env["DSTG"], env["RSTG"])
    l1 = (kind == "attn1")
    sm_scale = (1.0 / np.sqrt(D)) if not l1 else (1.0 / np.sqrt(C))

    def bank(i):
        return PS[:, 512 * i:512 * (i + 1)]

    def bankpair(i):
        return PS[:, 1024 * i:1024 * (i + 1)]

    t_entry = cx.val("t")

    for name, col0, buf in (("wq", 0, WB), ("wk", 8192, WB), ("wv", 16384, WB)):
        cx.wait("gpsimd", "t", t_entry if name == "wv" else cx.wb_free_lo)
        src = w[name].rearrange("(ci p) o -> p ci o", p=128)
        dst = buf[:, col0:col0 + 8192].rearrange("p (ci o) -> p ci o", ci=8)
        cx.dma("gpsimd", lambda e, dst=dst, src=src: e.dma_start(out=dst, in_=src), "w")
    if l1:
        cx.dma("gpsimd", lambda e: e.dma_start(out=BQP[:, :], in_=w["bq"]), "w")
        cx.dma("gpsimd", lambda e: e.dma_start(out=BKP[:, :], in_=w["bk"]), "w")
    w_all = cx.val("w")

    cx.op("gpsimd", lambda e: e.memset(MQS[:, :], 0.0))
    cx.op("gpsimd", lambda e: e.memset(MKS[:, :], 0.0), inc=[("g", 1)])
    g_z = cx.val("g")
    trw = {}

    def phase_a_window(j):
        w2 = j // 8
        if w2 == 0:
            cx.wait("scalar", "t", t_entry)
        wait_all_sg(cx, "scalar")
        for ci in range(NCH):
            dst = A1[:, ci * 4096 + 1024 * w2:ci * 4096 + 1024 * (w2 + 1)]
            src = XT[1024 * w2:1024 * (w2 + 1), 128 * ci:128 * ci + 128]
            cx.dma("scalar", lambda e, dst=dst, src=src: e.dma_start_transpose(
                dst, src), "tr")
        trw[w2] = cx.val("tr")

    evac_guard = {0: 0, 1: 0}

    def proj_pass_gen():
        cx.wait("tensor", "w", w_all)
        cx.wait("vector", "g", g_z)
        for wcol0, out_dram, bias_pp, track, mbuf in (
                (0, QTd, BQP if l1 else None, True, MQS),
                (8192, KTd, BKP if l1 else None, True, MKS)):
            for w2 in range(4):
                cx.wait("tensor", "tr", trw[w2])
                for dt in range(NCH):
                    it = w2 * 8 + dt
                    par = it % 2
                    pa, pb = bank(2 * par), bank(2 * par + 1)
                    cx.wait("tensor", "v", evac_guard[par])
                    for ci in range(NCH):
                        lhsT = WB[:, wcol0 + 1024 * ci + 128 * dt:
                                  wcol0 + 1024 * ci + 128 * dt + 128]
                        rhs_a = A1[:, ci * 4096 + 1024 * w2:
                                   ci * 4096 + 1024 * w2 + 512]
                        rhs_b = A1[:, ci * 4096 + 1024 * w2 + 512:
                                   ci * 4096 + 1024 * (w2 + 1)]
                        mm2(cx, lhsT, rhs_a, rhs_b, pa, pb,
                            start=(ci == 0), stop=(ci == NCH - 1),
                            tinc=(ci == NCH - 1))
                    t_done = cx.val("t")
                    sl = it % 8
                    stg = AR2[:, 1024 * sl:1024 * sl + 1024]
                    cx.wait("vector", "t", t_done)
                    cx.wait("vector", "pq1", max(0, cx.val("pq1") - 7 * 16))
                    if bias_pp is not None:
                        cx.op("vector", lambda e, stg=stg, dt=dt, pa=pa:
                              e.tensor_scalar(stg[:, 0:512], pa,
                                              bias_pp[:, dt:dt + 1], None,
                                              ALU.add))
                        cx.op("vector", lambda e, stg=stg, dt=dt, pb=pb:
                              e.tensor_scalar(stg[:, 512:1024], pb,
                                              bias_pp[:, dt:dt + 1], None,
                                              ALU.add), inc=[("v", 1)])
                    else:
                        cx.op("vector", lambda e, stg=stg, pa=pa: e.tensor_copy(
                            stg[:, 0:512], pa))
                        cx.op("vector", lambda e, stg=stg, pb=pb: e.tensor_copy(
                            stg[:, 512:1024], pb), inc=[("v", 1)])
                    v_evac = cx.val("v")
                    evac_guard[par] = v_evac
                    if track:
                        cx.wait("vector", "v", v_evac)
                        cx.op("vector", lambda e, stg=stg: e.reduce_max(
                            env["STATS"][:, 7:8], stg, axis=AX.X,
                            apply_absolute_value=True), inc=[("v", 1)])
                        v_r = cx.val("v")
                        cx.wait("vector", "v", v_r)
                        cx.op("vector", lambda e, dt=dt, mbuf=mbuf: e.tensor_tensor(
                            mbuf[:, dt:dt + 1], mbuf[:, dt:dt + 1],
                            env["STATS"][:, 7:8], ALU.max), inc=[("v", 1)])
                    cx.wait("gpsimd", "v", v_evac)
                    cx.dma("gpsimd", lambda e, stg=stg, dt=dt, w2=w2,
                           out_dram=out_dram: e.dma_start(
                        out=out_dram[128 * dt:128 * (dt + 1),
                                     1024 * w2:1024 * (w2 + 1)],
                        in_=stg), "pq1")
                    yield

    pgen = proj_pass_gen()
    tiles_seen = [0]

    def phase_a_tile(i):
        tiles_seen[0] = i
        if INTERLEAVE_PROJ and i >= 11:
            next(pgen, None)

    emit_ln(cx, env, rsrc, 0, NT, window_cb=phase_a_window,
            tile_cb=phase_a_tile)
    for _ in pgen:
        pass
    if trunc == "proj":
        return

    if l1 and not NO_BCAST:
        cx.dma("gpsimd", lambda e: e.dma_start(out=BROW[:, :], in_=w["bvr"]), "w")
        w_bv = cx.val("w")
        cx.wait("tensor", "w", w_bv)
        bcast_row(cx, env, BROW, BBC2, banks=(4, 5))
        t_after_bv = cx.val("t")
        cx.wait("gpsimd", "t", t_after_bv)
        cx.dma("gpsimd", lambda e: e.dma_start(out=BROW[:, :], in_=w["bkr"]), "w")
        w_bk = cx.val("w")
        cx.wait("tensor", "w", w_bk)
        bcast_row(cx, env, BROW, BBC, banks=(4, 5))

    evac_v = {0: evac_guard[0], 1: evac_guard[1]}
    kst_guard = 0
    l1g = l1 and not NO_GRAM
    if l1g:
        cx.wait("vector", "pq1", cx.val("pq1"))
    for j in range(NT):
        par = j % 2
        pv = bankpair(par)
        pkv = bankpair(2) if l1g else None
        cx.wait("tensor", "v", evac_v[par])
        for ci in range(NCH):
            lhsT = A1[:, ci * 4096 + 128 * j:ci * 4096 + 128 * (j + 1)]
            cx.op("tensor", lambda e, lhsT=lhsT, pv=pv, ci=ci: e.matmul(
                pv[:, 0:512], lhsT, WB[:, 16384 + 1024 * ci:16384 + 1024 * ci + 512],
                start=(ci == 0), stop=(ci == NCH - 1)))
            cx.op("tensor", lambda e, lhsT=lhsT, pv=pv, ci=ci: e.matmul(
                pv[:, 512:1024], lhsT,
                WB[:, 16384 + 1024 * ci + 512:16384 + 1024 * (ci + 1)],
                start=(ci == 0), stop=(ci == NCH - 1)),
                inc=[("t", 1)] if (not l1g and ci == NCH - 1) else None)
            if l1g:
                cx.op("tensor", lambda e, lhsT=lhsT, pkv=pkv, ci=ci: e.matmul(
                    pkv[:, 0:512], lhsT, WB[:, 8192 + 1024 * ci:8192 + 1024 * ci + 512],
                    start=(ci == 0), stop=(ci == NCH - 1)))
                cx.op("tensor", lambda e, lhsT=lhsT, pkv=pkv, ci=ci: e.matmul(
                    pkv[:, 512:1024], lhsT,
                    WB[:, 8192 + 1024 * ci + 512:8192 + 1024 * (ci + 1)],
                    start=(ci == 0), stop=(ci == NCH - 1)),
                    inc=[("t", 1)] if ci == NCH - 1 else None)
        t_v = cx.val("t")
        sl = j % 4
        ssem = "sg%d" % sl
        stg = STG[sl]
        cx.wait("vector", "t", t_v)
        cx.wait("vector", ssem, cx.val(ssem))
        stg3 = stg.rearrange("p (h d) -> p h d", h=H)
        pv3 = pv.rearrange("p (h d) -> p h d", h=H)
        if l1 and not NO_BCAST:
            cx.op("vector", lambda e, stg3=stg3, pv3=pv3: e.tensor_tensor(
                stg3[:, :, 0:64], pv3,
                BBC2[:, :].rearrange("p (h d) -> p h d", h=H), ALU.add),
                inc=[("v", 1)])
        else:
            cx.op("vector", lambda e, stg3=stg3, pv3=pv3: e.tensor_copy(
                stg3[:, :, 0:64], pv3), inc=[("v", 1)])
        v_evac = cx.val("v")
        cx.wait("vector", "v", v_evac)
        cx.op("vector", lambda e, stg3=stg3: e.memset(stg3[:, :, 64:65], 1.0),
              inc=[("v", 1)])
        v_ones = cx.val("v")
        evac_v[par] = v_ones
        if l1g and not NO_GRAMMM:
            kst = env["AR2"][:, 0:1024]
            if not NO_KSTTT:
                cx.wait("vector", "t", kst_guard)
                cx.op("vector", lambda e, kst=kst, pkv=pkv: e.tensor_tensor(
                    kst, pkv, BBC[:, :], ALU.add), inc=[("v", 1)])
                v_kst = cx.val("v")
                cx.wait("tensor", "v", v_kst)
            if not NO_GRMM:
                for h in range(H):
                    lh = kst[:, 64 * h:64 * (h + 1)]
                    st = (j == 0 and (h == 0 or h == 8)) if not GRAM_SS else True
                    sp = False if not GRAM_SS else True
                    cx.op("tensor", lambda e, lh=lh, st=st, sp=sp, h=h: e.matmul(
                        PS[0:64, 3072 + 64 * h:3072 + 64 * (h + 1)], lh, lh,
                        start=st, stop=sp),
                        inc=[("t", 1)] if h == H - 1 else None)
                kst_guard = cx.val("t")
        cx.wait("sync", "v", v_ones)
        cx.dma("sync", lambda e, stg=stg, j=j: e.dma_start(
            out=VSd[128 * j:128 * (j + 1), :], in_=stg[:, :]), ssem)

    cx.wait("gpsimd", "t", cx.val("t"))
    srco = w["wo"].rearrange("(ci p) o -> p ci o", p=128)
    dsto = WB[:, 12352:20544].rearrange("p (ci o) -> p ci o", ci=8)
    cx.dma("gpsimd", lambda e: e.dma_start(out=dsto, in_=srco), "w")
    w_wo = cx.val("w")

    if trunc == "vpass":
        return
    if l1:
        v_xb = emit_newton(cx, env)
        cx.wait("vector", "v", v_xb)
        cx.op("vector", lambda e: e.reduce_max(
            env["STATS"][:, 15:16], env["XST"][:, :], axis=AX.X,
            apply_absolute_value=True), inc=[("v", 1)])
        v_gm = cx.val("v")
        MROW3 = env["DPREV"][0:1, 0:128]
        cx.wait("gpsimd", "v", v_gm)
        cx.dma("gpsimd", lambda e: e.dma_start(
            out=MQd[2048:2176].rearrange("(a c) -> a c", c=1),
            in_=env["STATS"][:, 15:16]), "mq")
        vq3 = cx.val("mq")
        cx.wait("gpsimd", "mq", vq3)
        cx.dma("gpsimd", lambda e, MROW3=MROW3: e.dma_start(
            out=MROW3, in_=MQd[2048:2176].rearrange("(a c) -> a c", a=1)), "mq")
        mg_done = cx.val("mq")
        cx.wait("vector", "mq", mg_done)
        cx.op("vector", lambda e, MROW3=MROW3: e.reduce_max(
            MAXR[0:1, 67:68], MROW3, axis=AX.X), inc=[("v", 1)])
    if trunc == "newton":
        return

    MROW = env["BBC2"][0:1, :]
    MROW2 = env["BBC2"][64:65, :]

    def fold_max(mbuf, row, col):
        cx.wait("gpsimd", "v", cx.val("v"))
        cx.dma("gpsimd", lambda e, mbuf=mbuf, col=col: e.dma_start(
            out=MQd[1024 * col:1024 * (col + 1)], in_=mbuf[:, :]), "mq")
        vq = cx.val("mq")
        cx.wait("gpsimd", "mq", vq)
        cx.dma("gpsimd", lambda e, row=row, col=col: e.dma_start(
            out=row, in_=MQd[1024 * col:1024 * (col + 1)]
            .rearrange("(a c) -> a c", a=1)), "mq")
        return cx.val("mq")

    fold_max(MQS, MROW, 0)
    mq_done = fold_max(MKS, MROW2, 1)
    cx.wait("vector", "mq", mq_done)
    cx.op("vector", lambda e: e.reduce_max(
        MAXR[0:1, 64:65], MROW, axis=AX.X))
    cx.op("vector", lambda e: e.reduce_max(
        MAXR[0:1, 65:66], MROW2, axis=AX.X), inc=[("v", 1)])
    v_hm = cx.val("v")
    cx.wait("vector", "v", v_hm)
    cx.op("vector", lambda e: e.tensor_tensor(
        MAXR[0:1, 66:67], MAXR[0:1, 64:65], MAXR[0:1, 65:66], ALU.mult),
        inc=[("v", 1)])
    v_b1 = cx.val("v")
    cx.wait("vector", "v", v_b1)
    if l1:
        cx.op("vector", lambda e: e.tensor_tensor(
            MAXR[0:1, 66:67], MAXR[0:1, 66:67], MAXR[0:1, 67:68], ALU.mult),
            inc=[("v", 1)])
        cx.wait("vector", "v", cx.val("v"))
    cx.op("vector", lambda e: e.tensor_scalar(
        MAXR[0:1, 66:67], MAXR[0:1, 66:67],
        (64.0 * 64.0 if l1 else 64.0) * sm_scale, 80.0,
        ALU.mult, ALU.min), inc=[("v", 1)])
    v_b2 = cx.val("v")
    cx.wait("vector", "v", v_b2)
    cx.op("vector", lambda e: e.tensor_scalar_mul(
        MAXR[0:1, 66:67], MAXR[0:1, 66:67], -1.0), inc=[("v", 1)])
    v_b3 = cx.val("v")
    cx.wait("tensor", "v", v_b3)
    cx.op("tensor", lambda e: e.matmul(PS[:, 0:1], ONES1[:, :], MAXR[0:1, 66:67],
                                       start=True, stop=True), inc=[("t", 1)])
    t_nb = cx.val("t")
    cx.wait("vector", "t", t_nb)
    cx.op("vector", lambda e: e.tensor_copy(NEGB[:, :], PS[:, 0:1]), inc=[("v", 1)])
    v_nb = cx.val("v")
    cx.wait("tensor", "v", v_nb)
    qp_mile = emit_qproj(cx, env, v_xb) if l1 else None

    qsrc = QPTd if l1 else QTd
    dn_base = cx.val("dn")
    pair_done = {}
    s_war = {0: 0, 1: 0}
    eb_war = {0: 0, 1: 0}
    o_ev = {0: 0, 1: 0}
    for p in range(8):
        qb = WB[:, 0:4096]
        kb = WB[:, 4096:8192]
        vb = WB[:, 8192:8192 + 4160]
        tp_guard = cx.val("t") if p == 0 else pair_done[p - 1]
        cx.wait("sync", "pq1", cx.val("pq1"))
        if qp_mile is not None:
            for snm, sval in qp_mile[p]:
                cx.wait("sync", snm, sval)
        cx.wait("sync", "t", tp_guard)
        q_lds = []
        for c, sem in ((0, "pq0"), (1, "pk1")):
            cx.dma("sync", lambda e, qb=qb, p=p, c=c: e.dma_start(
                out=qb[:, 2048 * c:2048 * (c + 1)],
                in_=qsrc[128 * p:128 * (p + 1), 2048 * c:2048 * (c + 1)]), sem)
            q_lds.append((sem, cx.val(sem)))
        cx.wait("scalar", "pq1", cx.val("pq1"))
        cx.wait("scalar", "t", tp_guard)
        k_lds = []
        for c, sem in ((0, "pk0"), (1, "rx0")):
            cx.dma("scalar", lambda e, kb=kb, p=p, c=c: e.dma_start(
                out=kb[:, 2048 * c:2048 * (c + 1)],
                in_=KTd[128 * p:128 * (p + 1), 2048 * c:2048 * (c + 1)]), sem)
            k_lds.append((sem, cx.val(sem)))
        wait_all_sg(cx, "gpsimd")
        cx.wait("gpsimd", "t", tp_guard)
        v_lds = {}
        for g in range(4):
            sem = "pv0" if g < 2 else "rx1"
            cx.dma("gpsimd", lambda e, vb=vb, p=p, g=g: e.dma_start(
                out=vb.rearrange("q (j d) -> q j d", d=130)[:, 8 * g:8 * (g + 1), :],
                in_=VSd[:, 65 * 2 * p:65 * 2 * p + 130]
                .rearrange("(j q) d -> q j d", q=128)[:, 8 * g:8 * (g + 1), :]),
                sem)
            v_lds[g] = (sem, cx.val(sem))
        q_seen, k_seen, v_seen = set(), set(), set()

        gates = {}

        def opair(wi):
            ob = 4 if wi % 2 == 0 else 6
            return (PS[0:65, ob * 512:(ob + 1) * 512],
                    PS[0:65, (ob + 1) * 512:(ob + 2) * 512])

        def emit_S(wi, j, jm, qb=qb, kb=kb):
            par = j % 2
            s01 = PS[:, 1024 * par:1024 * (par + 1)]
            cx.wait("tensor", "a", s_war[par])
            qc = wi // 4
            if qc not in q_seen:
                q_seen.add(qc)
                cx.wait("tensor", *q_lds[qc])
            kc = j // 16
            if kc not in k_seen:
                k_seen.add(kc)
                cx.wait("tensor", *k_lds[kc])
            cx.op("tensor", lambda e, kb=kb, qb=qb, s01=s01, j=j, wi=wi:
                  e.matmul(s01[:, 0:512], kb[0:64, 128 * j:128 * (j + 1)],
                           qb[0:64, SW * wi:SW * (wi + 1)],
                           start=True, stop=True))
            cx.op("tensor", lambda e, kb=kb, qb=qb, s01=s01, j=j, wi=wi:
                  e.matmul(s01[:, 512:1024], kb[64:128, 128 * j:128 * (j + 1)],
                           qb[64:128, SW * wi:SW * (wi + 1)],
                           start=True, stop=True), inc=[("t", 1)])
            t_s = cx.val("t")
            cx.wait("scalar", "t", t_s)
            cx.wait("scalar", "t", eb_war[par])
            cx.op("scalar", lambda e, s01=s01, par=par: e.activation(
                EB[par][:, :], s01, AF.Exp, bias=NEGB[:, 0:1],
                scale=sm_scale), inc=[("a", 1)])
            a_exp = cx.val("a")
            s_war[par] = a_exp
            masked = (not l1) and (j >= 4 * wi)
            if masked:
                phi = j - 4 * wi
                cx.wait("vector", "a", a_exp)
                cx.op("vector", lambda e, par=par, phi=phi: e.tensor_tensor(
                    EB[par][:, 0:512], EB[par][:, 0:512],
                    MASKS[:, SW * phi:SW * (phi + 1)], ALU.mult))
                cx.op("vector", lambda e, par=par, phi=phi: e.tensor_tensor(
                    EB[par][:, 512:1024], EB[par][:, 512:1024],
                    MASKS[:, SW * phi:SW * (phi + 1)], ALU.mult),
                    inc=[("v", 1)])
                gates[(wi, j)] = ("v", cx.val("v"))
            else:
                gates[(wi, j)] = ("a", a_exp)

        def emit_AV(wi, j, jm, vb=vb):
            par = j % 2
            o0, o1 = opair(wi)
            g_eng, g_val = gates[(wi, j)]
            cx.wait("tensor", g_eng, g_val)
            if j == 0:
                cx.wait("tensor", "v", o_ev[wi % 2])
            vg = j // 16
            if vg not in v_seen:
                v_seen.add(vg)
                cx.wait("tensor", *v_lds[2 * vg + 1])
            cx.op("tensor", lambda e, vb=vb, o0=o0, j=j, jm=jm, par=par:
                  e.matmul(o0, vb[:, 130 * j:130 * j + 65],
                           EB[par][:, 0:512],
                           start=(j == 0), stop=(j == jm - 1)))
            cx.op("tensor", lambda e, vb=vb, o1=o1, j=j, jm=jm, par=par:
                  e.matmul(o1, vb[:, 130 * j + 65:130 * j + 130],
                           EB[par][:, 512:1024],
                           start=(j == 0), stop=(j == jm - 1)),
                  inc=[("t", 1)])
            eb_war[par] = cx.val("t")
            if j == jm - 1:
                emit_evac(wi)

        def emit_evac(wi):
            o0, o1 = opair(wi)
            t_av = cx.val("t")
            idx = p * NWIN + wi
            sl0, sl1 = 0, 1
            cx.wait("vector", "t", t_av)
            cx.op("vector", lambda e, p=p, wi=wi, o0=o0: e.tensor_copy(
                A1[0:64, 4096 * p + SW * wi:4096 * p + SW * (wi + 1)], o0[0:64, :]))
            cx.op("vector", lambda e, p=p, wi=wi, o1=o1: e.tensor_copy(
                A1[64:128, 4096 * p + SW * wi:4096 * p + SW * (wi + 1)], o1[0:64, :]),
                inc=[("v", 1)])
            cx.wait("vector", "dn", dn_base + 16 * (2 * idx))
            cx.op("vector", lambda e, sl0=sl0, o0=o0: e.tensor_copy(
                DSTG[sl0][:, :], o0[64:65, :]))
            cx.op("vector", lambda e, sl1=sl1, o1=o1: e.tensor_copy(
                DSTG[sl1][:, :], o1[64:65, :]), inc=[("v", 1)])
            v_den = cx.val("v")
            cx.wait("gpsimd", "v", v_den)
            cx.dma("gpsimd", lambda e, sl0=sl0, p=p, wi=wi: e.dma_start(
                out=DEN[(2 * p) * 8 + wi:(2 * p) * 8 + wi + 1, :],
                in_=DSTG[sl0][:, :]), "dn")
            cx.dma("gpsimd", lambda e, sl1=sl1, p=p, wi=wi: e.dma_start(
                out=DEN[(2 * p + 1) * 8 + wi:(2 * p + 1) * 8 + wi + 1, :],
                in_=DSTG[sl1][:, :]), "dn")
            o_ev[wi % 2] = v_den

        seq = []
        for wi in range(NWIN):
            jm = NT if l1 else 4 * wi + 4
            for j in range(jm):
                seq.append((wi, j, jm))
        emit_S(*seq[0])
        for idx in range(1, len(seq)):
            emit_S(*seq[idx])
            emit_AV(*seq[idx - 1])
        emit_AV(*seq[-1])
        pair_done[p] = cx.val("t")
        if trunc == "c1" and p == 0:
            return

    APLd = env["APLd"]
    DPREV, BBC2v = env["DPREV"], env["BBC2"]
    aslot = [DPREV[0:1, 0:1024], BBC2[0:1, 0:1024], env["BBC"][0:1, 0:1024]]
    cx.wait("vector", "dn", cx.val("dn"))
    cx.op("vector", lambda e: e.reciprocal(RECIP[:, :], DEN[:, :]), inc=[("v", 1)])
    v_rec = cx.val("v")
    cx.wait("gpsimd", "v", v_rec)
    cx.dma("gpsimd", lambda e: e.dma_start(out=APLd, in_=RECIP[:, :]), "mq")
    r_full = cx.val("mq")
    cx.wait("gpsimd", "mq", r_full)
    apply_done = {}
    slot_free = {0: 0, 1: 0, 2: 0}
    for g in range(64):
        h = g // 4
        k = g % 4
        s = g % 3
        rsem = "rx%d" % (g % 2)
        cx.wait("gpsimd", "t", slot_free[s])
        cx.dma("gpsimd", lambda e, h=h, k=k, s=s: e.dma_start(
            out=aslot[s],
            in_=APLd[8 * h + 2 * k:8 * h + 2 * k + 2, :]
            .rearrange("(a r) c -> a (r c)", a=1)), rsem)
        r_mv = cx.val(rsem)
        bp = 2 + 2 * s
        bb = PS[0:64, bp * 512:(bp + 2) * 512]
        if g >= 3:
            cx.wait("tensor", "v", apply_done[g - 3])
        cx.wait("tensor", rsem, r_mv)
        cx.op("tensor", lambda e, bb=bb, s=s: e.matmul(
            bb[:, 0:512], ONES1[:, 0:64], aslot[s][:, 0:512],
            start=True, stop=True))
        cx.op("tensor", lambda e, bb=bb, s=s: e.matmul(
            bb[:, 512:1024], ONES1[:, 0:64], aslot[s][:, 512:1024],
            start=True, stop=True), inc=[("t", 1)])
        slot_free[s] = t_bc = cx.val("t")
        pi = h // 2
        half = h % 2
        ao = A1[64 * half:64 * half + 64,
                4096 * pi + 1024 * k:4096 * pi + 1024 * (k + 1)]
        cx.wait("vector", "t", t_bc)
        cx.op("vector", lambda e, ao=ao, bb=bb: e.tensor_tensor(
            ao, ao, bb, ALU.mult), inc=[("v", 1)])
        apply_done[g] = cx.val("v")
    v_apply = cx.val("v")
    cx.wb_free_lo = cx.val("t")
    if trunc == "apply":
        return

    cx.wait("gpsimd", "t", cx.val("t"))
    cx.dma("gpsimd", lambda e: e.dma_start(out=BROW[:, :], in_=w["bo"]), "w")
    w_bo = cx.val("w")
    cx.wait("tensor", "v", v_apply)
    cx.wait("tensor", "w", w_bo)
    cx.wait("tensor", "w", w_wo)
    bcast_row(cx, env, BROW, BBC, banks=(6, 7))
    v_bo = cx.val("v")
    evac_d = {0: 0, 1: 0}
    for i in range(NT):
        par = i % 2
        pd = bankpair(par)
        sl = i % 3
        rsem = "rr%d" % sl
        rt = RQ[sl]
        geng, gval = cx.rr_guard[sl]
        cx.wait("sync", geng, gval)
        cx.wait("sync", "str", cx.str_mile.get(i, 0))
        cx.dma("sync", lambda e, rt=rt, i=i: e.dma_start(
            out=rt[:, :], in_=rsrc[128 * i:128 * (i + 1), :]), rsem)
        r_ld = cx.val(rsem)
        cx.wait("tensor", "v", evac_d[par])
        for ci in range(NCH):
            lhsT = A1[:, 4096 * ci + 128 * i:4096 * ci + 128 * (i + 1)]
            mm2(cx, lhsT, WB[:, 12352 + 1024 * ci:12352 + 1024 * ci + 512],
                WB[:, 12352 + 1024 * ci + 512:12352 + 1024 * (ci + 1)],
                pd[:, 0:512], pd[:, 512:1024],
                start=(ci == 0), stop=(ci == NCH - 1), tinc=(ci == NCH - 1))
        t_wo = cx.val("t")
        cx.wait("vector", "t", t_wo)
        cx.wait("vector", rsem, r_ld)
        cx.wait("vector", "v", v_bo)
        cx.op("vector", lambda e, rt=rt, pd=pd: e.tensor_tensor(
            rt[:, :], rt[:, :], pd[:, :], ALU.add), inc=[("v", 1)])
        v_add1 = cx.val("v")
        evac_d[par] = v_add1
        cx.wait("vector", "v", v_add1)
        cx.op("vector", lambda e, rt=rt: e.tensor_tensor(
            rt[:, :], rt[:, :], BBC[:, :], ALU.add), inc=[("v", 1)])
        v_add2 = cx.val("v")
        cx.wait("scalar", "v", v_add2)
        cx.dma("scalar", lambda e, rt=rt, i=i: e.dma_start(
            out=rdst[128 * i:128 * (i + 1), :], in_=rt[:, :]), "str")
        cx.rr_guard[sl] = ("str", cx.val("str"))
        cx.str_mile[i] = cx.val("str")


def emit_newton(cx, env):
    PS, GRAM, ABD, XBD, XST, XSTB, YSB, I2S, TWOI8 = (
        env["PS"], env["GRAM"], env["ABD"], env["XBD"], env["XST"], env["XSTB"],
        env["YSB"], env["I2S"], env["TWOI8"])
    RS16, CIN, CBC, ONES64, ONES1, MQS, STG = (
        env["RS16"], env["CIN"], env["CBC"], env["ONES64"], env["ONES1"],
        env["MQS"], env["STG"])
    QTd, QPTd, WB = env["QTd"], env["QPTd"], env["WB"]

    t_gram = cx.val("t")
    cx.wait("vector", "t", t_gram)
    cx.op("vector", lambda e: e.tensor_copy(GRAM[:, 0:512], PS[0:64, 3072:3584]))
    cx.op("vector", lambda e: e.tensor_copy(GRAM[:, 512:1024], PS[0:64, 3584:4096]),
          inc=[("v", 1)])
    v_gram = cx.val("v")
    cx.wait("vector", "v", v_gram)
    cx.op("vector", lambda e: e.reduce_sum(
        RS16[:, :], GRAM[:, :].rearrange("p (h d) -> p h d", h=H), axis=AX.X,
        apply_absolute_value=True), inc=[("v", 1)])
    v_rs = cx.val("v")
    cx.wait("tensor", "v", v_rs)
    cx.op("tensor", lambda e: e.matmul(PS[0:1, 0:H], ONES64[:, :], RS16[:, :],
                                       start=True, stop=True), inc=[("t", 1)])
    t_rs = cx.val("t")
    cx.wait("vector", "t", t_rs)
    cx.op("vector", lambda e: e.tensor_copy(CIN[:, :], PS[0:1, 0:H]), inc=[("v", 1)])
    v_ci = cx.val("v")
    cx.wait("vector", "v", v_ci)
    cx.op("vector", lambda e: e.reciprocal(CIN[:, :], CIN[:, :]), inc=[("v", 1)])
    v_cr = cx.val("v")
    cx.wait("tensor", "v", v_cr)
    cx.op("tensor", lambda e: e.matmul(PS[:, 0:H], ONES1[:, :], CIN[:, :],
                                       start=True, stop=True), inc=[("t", 1)])
    t_cb = cx.val("t")
    cx.wait("vector", "t", t_cb)
    cx.op("vector", lambda e: e.tensor_copy(CBC[:, :], PS[:, 0:H]), inc=[("v", 1)])
    v_cb = cx.val("v")
    cx.wait("tensor", "v", v_cb)

    for p in range(8):
        cx.op("gpsimd", lambda e, p=p: e.memset(ABD[p][:, :], 0.0))
        cx.op("gpsimd", lambda e, p=p: e.memset(XBD[p][:, :], 0.0))
    cx.op("gpsimd", lambda e: e.nop(), inc=[("g", 1)])
    g_zero = cx.val("g")
    cx.wait("gpsimd", "v", v_gram)
    for p in range(8):
        cx.dma("gpsimd", lambda e, p=p: e.dma_start(
            out=ABD[p][0:64, 0:64], in_=GRAM[:, 64 * (2 * p):64 * (2 * p) + 64]), "mq")
        cx.dma("gpsimd", lambda e, p=p: e.dma_start(
            out=ABD[p][64:128, 64:128],
            in_=GRAM[:, 64 * (2 * p + 1):64 * (2 * p + 1) + 64]), "mq")
    abd_done = cx.val("mq")

    cx.wait("vector", "g", g_zero)
    cx.wait("vector", "v", v_cb)
    for p in range(8):
        cx.op("vector", lambda e, p=p: e.tensor_scalar(
            XST[0:64, 64 * p:64 * p + 64], I2S[0:64, :],
            CBC[0:64, 2 * p:2 * p + 1], None, ALU.mult))
        cx.op("vector", lambda e, p=p: e.tensor_scalar(
            XST[64:128, 64 * p:64 * p + 64], I2S[64:128, :],
            CBC[64:128, 2 * p + 1:2 * p + 2], None, ALU.mult))
    cx.op("vector", lambda e: e.nop(), inc=[("v", 1)])
    prev_v = cx.val("v")

    for it in range(NEWTON_ITERS):
        par = it % 2
        t1b = PS[:, par * 512:par * 512 + 512]
        x2b = PS[:, (2 + par) * 512:(2 + par) * 512 + 512]
        cx.wait("vector", "v", prev_v)
        for p in range(8):
            cx.op("vector", lambda e, p=p: e.tensor_copy(
                XBD[p][0:64, 0:64], XST[0:64, 64 * p:64 * p + 64]))
            cx.op("vector", lambda e, p=p: e.tensor_copy(
                XBD[p][64:128, 64:128], XST[64:128, 64 * p:64 * p + 64]))
        cx.op("vector", lambda e: e.nop(), inc=[("v", 1)])
        v_bd = cx.val("v")
        cx.wait("tensor", "v", v_bd)
        if it == 0:
            cx.wait("tensor", "mq", abd_done)
        for p in range(8):
            cx.op("tensor", lambda e, p=p, t1b=t1b: e.matmul(
                t1b[:, 64 * p:64 * p + 64], ABD[p][:, :],
                XST[:, 64 * p:64 * p + 64], start=True, stop=True),
                inc=[("t", 1)] if p == 7 else None)
        t_t1 = cx.val("t")
        cx.wait("vector", "t", t_t1)
        cx.op("vector", lambda e, t1b=t1b: e.scalar_tensor_tensor(
            YSB[:, :], t1b, -1.0, TWOI8[:, :], ALU.mult, ALU.add), inc=[("v", 1)])
        v_y = cx.val("v")
        cx.wait("tensor", "v", v_y)
        for p in range(8):
            cx.op("tensor", lambda e, p=p, x2b=x2b: e.matmul(
                x2b[:, 64 * p:64 * p + 64], XBD[p][:, :],
                YSB[:, 64 * p:64 * p + 64], start=True, stop=True),
                inc=[("t", 1)] if p == 7 else None)
        t_x2 = cx.val("t")
        cx.wait("vector", "t", t_x2)
        cx.op("vector", lambda e, x2b=x2b: e.tensor_copy(XST[:, :], x2b),
              inc=[("v", 1)])
        prev_v = cx.val("v")
    cx.wait("vector", "v", prev_v)
    cx.op("vector", lambda e: e.tensor_copy(XSTB[:, :], XST[:, :]), inc=[("v", 1)])
    v_xb = cx.val("v")
    return v_xb


def emit_qproj(cx, env, v_xb):
    PS, XSTB, STG = env["PS"], env["XSTB"], env["STG"]
    QTd, QPTd, WB = env["QTd"], env["QPTd"], env["WB"]
    qp_ev = {0: 0, 1: 0}
    qp_t = {}
    qp_mile = {}
    t0 = cx.val("t")
    cx.wait("sync", "pq1", cx.val("pq1"))
    for p in range(8):
        qbuf = WB[:, 4096 * (p % 2):4096 * (p % 2) + 4096]
        cx.wait("sync", "t", qp_t.get(p - 2, t0))
        cx.dma("sync", lambda e, p=p, qbuf=qbuf: e.dma_start(
            out=qbuf, in_=QTd[128 * p:128 * (p + 1), :]), "pq0")
        q_ld = cx.val("pq0")
        cx.wait("tensor", "pq0", q_ld)
        if p == 0:
            cx.wait("tensor", "v", v_xb)
        for h2 in range(2):
            for sc in range(8):
                par = sc % 2
                ob = PS[0:64, (4 + par) * 512:(4 + par) * 512 + 512]
                cx.wait("tensor", "v", qp_ev[par])
                lh = XSTB[64 * h2:64 * h2 + 64, 64 * p:64 * p + 64]
                cx.op("tensor", lambda e, lh=lh, ob=ob, h2=h2, sc=sc: e.matmul(
                    ob, lh, qbuf[64 * h2:64 * h2 + 64, 512 * sc:512 * (sc + 1)],
                    start=True, stop=True), inc=[("t", 1)])
                t_qp = cx.val("t")
                sl = (p * 16 + h2 * 8 + sc) % 4
                ssem = "sg%d" % sl
                stg = STG[sl][0:64, 0:512]
                cx.wait("vector", "t", t_qp)
                cx.wait("vector", ssem, cx.val(ssem))
                cx.op("vector", lambda e, stg=stg, ob=ob: e.tensor_copy(stg, ob),
                      inc=[("v", 1)])
                qp_ev[par] = cx.val("v")
                v_ev = cx.val("v")
                cx.wait("sync", "v", v_ev)
                cx.dma("sync", lambda e, p=p, h2=h2, sc=sc, stg=stg: e.dma_start(
                    out=QPTd[128 * p + 64 * h2:128 * p + 64 * h2 + 64,
                             512 * sc:512 * (sc + 1)], in_=stg), ssem)
        qp_t[p] = cx.val("t")
        qp_mile[p] = [("sg%d" % sl, cx.val("sg%d" % sl)) for sl in range(4)]
    return qp_mile


def emit_ffn(cx, env, w, rsrc, rdst):
    A1, WB, PS, STG, AR2, RQ = (env["A1"], env["WB"], env["PS"], env["STG"],
                                env["AR2"], env["RQ"])
    XT, MUR, MUK, DPREV, XPC, XPREV = (env["XT"], env["MUR"], env["MUK"],
                                       env["DPREV"], env["XPC"], env["XPREV"])

    def bankpair(i):
        return PS[:, 1024 * i:1024 * (i + 1)]

    t_entry = cx.val("t")

    for name, col0 in (("wr", 0), ("wkf", 8192), ("wvf", 16384)):
        cx.wait("gpsimd", "t", cx.wb_free_lo if name == "wr" else t_entry)
        src = w[name].rearrange("(ci p) o -> p ci o", p=128)
        dst = WB[:, col0:col0 + 8192].rearrange("p (ci o) -> p ci o", ci=8)
        cx.dma("gpsimd", lambda e, dst=dst, src=src: e.dma_start(out=dst, in_=src), "w")
    cx.dma("gpsimd", lambda e: e.dma_start(out=MUR[:, :], in_=w["mur"]), "w")
    cx.dma("gpsimd", lambda e: e.dma_start(out=MUK[:, :], in_=w["muk"]), "w")
    w_all = cx.val("w")
    cx.wait("tensor", "w", w_all)
    cx.wait("vector", "w", w_all)

    bank_free = {i: ("v", 0) for i in range(4)}
    state = {"v_mix": 0, "v_mult_last": 0, "dprev_free": ("v", 0),
             "xprev_free": 0}

    def prep_quarter(q):
        t0, t1 = q * 8, (q + 1) * 8
        if q == 0:
            cx.op("gpsimd", lambda e: e.memset(XPC[:, :], 0.0), inc=[("g", 1)])
            cx.wait("vector", "g", cx.val("g"))
            cx.op("vector", lambda e: e.nop(), inc=[("v", 1)])
        else:
            cx.wait("vector", "v", state["v_mix"])
            for ci in range(NCH):
                cx.op("vector", lambda e, ci=ci: e.tensor_copy(
                    XPC[:, ci:ci + 1], A1[:, 1024 * ci + 1023:1024 * ci + 1024]))
            cx.op("vector", lambda e: e.nop(), inc=[("v", 1)])
        v_bnd = cx.val("v")
        emit_ln(cx, env, rsrc, t0, t1, act_sum=(q == 0))
        wait_all_sg(cx, "scalar")
        if q == 0:
            cx.wait("scalar", "t", t_entry)
        cx.wait("scalar", "v", v_bnd)
        for ci in range(NCH):
            dst = A1[:, 1024 * ci:1024 * ci + 1024]
            src = XT[t0 * 128:t1 * 128, 128 * ci:128 * ci + 128]
            cx.dma("scalar", lambda e, dst=dst, src=src: e.dma_start_transpose(
                dst, src), "tr")
        return cx.val("tr")

    def emit_mix_half(ci, mu, dst_off, tr_q, t_gate):
        xt = A1[:, 1024 * ci:1024 * ci + 1024]
        xd = A1[:, dst_off + 1024 * ci:dst_off + 1024 * ci + 1024]
        if ci == 0:
            cx.wait("vector", "tr", tr_q)
            cx.wait("vector", "t", t_gate)
            cx.wait("vector", "w", w_all)
        cx.wait("vector", "v", state["xprev_free"])
        cx.op("vector", lambda e, xt=xt: e.tensor_tensor(
            XPREV[:, 1:1024], xt[:, 1:1024], xt[:, 0:1023], ALU.subtract))
        cx.op("vector", lambda e, xt=xt, ci=ci: e.tensor_tensor(
            XPREV[:, 0:1], xt[:, 0:1], XPC[:, ci:ci + 1], ALU.subtract),
            inc=[("v", 1)])
        v_d = cx.val("v")
        cx.wait("vector", "v", v_d)
        cx.op("vector", lambda e, xd=xd, xt=xt, ci=ci, mu=mu: e.scalar_tensor_tensor(
            xd[:, 1:1024], XPREV[:, 1:1024], mu[:, ci:ci + 1], xt[:, 0:1023],
            ALU.mult, ALU.add))
        cx.op("vector", lambda e, xd=xd, ci=ci, mu=mu: e.scalar_tensor_tensor(
            xd[:, 0:1], XPREV[:, 0:1], mu[:, ci:ci + 1], XPC[:, ci:ci + 1],
            ALU.mult, ALU.add), inc=[("v", 1)])
        state["v_mix"] = cx.val("v")
        state["xprev_free"] = state["v_mix"]

    def r_pass(q, v_xr):
        for jl in range(8):
            par = jl % 4
            pp = bankpair(par)
            cx.wait("tensor", *bank_free[par])
            if jl == 0:
                cx.wait("tensor", "v", v_xr)
            for ci in range(NCH):
                lhsT = A1[:, 8192 + 1024 * ci + 128 * jl:
                          8192 + 1024 * ci + 128 * (jl + 1)]
                mm2(cx, lhsT, WB[:, 1024 * ci:1024 * ci + 512],
                    WB[:, 1024 * ci + 512:1024 * (ci + 1)],
                    pp[:, 0:512], pp[:, 512:1024],
                    start=(ci == 0), stop=(ci == NCH - 1), tinc=(ci == NCH - 1))
            t_r = cx.val("t")
            cx.wait("scalar", "t", t_r)
            if jl == 0:
                cx.wait("scalar", "v", state["v_mult_last"])
            cx.op("scalar", lambda e, jl=jl, pp=pp: e.activation(
                AR2[:, 1024 * jl:1024 * (jl + 1)], pp, AF.Sigmoid), inc=[("a", 1)])
            bank_free[par] = ("a", cx.val("a"))

    def k_pass(q, v_xk, mix_next):
        for dt in range(8):
            par = dt % 4
            pp = bankpair(par)
            cx.wait("tensor", *bank_free[par])
            if dt == 0:
                cx.wait("tensor", "v", v_xk)
            for ci in range(NCH):
                lhsT = WB[:, 8192 + 1024 * ci + 128 * dt:
                          8192 + 1024 * ci + 128 * dt + 128]
                rhs_a = A1[:, 16384 + 1024 * ci:16384 + 1024 * ci + 512]
                rhs_b = A1[:, 16384 + 1024 * ci + 512:16384 + 1024 * (ci + 1)]
                mm2(cx, lhsT, rhs_a, rhs_b, pp[:, 0:512], pp[:, 512:1024],
                    start=(ci == 0), stop=(ci == NCH - 1), tinc=(ci == NCH - 1))
            t_k = cx.val("t")
            cx.wait("scalar", "t", t_k)
            cx.wait("scalar", *state["dprev_free"])
            cx.op("scalar", lambda e, pp=pp: e.activation(
                DPREV[:, :], pp[:, 0:1024], AF.Relu), inc=[("a", 1)])
            a_rl = cx.val("a")
            bank_free[par] = ("a", a_rl)
            cx.wait("scalar", "a", a_rl)
            cx.op("scalar", lambda e, dt=dt: e.activation(
                A1[:, 24576 + 1024 * dt:24576 + 1024 * (dt + 1)], DPREV[:, :],
                AF.Square), inc=[("a", 1)])
            state["dprev_free"] = ("a", cx.val("a"))
            if mix_next is not None:
                emit_mix_half(dt, MUR, 8192, mix_next[0], mix_next[1])
        return cx.val("t"), cx.val("a")

    def v_pass(q, t_k_all, a_k_all, mix_next):
        t0 = q * 8
        for jl in range(8):
            i = t0 + jl
            par = jl % 4
            pp = bankpair(par)
            sl = i % 3
            rsem = "rr%d" % sl
            rt = RQ[sl]
            geng, gval = cx.rr_guard[sl]
            cx.wait("sync", geng, gval)
            cx.wait("sync", "str", cx.str_mile.get(i, 0))
            cx.dma("sync", lambda e, rt=rt, i=i: e.dma_start(
                out=rt[:, :], in_=rsrc[128 * i:128 * (i + 1), :]), rsem)
            r_ld = cx.val(rsem)
            cx.wait("tensor", *bank_free[par])
            if jl == 0:
                cx.wait("tensor", "a", a_k_all)
            for dt in range(NCH):
                lhsT = A1[:, 24576 + 1024 * dt + 128 * jl:
                          24576 + 1024 * dt + 128 * (jl + 1)]
                mm2(cx, lhsT, WB[:, 16384 + 1024 * dt:16384 + 1024 * dt + 512],
                    WB[:, 16384 + 1024 * dt + 512:16384 + 1024 * (dt + 1)],
                    pp[:, 0:512], pp[:, 512:1024],
                    start=(dt == 0), stop=(dt == NCH - 1), tinc=(dt == NCH - 1))
            t_v = cx.val("t")
            cx.wait("vector", "t", t_v)
            cx.wait("vector", *state["dprev_free"])
            cx.op("vector", lambda e, pp=pp, jl=jl: e.tensor_tensor(
                DPREV[:, :], pp[:, 0:1024], AR2[:, 1024 * jl:1024 * (jl + 1)],
                ALU.mult), inc=[("v", 1)])
            v_m = cx.val("v")
            bank_free[par] = ("v", v_m)
            cx.wait("vector", rsem, r_ld)
            cx.wait("vector", "v", v_m)
            cx.op("vector", lambda e, rt=rt: e.tensor_tensor(
                rt[:, :], rt[:, :], DPREV[:, :], ALU.add), inc=[("v", 1)])
            v_add = cx.val("v")
            state["dprev_free"] = ("v", v_add)
            state["v_mult_last"] = v_add
            cx.wait("scalar", "v", v_add)
            cx.dma("scalar", lambda e, rt=rt, i=i: e.dma_start(
                out=rdst[128 * i:128 * (i + 1), :], in_=rt[:, :]), "str")
            cx.rr_guard[sl] = ("str", cx.val("str"))
            cx.str_mile[i] = cx.val("str")
            if mix_next is not None:
                emit_mix_half(jl, MUK, 16384, mix_next[0], mix_next[1])

    tr0 = prep_quarter(0)
    for ci in range(NCH):
        emit_mix_half(ci, MUR, 8192, tr0, t_entry)
    v_xr = state["v_mix"]
    for ci in range(NCH):
        emit_mix_half(ci, MUK, 16384, tr0, t_entry)
    v_xk = state["v_mix"]
    for q in range(NQ):
        r_pass(q, v_xr)
        tr_next = prep_quarter(q + 1) if q < NQ - 1 else None
        t_r_all = cx.val("t")
        t_k_all, a_k_all = k_pass(q, v_xk,
                                  (tr_next, t_r_all) if q < NQ - 1 else None)
        if q == NQ - 1:
            cx.wb_free_lo = t_k_all
        v_xr = state["v_mix"]
        v_pass(q, t_k_all, a_k_all,
               (tr_next, t_k_all) if q < NQ - 1 else None)
        v_xk = state["v_mix"]



def _prep_weights(inputs):
    f = np.ascontiguousarray
    out = {}
    out[("attn0", 0)] = dict(
        wq=f(inputs["L0_attn_Wq"].transpose(1, 0, 2).reshape(C, C)),
        wk=f(inputs["L0_attn_Wk"].transpose(1, 0, 2).reshape(C, C)),
        wv=f(inputs["L0_attn_Wv"].transpose(1, 0, 2).reshape(C, C)),
        wo=f(inputs["L0_attn_Wo"].T),
        bo=f(inputs["L0_attn_bo"].reshape(1, C)),
    )
    out[("attn1", 1)] = dict(
        wq=f(inputs["L1_attn_Wq"].T),
        wk=f(inputs["L1_attn_Wk"].T),
        wv=f(inputs["L1_attn_Wv"].T),
        wo=f(inputs["L1_attn_Wfc"].T),
        bq=f(inputs["L1_attn_bq"].reshape(8, 128).T),
        bk=f(inputs["L1_attn_bk"].reshape(8, 128).T),
        bvr=f(inputs["L1_attn_bv"].reshape(1, C)),
        bkr=f(inputs["L1_attn_bk"].reshape(1, C)),
        bo=f(inputs["L1_attn_bfc"].reshape(1, C)),
    )
    for L in (0, 1):
        p = "L%d_ffn_" % L
        out[("ffn", L)] = dict(
            wr=f(inputs[p + "W_r"].T),
            wkf=f(inputs[p + "W_k"].T),
            wvf=f(inputs[p + "W_v"].T),
            mur=f(inputs[p + "mu_r"].reshape(8, 128).T),
            muk=f(inputs[p + "mu_k"].reshape(8, 128).T),
        )
    return out


FUSED = True


def kernel(**inputs):
    x = np.ascontiguousarray(np.asarray(inputs["L0_attn_x"], dtype=np.float32))
    wmaps = _prep_weights({k: np.asarray(v) for k, v in inputs.items()})
    B = x.shape[0]
    ranges = [(0, 4)] if FUSED else [(i, i + 1) for i in range(4)]
    cur = [np.ascontiguousarray(x[b]) for b in range(B)]
    for (lo, hi) in ranges:
        nc = build_program(lo, hi)
        in_maps = []
        for b in range(B):
            m = {"xin": cur[b]}
            for si, (kind, L) in enumerate(SUBS[lo:hi]):
                for k, arr in wmaps[(kind, L)].items():
                    m["s%d_%s" % (si, k)] = arr
            in_maps.append(m)
        res = run_bass_kernel_spmd(nc, in_maps, core_ids=list(range(8)))
        cur = [np.asarray(res.results[b]["xout"]) for b in range(B)]
    return np.stack(cur, axis=0)
